# version 20
# baseline (speedup 1.0000x reference)
import numpy as np
import ml_dtypes
import concourse.bass as bass
import concourse.mybir as mybir
from concourse.bass_utils import run_bass_kernel_spmd

F32 = mybir.dt.float32
BF16 = mybir.dt.bfloat16
U8 = mybir.dt.uint8
I16 = mybir.dt.int16
AF = mybir.ActivationFunctionType
ALU = mybir.AluOpType
AX = mybir.AxisListType
NBF = ml_dtypes.bfloat16

T, D, H, DK = 4096, 1024, 16, 64
C = 128
TILE_A = 512
TILE_B = 1024
NT_A = T // TILE_A
NT_B = T // TILE_B
DI = 2816
NCK = DI // 256
EPS_LN = 1e-5
MAGIC = np.float32(1.5 * 2 ** 23)


class _Ev:
    __slots__ = ("op",)

    def __init__(self, op):
        self.op = op


class _Op:
    __slots__ = ("stream", "fn", "inc", "deps", "needed", "cum", "idx", "grp")

    def __init__(self, stream, fn, inc, deps, idx, grp):
        self.stream = stream
        self.fn = fn
        self.inc = inc
        self.deps = deps
        self.needed = False
        self.cum = 0
        self.idx = idx
        self.grp = grp


class Buf:
    __slots__ = ("writer", "readers", "name")

    def __init__(self, name=""):
        self.writer = None
        self.readers = []
        self.name = name


SERIALIZE = False


class Tracker:

    def __init__(self):
        self.streams = {k: [] for k in ("pe", "dve", "act", "gp", "sync")}
        self.nops = 0
        self.last = {k: None for k in self.streams}
        self.groups = {}
        self.glast = {}

    def op(self, stream, fn, reads=(), writes=(), inc=1, grp=None):
        assert (inc == 16) == (grp is not None), f"grp required iff DMA ({grp})"
        deps = []
        if SERIALIZE:
            deps = [o for o in self.last.values() if o is not None]
        for b in reads:
            if b.writer is not None:
                deps.append(b.writer)
        for b in writes:
            if b.writer is not None:
                deps.append(b.writer)
            deps.extend(b.readers)
        deps = [d for d in deps
                if d.inc == 16 or not (stream == "pe" and d.stream == "pe")]
        o = _Op(stream, fn, inc, deps, self.nops, grp)
        self.nops += 1
        self.streams[stream].append(o)
        self.last[stream] = o
        if grp is not None:
            self.groups.setdefault(grp, 0)
            self.glast[grp] = o
        for b in reads:
            b.readers.append(o)
        for b in writes:
            b.writer = o
            b.readers = []
        return o

    def barrier(self):
        bufs = []
        for s, ops in self.streams.items():
            b = Buf(f"bar_{s}")
            b.writer = ops[-1] if ops else None
            bufs.append(b)
        for g, o in self.glast.items():
            b = Buf(f"bar_{g}")
            b.writer = o
            bufs.append(b)
        return bufs

    def finalize(self, nc, st, block):
        stream_sems = {s: st.enter_context(nc.semaphore(f"s_{s}"))
                       for s in self.streams}
        group_sems = {g: st.enter_context(nc.semaphore(f"sg_{g}"))
                      for g in self.groups}
        for ops in self.streams.values():
            for o in ops:
                for d in o.deps:
                    d.needed = True
        grp_stream = {}
        for s, ops in self.streams.items():
            c = 0
            for o in ops:
                if o.inc == 16:
                    self.groups[o.grp] += 16
                    o.cum = self.groups[o.grp]
                    grp_stream[o.grp] = s
                else:
                    if o.needed:
                        c += 1
                    o.cum = c

        def emit(stream):
            ops = self.streams[stream]
            eng_attr = {"pe": "tensor", "dve": "vector", "act": "scalar",
                        "gp": "gpsimd", "sync": "sync"}[stream]

            def body(eng):
                last_wait = {}
                for o in ops:
                    for d in o.deps:
                        if d.inc == 16:
                            sem, key, v = group_sems[d.grp], ("g", d.grp), d.cum
                        else:
                            if not d.needed:
                                continue
                            sem, key, v = (stream_sems[d.stream], ("s", d.stream),
                                           d.cum)
                        if v <= 0 or last_wait.get(key, -1) >= v:
                            continue
                        eng.wait_ge(sem, v)
                        last_wait[key] = v
                    ins = o.fn(eng)
                    if o.inc == 16:
                        ins.then_inc(group_sems[o.grp], 16)
                    elif o.needed:
                        ins.then_inc(stream_sems[o.stream], 1)
                for g, s2 in grp_stream.items():
                    if s2 == stream and self.groups[g] > 0:
                        eng.wait_ge(group_sems[g], self.groups[g])

            getattr(block, eng_attr)(body)
        for s in ("sync", "gp", "pe", "dve", "act"):
            emit(s)


def make_quant(tr, SAfn, prefix, bufmk, nsets=2, maxw=2816, share_u=None,
               cbuf=None, pipe3=False):
    nst = 3 if pipe3 else nsets
    nxn = 2 if pipe3 else nsets
    nm = 1 if pipe3 else nsets
    sts = [SAfn(f"{prefix}_st{s}", [128, 3 * (maxw // 16)], F32)
           for s in range(nst)]
    bst = [bufmk(f"{prefix}_st{s}") for s in range(nst)]
    xns = [SAfn(f"{prefix}_xn{s}", [128, maxw], BF16) for s in range(nxn)]
    bxn = [bufmk(f"{prefix}_xn{s}") for s in range(nxn)]
    us = [(share_u[0] if share_u else SAfn(f"{prefix}_u{s}", [128, maxw], F32))
          for s in range(nxn)]
    bu = [(share_u[1] if share_u else bufmk(f"{prefix}_u{s}"))
          for s in range(nxn)]
    qls = [SAfn(f"{prefix}_ql{s}", [128, maxw], BF16) for s in range(nxn)]
    bql = [bufmk(f"{prefix}_ql{s}") for s in range(nxn)]
    ms = [SAfn(f"{prefix}_m{s}", [128, maxw], U8) for s in range(nm)]
    bm = [bufmk(f"{prefix}_m{s}") for s in range(nm)]

    def quant_piece(xap, width, rbufs, wbufs, op, sidx, stage=0):
        st_t, st_b = sts[sidx % nst], bst[sidx % nst]
        xn_full, xn_b = xns[sidx % nxn], bxn[sidx % nxn]
        u_full, u_b = us[sidx % nxn], bu[sidx % nxn]
        ql_full, ql_b = qls[sidx % nxn], bql[sidx % nxn]
        m_full, m_b = ms[sidx % nm], bm[sidx % nm]
        nb = width // 16
        nbm = st_t.shape[1] // 3
        amax = st_t[:, 0:nb]
        scl = st_t[:, nbm:nbm + nb]
        inv = st_t[:, 2 * nbm:2 * nbm + nb]
        x3 = xap.rearrange("p (a b) -> p a b", b=16)
        xn = xn_full[:, 0:width]
        u = u_full[:, 0:width]
        ql = ql_full[:, 0:width]
        m = m_full[:, 0:width]
        creads = [cbuf] if cbuf is not None else []
        if stage in (0, "a"):
            op("dve", lambda e: e.tensor_reduce(
                amax, x3, AX.X, ALU.max, apply_absolute_value=True),
                reads=list(rbufs), writes=[st_b])
            op("dve", lambda e: e.tensor_scalar(
                scl, amax, 1.0 / 6.0, 1e-12, ALU.mult, ALU.max),
                reads=[st_b], writes=[st_b])
            op("dve", lambda e: e.reciprocal(inv, scl),
               reads=[st_b], writes=[st_b])
        if stage in (0, "b"):
            op("dve", lambda e: e.tensor_tensor(
                xn.rearrange("p (a b) -> p a b", b=16), x3,
                inv.unsqueeze(2).broadcast_to([128, nb, 16]), ALU.mult),
                reads=list(rbufs) + [st_b], writes=[xn_b])
            op("act", lambda e: e.activation(
                u, xn, AF.Identity, scale=2.0, bias=float(MAGIC)),
                reads=[xn_b] + creads, writes=[u_b])
            op("act", lambda e: e.activation(
                ql, u, AF.Identity, scale=0.5, bias=-float(MAGIC) * 0.5),
                reads=[u_b] + creads, writes=[ql_b])
            op("act", lambda e: e.activation(u, xn, AF.Abs),
               reads=[xn_b], writes=[u_b])
        if stage in (0, "c"):
            op("dve", lambda e: e.tensor_scalar(
                xn.bitcast(I16), xn.bitcast(I16), 32, None, ALU.add),
                reads=[xn_b], writes=[xn_b])
            op("dve", lambda e: e.tensor_scalar(
                xn.bitcast(I16), xn.bitcast(I16), -64, None, ALU.bitwise_and),
                reads=[xn_b], writes=[xn_b])
            op("dve", lambda e: e.tensor_scalar(
                m, u, 0.9, None, ALU.is_lt),
                reads=[u_b], writes=[m_b])
            op("dve", lambda e: e.copy_predicated(xn, m, ql),
                reads=[m_b, ql_b, xn_b], writes=[xn_b])
            op("dve", lambda e: e.tensor_tensor(
                x3, xn.rearrange("p (a b) -> p a b", b=16),
                scl.unsqueeze(2).broadcast_to([128, nb, 16]), ALU.mult),
                reads=[xn_b, st_b], writes=list(wbufs))
    return quant_piece


def _host_consts():
    ident = np.eye(128, dtype=NBF)
    icol4 = np.zeros((128, 512), dtype=NBF)
    for b in range(4):
        icol4[:, b * 128:(b + 1) * 128] = ident
    ones1 = np.ones((1, 128), dtype=NBF)
    ones16 = np.ones((16, 128), dtype=NBF)
    oh4 = np.zeros((16, 64), dtype=NBF)
    for c in range(4):
        for h in range(16):
            oh4[h, c * 16 + h] = 1
    return ident, icol4, ones1, ones16, oh4


def build_nc(debug=False):
    nc = bass.Bass("TRN2", target_bir_lowering=False)
    tr = Tracker()

    x_in = nc.dram_tensor("x", [T, D], F32, kind="ExternalInput").ap()
    wqT = nc.dram_tensor("wqT", [D, D], F32, kind="ExternalInput").ap()
    wkT = nc.dram_tensor("wkT", [D, D], F32, kind="ExternalInput").ap()
    wvT = nc.dram_tensor("wvT", [D, D], F32, kind="ExternalInput").ap()
    woT = nc.dram_tensor("woT", [D, D], F32, kind="ExternalInput").ap()
    wg_raw = nc.dram_tensor("wg_raw", [DI, D], F32, kind="ExternalInput").ap()
    wu_raw = nc.dram_tensor("wu_raw", [DI, D], F32, kind="ExternalInput").ap()
    wd_raw = nc.dram_tensor("wd_raw", [D, DI], F32, kind="ExternalInput").ap()
    gbc = {}
    for nm in ("wgam", "bgam", "wlam", "blam"):
        gbc[nm] = nc.dram_tensor(nm + "_in", [128, 64], F32, kind="ExternalInput").ap()
    c_ident = nc.dram_tensor("c_ident", [128, 128], BF16, kind="ExternalInput").ap()
    c_icol4 = nc.dram_tensor("c_icol4", [128, 512], BF16, kind="ExternalInput").ap()
    c_ones1 = nc.dram_tensor("c_ones1", [1, 128], BF16, kind="ExternalInput").ap()
    c_ones16 = nc.dram_tensor("c_ones16", [16, 128], BF16, kind="ExternalInput").ap()
    c_oh4 = nc.dram_tensor("c_oh4", [16, 64], BF16, kind="ExternalInput").ap()

    out_d = nc.dram_tensor("out", [T, D], F32, kind="ExternalOutput").ap()
    x1_kind = "ExternalOutput" if debug else "Internal"
    x1_d = nc.dram_tensor("x1_d", [T, D], BF16, kind=x1_kind).ap()
    wgq_d = nc.dram_tensor("wgq_d", [D, DI], BF16, kind=x1_kind).ap()
    wuq_d = nc.dram_tensor("wuq_d", [D, DI], BF16, kind="Internal").ap()
    wdq_d = nc.dram_tensor("wdq_d", [DI, D], BF16, kind=x1_kind).ap()
    dbg = {}
    if debug:
        dbg["q_dbg"] = nc.dram_tensor("q_dbg", [128, 8 * 512], BF16, kind="ExternalOutput").ap()
        dbg["k_dbg"] = nc.dram_tensor("k_dbg", [TILE_A, D], BF16, kind="ExternalOutput").ap()
        dbg["y_dbg"] = nc.dram_tensor("y_dbg", [TILE_A, D], BF16, kind="ExternalOutput").ap()
        dbg["rho_dbg"] = nc.dram_tensor("rho_dbg", [16, TILE_A], BF16, kind="ExternalOutput").ap()
        dbg["xt_dbg"] = nc.dram_tensor("xt_dbg", [128, 4 * 1024], BF16, kind="ExternalOutput").ap()
        dbg["xn1F_dbg"] = nc.dram_tensor("xn1F_dbg", [128, 4096], BF16, kind="ExternalOutput").ap()
        dbg["wq_dbg"] = nc.dram_tensor("wq_dbg", [128, 8 * 1024], BF16, kind="ExternalOutput").ap()
        dbg["xn1_dbg"] = nc.dram_tensor("xn1_dbg", [128, 4 * 1024], BF16, kind="ExternalOutput").ap()

    x_r = x_in.rearrange("(a p) f -> p a f", p=128)
    x1_r = x1_d.rearrange("(a p) f -> p a f", p=128)
    out_r = out_d.rearrange("(a p) f -> p a f", p=128)
    wgq_r = wgq_d.rearrange("(dc p) n -> p dc n", p=128)
    wuq_r = wuq_d.rearrange("(dc p) n -> p dc n", p=128)
    wdq_r = wdq_d.rearrange("(nc p) e -> p nc e", p=128)

    from contextlib import ExitStack
    with ExitStack() as _st:
        def SB(name, shape, dtype):
            return _st.enter_context(nc.sbuf_tensor(name, shape, dtype))

        def PS(name, shape, dtype):
            return _st.enter_context(nc.psum_tensor(name, shape, dtype))

        ident = SB("ident", [128, 128], BF16)
        icol4 = SB("icol4", [128, 512], BF16)
        ones1 = SB("ones1", [1, 128], BF16)
        ones16 = SB("ones16", [16, 128], BF16)
        oh4 = SB("oh4", [16, 64], BF16)
        cmagic = SB("cmagic", [128, 1], F32)
        cmagic2 = SB("cmagic2", [128, 1], F32)
        wgam = SB("wgam", [128, 64], F32)
        bgam = SB("bgam", [128, 64], F32)
        wlam = SB("wlam", [128, 64], F32)
        blam = SB("blam", [128, 64], F32)
        ln2st = SB("ln2st", [128, 64], F32)
        pA = PS("pA", [128, 512], F32)
        pB = PS("pB", [128, 512], F32)
        pR1 = PS("pR1", [128, 512], F32)
        pR2 = PS("pR2", [128, 512], F32)
        pPT = PS("pPT", [128, 512], F32)
        pY = PS("pY", [128, 512], F32)
        pM = PS("pM", [128, 512], F32)
        pT = PS("pT", [128, 1024], BF16)
        block = _st.enter_context(nc.Block())
        bx1d = Buf("x1_d")
        bln2 = Buf("ln2st")
        nc.const_aps.register(F32, float(MAGIC), cmagic[:])
        nc.const_aps.register(F32, -float(MAGIC) * 0.5, cmagic2[:])
        bwgq6 = [Buf(f"wgq_d{j}") for j in range(6)]
        bwuq6 = [Buf(f"wuq_d{j}") for j in range(6)]
        bwdq = Buf("wdq_d")
        b_const = Buf("consts")
        BC = {nm: b_const for nm in ("ident", "icol4", "ones1", "ones16", "oh4",
                                     "wgam", "bgam", "wlam", "blam")}
        pbuf = {id(p): Buf(nm) for nm, p in
                [("pA", pA), ("pB", pB), ("pR1", pR1), ("pR2", pR2),
                 ("pPT", pPT), ("pY", pY), ("pM", pM), ("pT", pT)]}

        def P(p):
            return pbuf[id(p)]

        for src, dst, nm in [(c_ident, ident, "ident"), (c_icol4, icol4, "icol4"),
                             (c_ones1, ones1, "ones1"), (c_ones16, ones16, "ones16"),
                             (c_oh4, oh4, "oh4"), (gbc["wgam"], wgam, "wgam"),
                             (gbc["bgam"], bgam, "bgam"), (gbc["wlam"], wlam, "wlam"),
                             (gbc["blam"], blam, "blam")]:
            tr.op("sync", (lambda s_, d_: lambda e: e.dma_start(out=d_[:], in_=s_))(src, dst),
                  writes=[BC[nm]], inc=16, grp="g_const")
        tr.op("dve", lambda e: e.memset(cmagic[:], float(MAGIC)), writes=[BC["wgam"]])
        tr.op("dve", lambda e: e.memset(cmagic2[:], -float(MAGIC) * 0.5),
              writes=[BC["wgam"]])

        with ExitStack() as _stA:
            def SA(name, shape, dtype, **kw):
                return _stA.enter_context(nc.sbuf_tensor(name, shape, dtype, **kw))

            def SA2(name, shape, dtype, **kw):
                return [SA(name + "_0", shape, dtype, **kw),
                        SA(name + "_1", shape, dtype, **kw)]

            wq_sb = SA("wq_sb", [128, 8 * 1024], BF16)
            wk_sb = SA("wk_sb", [128, 8 * 1024], BF16)
            wv_sb = SA("wv_sb", [128, 8 * 1024], BF16)
            wo_sb = SA("wo_sb", [128, 8 * 1024], BF16)
            S_sb = SA("S_sb", [128, 512], BF16)
            xt2 = SA2("xt", [128, 4 * 1024], BF16)
            xn12 = [SA("xn1", [128, 4 * 1024], BF16)] * 2
            xn1F2 = SA2("xn1F", [128, 8 * 512], BF16, is_transpose_dest=True)
            qF2 = [SA("qF", [128, 8 * 512], BF16)] * 2
            vF2 = [SA("vF", [128, 8 * 512], BF16)] * 2
            ktok2 = [SA("ktok", [128, 4 * 1024], BF16)] * 2
            vtok2 = [SA("vtok", [128, 4 * 1024], BF16, is_transpose_dest=True)] * 2
            run42 = [SA("run4", [128, 64], F32)] * 2
            din_T2 = [SA("din_T", [128, 64], F32)] * 2
            dcol2 = [SA("dcol", [128, 64], F32)] * 2
            ya = SA("ya", [128, 4 * 1024], BF16)
            yF = SA("yF", [128, 8 * 512], BF16, is_transpose_dest=True)
            scr = SA("scr", [128, 1024], F32)
            lnst = SA("lnst", [128, 40], F32)
            kst = SA("kst", [128, 6 * 64], F32)
            rho4 = SA("rho4", [128, 64], BF16)
            rho_ht = SA("rho_ht", [16, 512], BF16)
            rho0_ht = SA("rho0_ht", [16, 512], BF16)
            rhoS = SA("rhoS", [16, 512], BF16)
            din_bf = SA("din_bf", [16, 512], BF16)
            rstg = SA("rstg", [16, 64], BF16)
            rho0_sep = [SA("rho0_sep0", [1, 2048], BF16),
                        SA("rho0_sep1", [1, 2048], BF16)]
            E0 = SA("E0", [128, 512], BF16)
            E1 = SA("E1", [128, 512], BF16)
            mtp0 = SA("mtp0", [128, 512], BF16)
            mtp1 = SA("mtp1", [128, 512], BF16)
            ku0 = SA("ku0", [128, 256], BF16)
            ku1 = SA("ku1", [128, 256], BF16)
            vw0 = SA("vw0", [128, 256], BF16)
            vw1 = SA("vw1", [128, 256], BF16)
            vws = SA("vws", [128, 16], F32)
            y2s0 = SA("y2s0", [128, 256], F32)
            y2s1 = SA("y2s1", [128, 256], F32)
            wtile2 = [SA("wtile0", [128, 1408], BF16),
                      SA("wtile1", [128, 1408], BF16)]
            wttr = SA("wttr", [128, 11 * 128], BF16, is_transpose_dest=True)

            singles = ("S", "ya", "yF", "scr", "lnst", "kst", "rho4", "w4",
                       "rho_ht", "rho0_ht", "rhoS", "din_bf",
                       "rstg", "rho0_sep0", "rho0_sep1", "E0", "E1",
                       "mtp0", "mtp1",
                       "ku0", "ku1", "vw0", "vw1", "vws", "y2s0", "y2s1",
                       "wtile0", "wtile1", "wttr")
            bw = {nm: Buf(nm) for nm in singles}
            doubles = ("xt", "xn1", "xn1F", "qF", "vF", "ktok", "vtok",
                       "run4", "din_T", "dcol")
            real_doubles = ("xt", "xn1F")
            bw2 = {nm: ([Buf(nm + "_0"), Buf(nm + "_1")] if nm in real_doubles
                        else [Buf(nm)] * 2) for nm in doubles}

            for src_, dst in [(wqT, wq_sb), (wkT, wk_sb), (wvT, wv_sb), (woT, wo_sb)]:
                tr.op("gp", (lambda s_, d_: lambda e: e.dma_start(
                    out=d_[:].rearrange("p (dc e) -> p dc e", dc=8),
                    in_=s_.rearrange("(dc p) e -> p dc e", p=128)))(src_, dst),
                    writes=[bw["w4"]], inc=16, grp="g_attnw")
            tr.op("dve", lambda e: e.memset(S_sb[:], 0.0), writes=[bw["S"]])
            tr.op("dve", lambda e: e.memset(rhoS[:], 0.0), writes=[bw["rhoS"]])

            ya3 = ya[:].rearrange("p (a f) -> p a f", a=4)
            yFb = yF[:].rearrange("p (cc dc f) -> p cc dc f", cc=4, dc=8)
            wsb3 = {nm: w[:].rearrange("p (dc e) -> p dc e", dc=8)
                    for nm, w in [("wq", wq_sb), ("wk", wk_sb),
                                  ("wv", wv_sb), ("wo", wo_sb)]}

            HGRP = [[0, 2, 4, 6], [8, 10, 12, 14],
                    [1, 3, 5, 7], [9, 11, 13, 15]]
            GBASE = [0, 0, 64, 64]
            GOFF = [0, 8, 1, 9]

            def Ssl(h):
                return S_sb[(h % 2) * 64:(h % 2) * 64 + 64,
                            (h // 2) * 64:(h // 2) * 64 + 64]

            def Sgq(g):
                return S_sb[GBASE[g]:GBASE[g] + 64,
                            (g % 2) * 256:(g % 2) * 256 + 256]

            def hsel4(ap16, g):
                st_ = GOFF[g] % 2
                lo = (GOFF[g] % 8) // 2 + (GOFF[g] // 8) * 4
                lo = 4 if GOFF[g] >= 8 else 0
                return ap16[:, st_::2, :][:, lo:lo + 4, :]

            def hslice(buf3, h, cols):
                return buf3[(h % 2) * 64:(h % 2) * 64 + 64, h // 2, cols]

            class TV:

                def __init__(self, i):
                    b = i % 2
                    self.i = i
                    self.xt3 = xt2[b][:].rearrange("p (a f) -> p a f", a=4)
                    self.xn13 = xn12[b][:].rearrange("p (a f) -> p a f", a=4)
                    self.xn1 = xn12[b]
                    self.xn1F = xn1F2[b]
                    self.xn1Fb = xn1F2[b][:].rearrange(
                        "p (tt dc f) -> p tt dc f", tt=4, dc=8)
                    self.qF = qF2[b]
                    self.vF = vF2[b]
                    self.qF3 = qF2[b][:].rearrange("p (a f) -> p a f", a=8)
                    self.vF3 = vF2[b][:].rearrange("p (a f) -> p a f", a=8)
                    self.ktok = ktok2[b]
                    self.ktok3 = ktok2[b][:].rearrange("p (a f) -> p a f", a=4)
                    self.vtok = vtok2[b]
                    self.vtokb = vtok2[b][:].rearrange(
                        "p (ec tt f) -> p ec tt f", ec=8, tt=4)
                    self.run4 = run42[b]
                    self.din_T = din_T2[b]
                    self.dcol = dcol2[b]
                    for nm in doubles:
                        setattr(self, "b_" + nm, bw2[nm][b])

            def prep_a(i):
                v = TV(i)
                tr.op("gp", (lambda t_, v_: lambda e: e.dma_start(
                    out=v_.xt3, in_=x_r[:, t_ * 4:(t_ + 1) * 4, :]))(i, v),
                    writes=[v.b_xt], inc=16, grp="g_xt%d" % (i % 2))
                for tt in range(4):
                    tr.op("act", (lambda tt_, v_: lambda e: e.activation(
                        scr[:], v_.xt3[:, tt_, :], AF.Square,
                        accum_out=lnst[:, tt_:tt_ + 1]))(tt, v),
                        reads=[v.b_xt], writes=[bw["scr"], bw["lnst"]])
                    tr.op("dve", (lambda tt_, v_: lambda e: e.reduce_sum(
                        lnst[:, 4 + tt_:5 + tt_], v_.xt3[:, tt_, :], AX.X))(tt, v),
                        reads=[v.b_xt], writes=[bw["lnst"]])
                tr.op("dve", lambda e: e.tensor_scalar(
                    lnst[:, 8:12], lnst[:, 4:8], 1.0 / D, None, ALU.mult),
                    reads=[bw["lnst"]], writes=[bw["lnst"]])
                tr.op("dve", lambda e: e.tensor_scalar(
                    lnst[:, 12:16], lnst[:, 0:4], 1.0 / D, EPS_LN, ALU.mult, ALU.add),
                    reads=[bw["lnst"]], writes=[bw["lnst"]])
                tr.op("dve", lambda e: e.tensor_tensor(
                    lnst[:, 16:20], lnst[:, 8:12], lnst[:, 8:12], ALU.mult),
                    reads=[bw["lnst"]], writes=[bw["lnst"]])
                tr.op("dve", lambda e: e.tensor_tensor(
                    lnst[:, 20:24], lnst[:, 12:16], lnst[:, 16:20], ALU.subtract),
                    reads=[bw["lnst"]], writes=[bw["lnst"]])
                tr.op("act", lambda e: e.activation(
                    lnst[:, 24:28], lnst[:, 20:24], AF.Ln),
                    reads=[bw["lnst"]], writes=[bw["lnst"]])
                tr.op("act", lambda e: e.activation(
                    lnst[:, 28:32], lnst[:, 24:28], AF.Exp, scale=-0.5),
                    reads=[bw["lnst"]], writes=[bw["lnst"]])
                tr.op("dve", lambda e: e.tensor_tensor(
                    lnst[:, 32:36], lnst[:, 8:12], lnst[:, 28:32], ALU.mult),
                    reads=[bw["lnst"]], writes=[bw["lnst"]])
                tr.op("dve", lambda e: e.tensor_scalar(
                    lnst[:, 36:40], lnst[:, 32:36], -1.0, None, ALU.mult),
                    reads=[bw["lnst"]], writes=[bw["lnst"]])
                for tt in range(4):
                    tr.op("dve", (lambda tt_, v_: lambda e: e.tensor_scalar(
                        v_.xn13[:, tt_, :], v_.xt3[:, tt_, :],
                        lnst[:, 28 + tt_:29 + tt_], lnst[:, 36 + tt_:37 + tt_],
                        ALU.mult, ALU.add))(tt, v),
                        reads=[v.b_xt, bw["lnst"]], writes=[v.b_xn1])
                tr.op("sync", (lambda v_: lambda e: e.dma_start(
                    out=v_.xn1F[:].rearrange("p (a f) -> p a f", a=32),
                    in_=v_.xn1[:].rearrange("p (a f) -> p a f", a=32),
                    transpose=True))(v),
                    reads=[v.b_xn1], writes=[v.b_xn1F], inc=16,
                    grp="g_xn1F%d" % (i % 2))

            def prep_b(i):
                v = TV(i)
                for dst3, dstb, wnm in [(v.qF3, v.b_qF, "wq"), (v.vF3, v.b_vF, "wv")]:
                    for ec in range(8):
                        ps = pA if ec % 2 == 0 else pB

                        def mk(ec_, ps_, wnm_, v_):
                            def f(e):
                                ins = None
                                for dc in range(8):
                                    ins = e.matmul(
                                        ps_[:],
                                        wsb3[wnm_][:, dc, ec_ * 128:(ec_ + 1) * 128],
                                        v_.xn1Fb[:, :, dc, :],
                                        start=(dc == 0), stop=(dc == 7))
                                return ins
                            return f
                        tr.op("pe", mk(ec, ps, wnm, v),
                              reads=[bw["w4"], v.b_xn1F], writes=[P(ps)])
                        tr.op("act", (lambda ec_, ps_, d3_: lambda e: e.copy(
                            d3_[:, ec_, :], ps_[:]))(ec, ps, dst3),
                            reads=[P(ps)], writes=[dstb])
                for tt in range(4):
                    for eh in range(2):
                        ps = pA if eh == 0 else pB

                        def mkk(tt_, eh_, ps_, v_):
                            def f(e):
                                ins = None
                                for dc in range(8):
                                    ins = e.matmul(
                                        ps_[:], v_.xn1Fb[:, tt_, dc, :],
                                        wsb3["wk"][:, dc, eh_ * 512:(eh_ + 1) * 512],
                                        start=(dc == 0), stop=(dc == 7))
                                return ins
                            return f
                        tr.op("pe", mkk(tt, eh, ps, v),
                              reads=[bw["w4"], v.b_xn1F], writes=[P(ps)])
                        tr.op("act", (lambda tt_, eh_, ps_, v_: lambda e: e.copy(
                            v_.ktok3[:, tt_, eh_ * 512:(eh_ + 1) * 512], ps_[:]))(
                                tt, eh, ps, v),
                            reads=[P(ps)], writes=[v.b_ktok])
                tr.op("sync", (lambda v_: lambda e: e.dma_start(
                    out=v_.vtok[:].rearrange("p (a f) -> p a f", a=32),
                    in_=v_.vF[:].rearrange("p (a f) -> p a f", a=32),
                    transpose=True))(v),
                    reads=[v.b_vF], writes=[v.b_vtok], inc=16,
                    grp="g_vtok%d" % (i % 2))
                for tt in range(4):
                    tr.op("act", (lambda tt_, v_: lambda e: e.activation(
                        scr[:], v_.ktok3[:, tt_, :], AF.Square))(tt, v),
                        reads=[v.b_ktok], writes=[bw["scr"]])
                    tr.op("dve", (lambda tt_: lambda e: e.tensor_reduce(
                        kst[:, tt_ * 16:(tt_ + 1) * 16],
                        scr[:].rearrange("p (a b) -> p a b", b=64),
                        AX.X, ALU.add))(tt),
                        reads=[bw["scr"]], writes=[bw["kst"]])
                    tr.op("dve", (lambda tt_, v_: lambda e: e.tensor_reduce(
                        kst[:, 64 + tt_ * 16:64 + (tt_ + 1) * 16],
                        v_.ktok3[:, tt_, :].rearrange("p (a b) -> p a b", b=64),
                        AX.X, ALU.add))(tt, v),
                        reads=[v.b_ktok], writes=[bw["kst"]])
                tr.op("dve", lambda e: e.tensor_scalar(
                    kst[:, 128:192], kst[:, 0:64], 1e-24, None, ALU.max),
                    reads=[bw["kst"]], writes=[bw["kst"]])
                tr.op("act", lambda e: e.activation(
                    kst[:, 128:192], kst[:, 128:192], AF.Ln),
                    reads=[bw["kst"]], writes=[bw["kst"]])
                tr.op("act", lambda e: e.activation(
                    kst[:, 128:192], kst[:, 128:192], AF.Exp, scale=-0.5),
                    reads=[bw["kst"]], writes=[bw["kst"]])
                tr.op("dve", lambda e: e.tensor_tensor(
                    kst[:, 192:256], kst[:, 64:128], kst[:, 128:192], ALU.mult),
                    reads=[bw["kst"]], writes=[bw["kst"]])
                tr.op("dve", lambda e: e.scalar_tensor_tensor(
                    kst[:, 256:320], kst[:, 192:256], 1.0 / DK, wgam[:],
                    ALU.mult, ALU.mult),
                    reads=[bw["kst"], BC["wgam"]], writes=[bw["kst"]])
                tr.op("dve", lambda e: e.tensor_tensor(
                    kst[:, 256:320], kst[:, 256:320], bgam[:], ALU.add),
                    reads=[bw["kst"], BC["bgam"]], writes=[bw["kst"]])
                tr.op("act", lambda e: e.activation(
                    kst[:, 320:384], kst[:, 256:320], AF.Sigmoid),
                    reads=[bw["kst"]], writes=[bw["kst"]])
                tr.op("dve", lambda e: e.scalar_tensor_tensor(
                    kst[:, 256:320], kst[:, 192:256], 1.0 / DK, wlam[:],
                    ALU.mult, ALU.mult),
                    reads=[bw["kst"], BC["wlam"]], writes=[bw["kst"]])
                tr.op("dve", lambda e: e.tensor_tensor(
                    kst[:, 256:320], kst[:, 256:320], blam[:], ALU.add),
                    reads=[bw["kst"], BC["blam"]], writes=[bw["kst"]])
                tr.op("act", lambda e: e.activation(
                    kst[:, 256:320], kst[:, 256:320], AF.Sigmoid),
                    reads=[bw["kst"]], writes=[bw["kst"]])
                tr.op("dve", lambda e: e.tensor_scalar(
                    kst[:, 256:320], kst[:, 256:320], -1.0, 1.0, ALU.mult, ALU.add),
                    reads=[bw["kst"]], writes=[bw["kst"]])
                tr.op("dve", lambda e: e.tensor_tensor(
                    rho4[:], kst[:, 256:320], kst[:, 320:384], ALU.mult),
                    reads=[bw["kst"]], writes=[bw["rho4"]])
                tr.op("dve", (lambda v_: lambda e: e.tensor_scalar(
                    v_.run4[:], rho4[:], -1.0, 1.0, ALU.mult, ALU.add))(v),
                    reads=[bw["rho4"]], writes=[v.b_run4])
                tr.op("dve", (lambda v_: lambda e: e.tensor_tensor(
                    v_.run4[:], v_.run4[:], kst[:, 128:192], ALU.mult))(v),
                    reads=[v.b_run4, bw["kst"]], writes=[v.b_run4])
                for tt in range(4):
                    tr.op("pe", (lambda tt_: lambda e: e.transpose(
                        pT[0:16, 0:128], rho4[:, tt_ * 16:(tt_ + 1) * 16], ident[:]))(tt),
                        reads=[bw["rho4"], BC["ident"]], writes=[P(pT)])
                    tr.op("act", (lambda tt_: lambda e: e.copy(
                        rho_ht[:, tt_ * 128:(tt_ + 1) * 128], pT[0:16, 0:128]))(tt),
                        reads=[P(pT)], writes=[bw["rho_ht"]])
                tr.op("dve", lambda e: e.tensor_copy(rho0_ht[:], rho_ht[:]),
                      reads=[bw["rho_ht"]], writes=[bw["rho0_ht"]])
                tr.op("dve", lambda e: e.memset(
                    rho0_ht[:].rearrange("p (a f) -> p a f", a=4)[:, :, 0:1], 0.0),
                    writes=[bw["rho0_ht"]])
                tr.op("dve", lambda e: e.tensor_copy(
                    rhoS[:].rearrange("p (a f) -> p a f", a=4)[:, :, 0:1],
                    rho_ht[:].rearrange("p (a f) -> p a f", a=4)[:, :, 0:1]),
                    reads=[bw["rho_ht"]], writes=[bw["rhoS"]])
                tr.op("dve", lambda e: e.tensor_tensor_scan(
                    din_bf[:], rho0_ht[:], rhoS[:], 0.0, ALU.mult, ALU.add),
                    reads=[bw["rho0_ht"], bw["rhoS"]], writes=[bw["din_bf"]])
                for tt in range(4):
                    tr.op("pe", (lambda tt_: lambda e: e.transpose(
                        pT[0:128, 128:144], din_bf[:, tt_ * 128:(tt_ + 1) * 128],
                        ident[0:16, 0:16]))(tt),
                        reads=[bw["din_bf"], BC["ident"]], writes=[P(pT)])
                    tr.op("act", (lambda tt_, v_: lambda e: e.copy(
                        v_.din_T[:, tt_ * 16:(tt_ + 1) * 16], pT[0:128, 128:144]))(
                            tt, v),
                        reads=[P(pT)], writes=[v.b_din_T])
                tr.op("dve", lambda e: e.tensor_tensor(
                    rstg[:].rearrange("p (c j) -> p c j", c=4),
                    oh4[:].rearrange("p (c j) -> p c j", c=4),
                    din_bf[:, 127::128].unsqueeze(2).broadcast_to([16, 4, 16]),
                    ALU.mult),
                    reads=[bw["din_bf"], BC["oh4"]], writes=[bw["rstg"]])
                tr.op("pe", lambda e: e.matmul(
                    pM[0:128, 256:320], ones16[:], rstg[:], start=True, stop=True),
                    reads=[BC["ones16"], bw["rstg"]], writes=[P(pM)])
                tr.op("dve", (lambda v_: lambda e: e.tensor_copy(
                    v_.dcol[:], pM[0:128, 256:320]))(v),
                    reads=[P(pM)], writes=[v.b_dcol])
                if debug and i == 0:
                    tr.op("sync", lambda e: e.dma_start(
                        out=dbg["rho_dbg"], in_=rho_ht[:]),
                        reads=[bw["rho_ht"]], inc=16, grp="g_dbg0")
                    tr.op("sync", (lambda v_: lambda e: e.dma_start(
                        out=dbg["xt_dbg"], in_=v_.xt3[:, :, :].rearrange(
                            "p a f -> p (a f)") if False else xt2[0][:]))(v),
                        reads=[v.b_xt], inc=16, grp="g_dbg1")
                    tr.op("sync", lambda e: e.dma_start(
                        out=dbg["xn1_dbg"], in_=xn12[0][:]),
                        reads=[v.b_xn1], inc=16, grp="g_dbg2")
                    tr.op("sync", lambda e: e.dma_start(
                        out=dbg["xn1F_dbg"], in_=xn1F2[0][:]),
                        reads=[v.b_xn1F], inc=16, grp="g_dbg6")
                    tr.op("sync", lambda e: e.dma_start(
                        out=dbg["wq_dbg"], in_=wq_sb[:]),
                        reads=[bw["w4"]], inc=16, grp="g_dbg7")
                    tr.op("sync", lambda e: e.dma_start(
                        out=dbg["q_dbg"], in_=qF2[0][:]),
                        reads=[v.b_qF], inc=16, grp="g_dbg3")
                    tr.op("sync", (lambda v_: lambda e: e.dma_start(
                        out=dbg["k_dbg"].rearrange("(a p) f -> p a f", p=128),
                        in_=v_.ktok3))(v),
                        reads=[v.b_ktok], inc=16, grp="g_dbg4")

            def chunks(i):
                v = TV(i)
                def emit_bc_scan(cc, g):
                    sep = rho0_sep[cc % 2]
                    sepb = bw["rho0_sep%d" % (cc % 2)]
                    if g == 0:
                        tr.op("sync", (lambda cc_, sep_: lambda e: e.dma_start(
                            out=sep_[0:1, :].rearrange("q (h t) -> q h t", h=16),
                            in_=rho0_ht[:, cc_ * 128:(cc_ + 1) * 128]))(cc, sep),
                            reads=[bw["rho0_ht"]], writes=[sepb], inc=16,
                            grp="g_rsep%d" % (cc % 2))
                    pRho = pR1 if g % 2 == 0 else pR2
                    E_sb = E0 if g % 2 == 0 else E1
                    Ebuf = bw["E0"] if g % 2 == 0 else bw["E1"]

                    def mk_bc(g_, pRho_, sep_):
                        def f(e):
                            rhs = hsel4(sep_[0:1, :].rearrange(
                                "q (hh t) -> q hh t", hh=16), g_)
                            return e.matmul(pRho_[:], ones1[:], rhs,
                                            start=True, stop=True)
                        return f
                    tr.op("pe", mk_bc(g, pRho, sep),
                          reads=[BC["ones1"], sepb], writes=[P(pRho)])
                    tr.op("dve", (lambda E_, pRho_: lambda e: e.tensor_tensor_scan(
                        E_[:], pRho_[:], icol4[:], 0.0, ALU.mult, ALU.add))(
                            E_sb, pRho),
                        reads=[P(pRho), BC["icol4"]], writes=[Ebuf])

                emit_bc_scan(0, 0)
                for cc in range(4):
                    for g in range(4):
                        if g < 3:
                            emit_bc_scan(cc, g + 1)
                        elif cc < 3:
                            emit_bc_scan(cc + 1, 0)
                        heads = HGRP[g]
                        E_sb = E0 if g % 2 == 0 else E1
                        Ebuf = bw["E0"] if g % 2 == 0 else bw["E1"]
                        mtp = mtp0 if g % 2 == 0 else mtp1
                        Mbuf = bw["mtp0"] if g % 2 == 0 else bw["mtp1"]
                        ku = ku0 if g % 2 == 0 else ku1
                        Kbuf = bw["ku0"] if g % 2 == 0 else bw["ku1"]
                        vw = vw0 if g % 2 == 0 else vw1
                        Vbuf = bw["vw0"] if g % 2 == 0 else bw["vw1"]
                        y2s = y2s0 if g % 2 == 0 else y2s1
                        Ybuf = bw["y2s0"] if g % 2 == 0 else bw["y2s1"]

                        def mk_pt(cc_, heads_, v_):
                            def f(e):
                                ins = None
                                for j, h in enumerate(heads_):
                                    cols = slice(cc_ * 128, (cc_ + 1) * 128)
                                    ins = e.matmul(
                                        pPT[:, j * 128:(j + 1) * 128],
                                        hslice(v_.vF3, h, cols),
                                        hslice(v_.qF3, h, cols),
                                        start=True, stop=True)
                                return ins
                            return f
                        tr.op("pe", mk_pt(cc, heads, v),
                              reads=[v.b_vF, v.b_qF], writes=[P(pPT)])
                        tr.op("dve", (lambda mtp_, E_: lambda e: e.tensor_tensor(
                            mtp_[:], E_[:], pPT[:], ALU.mult))(mtp, E_sb),
                            reads=[Ebuf, P(pPT)], writes=[Mbuf])
                        kt4 = hsel4(v.ktok3[:, cc, :].rearrange(
                            "p (hh w) -> p hh w", hh=16), g)
                        run_q = hsel4(v.run4[:].rearrange(
                            "p (c hh) -> p c hh", c=4)[:, cc, :].unsqueeze(2), g)
                        tr.op("dve", (lambda ku_, kt4_, rq_: lambda e: e.tensor_tensor(
                            ku_[:].rearrange("p (j w) -> p j w", j=4), kt4_,
                            rq_.broadcast_to([128, 4, 64]), ALU.mult))(ku, kt4, run_q),
                            reads=[v.b_ktok, v.b_run4], writes=[Kbuf])

                        def mk_vws(E_, rq_):
                            def f(e):
                                el = E_[:].rearrange(
                                    "p (j t) -> p j t", j=4)[:, :, 127:128]
                                return e.tensor_tensor(
                                    vws[:, 0:4].unsqueeze(2), el, rq_, ALU.mult)
                            return f
                        tr.op("dve", mk_vws(E_sb, run_q),
                              reads=[Ebuf, v.b_run4], writes=[bw["vws"]])

                        def mk_vw(cc_, g_, vw_, v_):
                            def f(e):
                                ec0 = (GOFF[g_] % 8) // 2 + (GOFF[g_] // 8) * 4
                                woff = (GOFF[g_] % 2) * 64
                                vt4 = v_.vtokb[:, ec0:ec0 + 4, cc_, woff:woff + 64]
                                return e.tensor_tensor(
                                    vw_[:].rearrange("p (j w) -> p j w", j=4), vt4,
                                    vws[:, 0:4].unsqueeze(2).broadcast_to(
                                        [128, 4, 64]), ALU.mult)
                            return f
                        tr.op("dve", mk_vw(cc, g, vw, v),
                              reads=[v.b_vtok, bw["vws"]], writes=[Vbuf])

                        def mk_y(cc_, g_, heads_, mtp_, ku_, v_):
                            def f(e):
                                ins = None
                                for j, h in enumerate(heads_):
                                    cols = slice(cc_ * 128, (cc_ + 1) * 128)
                                    e.matmul(pY[:, j * 64:(j + 1) * 64],
                                             mtp_[:, j * 128:(j + 1) * 128],
                                             ku_[:, j * 64:(j + 1) * 64],
                                             start=True, stop=True)
                                    ins = e.matmul(
                                        pY[:, 256 + j * 64:256 + (j + 1) * 64],
                                        hslice(v_.qF3, h, cols), Ssl(h),
                                        start=True, stop=True)
                                return ins
                            return f
                        tr.op("pe", mk_y(cc, g, heads, mtp, ku, v),
                              reads=[Mbuf, Kbuf, v.b_qF, bw["S"]], writes=[P(pY)])
                        def mk_y2s(cc_, g_, y2s_, v_):
                            def f(e):
                                dq = hsel4(v_.din_T[:].rearrange(
                                    "p (c hh) -> p c hh", c=4)[:, cc_, :]
                                    .unsqueeze(2), g_)
                                return e.tensor_tensor(
                                    y2s_[:].rearrange("p (j w) -> p j w", j=4),
                                    pY[:, 256:512].rearrange(
                                        "p (j w) -> p j w", j=4),
                                    dq.broadcast_to([128, 4, 64]), ALU.mult)
                            return f
                        tr.op("dve", mk_y2s(cc, g, y2s, v),
                              reads=[P(pY), v.b_din_T], writes=[Ybuf])

                        def mk_ya(cc_, g_, y2s_):
                            def f(e):
                                out = hsel4(ya3[:, cc_, :].rearrange(
                                    "p (hh w) -> p hh w", hh=16), g_)
                                return e.tensor_tensor(
                                    out, pY[0:128, 0:256].rearrange(
                                        "p (j w) -> p j w", j=4),
                                    y2s_[:].rearrange("p (j w) -> p j w", j=4),
                                    ALU.add)
                            return f
                        tr.op("dve", mk_ya(cc, g, y2s),
                              reads=[P(pY), Ybuf], writes=[bw["ya"]])

                        def mk_dl(cc_, g_, heads_, vw_, v_):
                            def f(e):
                                ins = None
                                rb_ = GBASE[g_]
                                tp = (0, 64) if rb_ else None
                                for j, h in enumerate(heads_):
                                    ins = e.matmul(
                                        pM[rb_:rb_ + 64, j * 64:(j + 1) * 64],
                                        vw_[:, j * 64:(j + 1) * 64],
                                        v_.ktok3[:, cc_, h * 64:(h + 1) * 64],
                                        start=True, stop=True, tile_position=tp)
                                return ins
                            return f
                        tr.op("pe", mk_dl(cc, g, heads, vw, v),
                              reads=[Vbuf, v.b_ktok], writes=[P(pM)])

                        def mk_st1(cc_, g_, v_):
                            def f(e):
                                rb_ = GBASE[g_]
                                sg = Sgq(g_)
                                dq = hsel4(v_.dcol[rb_:rb_ + 64, :].rearrange(
                                    "p (c hh) -> p c hh", c=4)[:, cc_, :].unsqueeze(2),
                                    g_)
                                return e.tensor_tensor(
                                    sg.rearrange("p (j w) -> p j w", j=4),
                                    sg.rearrange("p (j w) -> p j w", j=4),
                                    dq.broadcast_to([64, 4, 64]), ALU.mult)
                            return f

                        def mk_st2(g_):
                            def f(e):
                                rb_ = GBASE[g_]
                                return e.tensor_tensor(
                                    Sgq(g_), Sgq(g_), pM[rb_:rb_ + 64, 0:256],
                                    ALU.add)
                            return f
                        tr.op("dve", mk_st1(cc, g, v),
                              reads=[v.b_dcol, bw["S"]], writes=[bw["S"]])
                        tr.op("dve", mk_st2(g),
                              reads=[P(pM), bw["S"]], writes=[bw["S"]])
                        _emit_wq(1)
                if debug and i == 0:
                    tr.op("sync", lambda e: e.dma_start(
                        out=dbg["y_dbg"].rearrange("(a p) f -> p a f", p=128),
                        in_=ya3),
                        reads=[bw["ya"]], inc=16, grp="g_dbg5")

            def finish(i):
                v = TV(i)
                tr.op("sync", lambda e: e.dma_start(
                    out=yF[:].rearrange("p (a f) -> p a f", a=32),
                    in_=ya[:].rearrange("p (a f) -> p a f", a=32),
                    transpose=True),
                    reads=[bw["ya"]], writes=[bw["yF"]], inc=16, grp="g_yF")
                for tt in range(4):
                    for eh in range(2):
                        ps = pA if eh == 0 else pB

                        def mko(tt_, eh_, ps_):
                            def f(e):
                                ins = None
                                for dc in range(8):
                                    ins = e.matmul(
                                        ps_[:], yFb[:, tt_, dc, :],
                                        wsb3["wo"][:, dc, eh_ * 512:(eh_ + 1) * 512],
                                        start=(dc == 0), stop=(dc == 7))
                                return ins
                            return f
                        tr.op("pe", mko(tt, eh, ps),
                              reads=[bw["w4"], bw["yF"]], writes=[P(ps)])
                        tr.op("dve", (lambda tt_, eh_, ps_, v_: lambda e: e.tensor_tensor(
                            v_.xt3[:, tt_, eh_ * 512:(eh_ + 1) * 512], ps_[:],
                            v_.xt3[:, tt_, eh_ * 512:(eh_ + 1) * 512], ALU.add))(
                                tt, eh, ps, v),
                            reads=[P(ps), v.b_xt], writes=[v.b_xt])
                for tt in range(4):
                    a = i * 4 + tt
                    tr.op("act", (lambda tt_, a_, v_: lambda e: e.activation(
                        scr[:], v_.xt3[:, tt_, :], AF.Square,
                        accum_out=ln2st[:, a_:a_ + 1]))(tt, a, v),
                        reads=[v.b_xt], writes=[bw["scr"], bln2])
                    tr.op("act", (lambda tt_, a_, v_: lambda e: e.activation(
                        scr[:], v_.xt3[:, tt_, :], AF.Identity,
                        accum_out=ln2st[:, 32 + a_:33 + a_]))(tt, a, v),
                        reads=[v.b_xt], writes=[bw["scr"], bln2])
                tr.op("sync", (lambda t_, v_: lambda e: e.dma_start(
                    out=x1_r[:, t_ * 4:(t_ + 1) * 4, :], in_=v_.xt3))(i, v),
                    reads=[v.b_xt], writes=[bx1d], inc=16, grp="g_x1d")

            quantA = make_quant(tr, SA, "qa", Buf, nsets=2, maxw=704,
                                share_u=None, cbuf=BC["wgam"], pipe3=True)
            _wq_raw_jobs = []

            def _mk_wq(raw_r, dst_r, nrows, width, nblk_tr, dram_buf0, lst):
                nhalf = 2 if width == 2816 else 1
                hw_ = width // nhalf
                pw = 704 if width == 2816 else 512
                nbh = nblk_tr // nhalf
                for rt in range(nrows):
                    if dram_buf0 is None:
                        dram_buf = (bwgq6 if dst_r is wgq_r else bwuq6)[
                            min(rt // 4, 5)]
                    else:
                        dram_buf = dram_buf0
                    for hh in range(nhalf):
                        lst.append(dict(rt=rt, hh=hh, raw_r=raw_r, dst_r=dst_r,
                                        hw=hw_, nbh=nbh, dram_buf=dram_buf,
                                        pw=pw))

            _jg, _ju, _jd = [], [], []
            _mk_wq(wg_raw, wgq_r, 22, 1024, 8, None, _jg)
            _mk_wq(wu_raw, wuq_r, 22, 1024, 8, None, _ju)
            _mk_wq(wd_raw, wdq_r, 8, 2816, 22, bwdq, _jd)
            _jobs = []
            for a, b2 in zip(_jg, _ju):
                _jobs.append(a)
                _jobs.append(b2)
            _jobs.extend(_jd)

            def _wq_load(jj):
                j = _jobs[jj]
                wt = wtile2[jj % 2]
                wtb = bw["wtile%d" % (jj % 2)]
                tr.op("gp", (lambda j_, wt_: lambda e: e.dma_start(
                    out=wt_[:, 0:j_["hw"]],
                    in_=j_["raw_r"][j_["rt"] * 128:(j_["rt"] + 1) * 128,
                                    j_["hh"] * j_["hw"]:(j_["hh"] + 1) * j_["hw"]]))(
                        j, wt),
                    writes=[wtb], inc=16, grp="g_wtile%d" % (jj % 2))

            def _wq_piece(jj, pc, stage):
                j = _jobs[jj]
                wt = wtile2[jj % 2]
                wtb = bw["wtile%d" % (jj % 2)]
                quantA(wt[:, pc * j["pw"]:(pc + 1) * j["pw"]], j["pw"],
                       [wtb], [wtb], tr.op, 2 * jj + pc, stage=stage)

            def _wq_store(jj):
                j = _jobs[jj]
                wt = wtile2[jj % 2]
                wtb = bw["wtile%d" % (jj % 2)]
                nbh_ = j["nbh"]
                tr.op("sync", (lambda j_, wt_, n_: lambda e: e.dma_start(
                    out=wttr[:, 0:n_ * 128].rearrange("p (a f) -> p a f", a=n_),
                    in_=wt_[:, 0:j_["hw"]].rearrange("p (a f) -> p a f", a=n_),
                    transpose=True))(j, wt, nbh_),
                    reads=[wtb], writes=[bw["wttr"]], inc=16, grp="g_wttr")
                tr.op("gp", (lambda j_, n_: lambda e: e.dma_start(
                    out=j_["dst_r"][:, j_["hh"] * n_:(j_["hh"] + 1) * n_,
                                    j_["rt"] * 128:(j_["rt"] + 1) * 128],
                    in_=wttr[:, 0:n_ * 128].rearrange("p (a f) -> p a f", a=n_)))(
                        j, nbh_),
                    reads=[bw["wttr"]], writes=[j["dram_buf"]], inc=16,
                    grp="g_wscr")

            NP = 2 * len(_jobs)
            _wqs = {"s": 0}

            def _emit_wq(nslots):
                for _ in range(nslots):
                    s = _wqs["s"]
                    if s > NP + 2:
                        return
                    _wqs["s"] += 1
                    if s == 0:
                        _wq_load(0)
                        _wq_load(1)
                        continue
                    p = s - 3
                    if 0 <= p < NP:
                        _wq_piece(p // 2, p % 2, "c")
                        if p % 2 == 1:
                            _wq_store(p // 2)
                            if p // 2 + 2 < len(_jobs):
                                _wq_load(p // 2 + 2)
                    p = s - 2
                    if 0 <= p < NP:
                        _wq_piece(p // 2, p % 2, "b")
                    p = s - 1
                    if 0 <= p < NP:
                        _wq_piece(p // 2, p % 2, "a")

            prep_a(0)
            prep_b(0)
            for i in range(NT_A):
                if i + 1 < NT_A:
                    prep_a(i + 1)
                chunks(i)
                finish(i)
                if i + 1 < NT_A:
                    prep_b(i + 1)
            _emit_wq(999)
            globals()["_EMIT_WQ"] = lambda n: None

        bar = tr.barrier()
        with ExitStack() as _stB:
            def SBB(name, shape, dtype, **kw):
                return _stB.enter_context(nc.sbuf_tensor(name, shape, dtype, **kw))

            x1t = SBB("x1t", [128, 12 * 1024], BF16)
            xn2 = SBB("xn2", [128, 8 * 1024], BF16)
            xqF = SBB("xqF", [128, 8 * 1024], BF16, is_transpose_dest=True)
            h_t = SBB("h_t", [128, 8 * 2816], BF16)
            hqF0 = SBB("hqF0", [128, 22 * 128], BF16, is_transpose_dest=True)
            hqF1 = hqF0
            wgc0 = SBB("wgc0", [128, 8 * 512], BF16)
            wgc1 = SBB("wgc1", [128, 8 * 512], BF16)
            wuc0 = SBB("wuc0", [128, 8 * 512], BF16)
            wuc1 = SBB("wuc1", [128, 8 * 512], BF16)
            wd_sb = SBB("wd_sb", [128, 22 * 1024], BF16)
            sg0 = SBB("sg0", [128, 512], BF16)
            sg1 = SBB("sg1", [128, 512], BF16)
            ost0 = SBB("ost0", [128, 1024], F32)
            ost1 = SBB("ost1", [128, 1024], F32)
            lnst2 = SBB("lnst2", [128, 80], F32)
            bb = {nm: Buf(nm) for nm in
                  ("h_t", "hqF0", "hqF1", "wgc0", "wgc1",
                   "wuc0", "wuc1", "wd", "sg0", "sg1", "ost0", "ost1",
                   "lnst2")}
            bb["hqF1"] = bb["hqF0"]
            bb_x1tq = [Buf("x1tq0"), Buf("x1tq1"), Buf("x1tq2")]
            bb_xn2q = [Buf("xn2q0"), Buf("xn2q1")]
            bb_xqFq = [Buf("xqFq0"), Buf("xqFq1")]
            first = {s: True for s in tr.streams}

            _orig_op = tr.op

            def opB(stream, fn, reads=(), writes=(), inc=1, grp=None):
                if first[stream]:
                    first[stream] = False
                    reads = list(reads) + bar
                return _orig_op(stream, fn, reads=reads, writes=writes, inc=inc,
                                grp=grp)

            quantB = make_quant(tr, SBB, "qb", Buf, nsets=2, maxw=512,
                                cbuf=BC["wgam"], pipe3=True)

            qB_q = []
            qst = {"a": 0, "b": 0, "c": 0}

            def qB_push(xap, width, rb, wb):
                qB_q.append((xap, width, rb, wb))
                return len(qB_q) - 1

            def qB_step():
                if qst["b"] > qst["c"]:
                    p = qst["c"]
                    x, w, rb, wb = qB_q[p]
                    quantB(x, w, rb, wb, opB, p, stage="c")
                    qst["c"] += 1
                if qst["a"] > qst["b"]:
                    p = qst["b"]
                    x, w, rb, wb = qB_q[p]
                    quantB(x, w, rb, wb, opB, p, stage="b")
                    qst["b"] += 1
                if qst["a"] < len(qB_q):
                    p = qst["a"]
                    x, w, rb, wb = qB_q[p]
                    quantB(x, w, rb, wb, opB, p, stage="a")
                    qst["a"] += 1

            def qB_drain(pidx):
                while qst["c"] <= pidx:
                    qB_step()



            x1t3 = x1t[:].rearrange("p (a f) -> p a f", a=12)

            def slotq(i, q):
                return (2 * i + q) % 3
            xn23 = xn2[:].rearrange("p (a f) -> p a f", a=8)
            xqFb = xqF[:].rearrange("p (tt dc f) -> p tt dc f", tt=8, dc=8)
            ht3 = h_t[:].rearrange("p (a f) -> p a f", a=8)
            wd3 = wd_sb[:].rearrange("p (nc e) -> p nc e", nc=22)

            def prep_work(i, q):
                sq = slotq(i, q)
                bq = bb_x1tq[sq]
                opB("gp", (lambda t_, q_, sq_: lambda e: e.dma_start(
                    out=x1t3[:, sq_ * 4:(sq_ + 1) * 4, :],
                    in_=x1_r[:, t_ * 8 + q_ * 4:t_ * 8 + (q_ + 1) * 4, :]))(
                        i, q, sq),
                    reads=[bx1d], writes=[bq], inc=16, grp="g_x1tq%d" % sq)
                o = q * 4
                a0 = i * 8 + q * 4

                def L(base):
                    return lnst2[:, base + o:base + o + 4]
                opB("dve", (lambda a0_: lambda e: e.tensor_scalar(
                    L(16), ln2st[:, 32 + a0_:36 + a0_], 1.0 / D, None,
                    ALU.mult))(a0),
                    reads=[bln2], writes=[bb["lnst2"]])
                opB("dve", (lambda a0_: lambda e: e.tensor_scalar(
                    L(24), ln2st[:, a0_:a0_ + 4], 1.0 / D, EPS_LN,
                    ALU.mult, ALU.add))(a0),
                    reads=[bln2], writes=[bb["lnst2"]])
                opB("dve", lambda e: e.tensor_tensor(
                    L(32), L(16), L(16), ALU.mult),
                    reads=[bb["lnst2"]], writes=[bb["lnst2"]])
                opB("dve", lambda e: e.tensor_tensor(
                    L(40), L(24), L(32), ALU.subtract),
                    reads=[bb["lnst2"]], writes=[bb["lnst2"]])
                opB("act", lambda e: e.activation(L(48), L(40), AF.Ln),
                    reads=[bb["lnst2"]], writes=[bb["lnst2"]])
                opB("act", lambda e: e.activation(L(56), L(48), AF.Exp, scale=-0.5),
                    reads=[bb["lnst2"]], writes=[bb["lnst2"]])
                opB("dve", lambda e: e.tensor_tensor(
                    L(64), L(16), L(56), ALU.mult),
                    reads=[bb["lnst2"]], writes=[bb["lnst2"]])
                opB("dve", lambda e: e.tensor_scalar(
                    L(72), L(64), -1.0, None, ALU.mult),
                    reads=[bb["lnst2"]], writes=[bb["lnst2"]])
                for tt in range(q * 4, q * 4 + 4):
                    st_ = sq * 4 + (tt % 4)
                    opB("dve", (lambda tt_, st2_: lambda e: e.tensor_scalar(
                        xn23[:, tt_, :], x1t3[:, st2_, :],
                        lnst2[:, 56 + tt_:57 + tt_], lnst2[:, 72 + tt_:73 + tt_],
                        ALU.mult, ALU.add))(tt, st_),
                        reads=[bq, bb["lnst2"]], writes=[bb_xn2q[q]])
                last = None
                for tt in range(q * 4, q * 4 + 4):
                    for pc in range(2):
                        last = qB_push(xn23[:, tt, pc * 512:(pc + 1) * 512],
                                       512, [bb_xn2q[q]], [bb_xn2q[q]])
                        qB_step()
                prep_last[q] = last

            def prep_xbar(i, q):
                qB_drain(prep_last[q])
                opB("sync", (lambda q_: lambda e: e.dma_start(
                    out=xqF[:].rearrange("p (a f) -> p a f", a=64)[
                        :, q_ * 32:(q_ + 1) * 32, :],
                    in_=xn2[:].rearrange("p (a f) -> p a f", a=64)[
                        :, q_ * 32:(q_ + 1) * 32, :],
                    transpose=True))(q),
                    reads=[bb_xn2q[q]], writes=[bb_xqFq[q]], inc=16,
                    grp="g_xqFq%d" % q)

            prep_last = {}

            NKW = [512] * 5 + [256]
            NKOFF = [0, 512, 1024, 1536, 2048, 2560]

            def b_gate_nck(i, nck, fuse_tail):
                nw = NKW[nck]
                no = NKOFF[nck]
                wg_c = wgc0 if nck % 2 == 0 else wgc1
                wu_c = wuc0 if nck % 2 == 0 else wuc1
                Bg = bb["wgc0"] if nck % 2 == 0 else bb["wgc1"]
                Bu = bb["wuc0"] if nck % 2 == 0 else bb["wuc1"]
                opB("sync", (lambda no_, nw_, w_: lambda e: e.dma_start(
                    out=w_[:, 0:8 * nw_].rearrange("p (dc f) -> p dc f", dc=8),
                    in_=wgq_r[:, :, no_:no_ + nw_]))(no, nw, wg_c),
                    reads=[bwgq6[nck]], writes=[Bg], inc=16,
                    grp="g_wgc0" if nck % 2 == 0 else "g_wgc1")
                opB("sync", (lambda no_, nw_, w_: lambda e: e.dma_start(
                    out=w_[:, 0:8 * nw_].rearrange("p (dc f) -> p dc f", dc=8),
                    in_=wuq_r[:, :, no_:no_ + nw_]))(no, nw, wu_c),
                    reads=[bwuq6[nck]], writes=[Bu], inc=16,
                    grp="g_wuc0" if nck % 2 == 0 else "g_wuc1")
                for tt in range(8):
                    pg = pA if tt % 2 == 0 else pR1
                    pu = pB if tt % 2 == 0 else pR2
                    sg = sg0 if tt % 2 == 0 else sg1
                    Bs = bb["sg0"] if tt % 2 == 0 else bb["sg1"]

                    def mkg(tt_, nw_, w_, ps_):
                        def f(e):
                            ins = None
                            for dc in range(8):
                                ins = e.matmul(
                                    ps_[:, 0:nw_],
                                    xqFb[:, tt_, dc, :],
                                    w_[:, 0:8 * nw_].rearrange(
                                        "p (dc f) -> p dc f", dc=8)[:, dc, :],
                                    start=(dc == 0), stop=(dc == 7))
                            return ins
                        return f
                    opB("pe", mkg(tt, nw, wg_c, pg),
                        reads=[bb_xqFq[tt // 4], Bg], writes=[P(pg)])
                    opB("pe", mkg(tt, nw, wu_c, pu),
                        reads=[bb_xqFq[tt // 4], Bu], writes=[P(pu)])
                    opB("act", (lambda nw_, ps_, sg_: lambda e: e.activation(
                        sg_[:, 0:nw_], ps_[:, 0:nw_], AF.Silu))(nw, pg, sg),
                        reads=[P(pg)], writes=[Bs])
                    opB("dve", (lambda tt_, no_, nw_, ps_, sg_: lambda e: e.tensor_tensor(
                        ht3[:, tt_, no_:no_ + nw_], ps_[:, 0:nw_],
                        sg_[:, 0:nw_], ALU.mult))(tt, no, nw, pu, sg),
                        reads=[P(pu), Bs], writes=[bb["h_t"]])
                    hlast[tt] = qB_push(ht3[:, tt, no:no + nw], nw,
                                        [bb["h_t"]], [bb["h_t"]])
                    qB_step()
                    if fuse_tail and tt >= 1:
                        b_tail_tt(i, tt - 1)
                        if tt - 1 == 3 and i + 1 < NT_B:
                            prep_work(i + 1, 1)
                            prep_xbar(i + 1, 0)
                if fuse_tail:
                    b_tail_tt(i, 7)
                    if i + 1 < NT_B:
                        prep_xbar(i + 1, 1)

            hlast = {}

            def b_tail_tt(i, tt):
                qB_drain(hlast[tt])
                hqF = hqF0 if tt % 2 == 0 else hqF1
                Bh = bb["hqF0"] if tt % 2 == 0 else bb["hqF1"]
                opB("sync", (lambda tt_, hq_: lambda e: e.dma_start(
                    out=hq_[:].rearrange("p (a f) -> p a f", a=22),
                    in_=ht3[:, tt_, :].rearrange("p (a f) -> p a f", a=22),
                    transpose=True))(tt, hqF),
                    reads=[bb["h_t"]], writes=[Bh], inc=16,
                    grp="g_hqF0" if tt % 2 == 0 else "g_hqF1")
                hq3 = hqF[:].rearrange("p (a f) -> p a f", a=22)
                pdd = (pPT, pY)
                for eh in range(2):
                    pd = pdd[eh]

                    def mkd(tt_, eh_, hq3_, pd_):
                        def f(e):
                            ins = None
                            for ncc in range(22):
                                ins = e.matmul(
                                    pd_[:], hq3_[:, ncc, :],
                                    wd3[:, ncc, eh_ * 512:(eh_ + 1) * 512],
                                    start=(ncc == 0), stop=(ncc == 21))
                            return ins
                        return f
                    opB("pe", mkd(tt, eh, hq3, pd),
                        reads=[Bh, bb["wd"]], writes=[P(pd)])
                ost = ost0 if tt % 2 == 0 else ost1
                Bo = bb["ost0"] if tt % 2 == 0 else bb["ost1"]
                st_ = slotq(i, tt // 4) * 4 + (tt % 4)
                for eh in range(2):
                    pd = pdd[eh]
                    opB("dve", (lambda st2_, eh_, pd_, o_: lambda e: e.tensor_tensor(
                        o_[:, eh_ * 512:(eh_ + 1) * 512], pd_[:],
                        x1t3[:, st2_, eh_ * 512:(eh_ + 1) * 512], ALU.add))(
                            st_, eh, pd, ost),
                        reads=[P(pd), bb_x1tq[slotq(i, tt // 4)]], writes=[Bo])
                opB("sync", (lambda t_, tt_, o_: lambda e: e.dma_start(
                    out=out_r[:, t_ * 8 + tt_, :], in_=o_[:]))(i, tt, ost),
                    reads=[Bo], inc=16,
                    grp="g_out0" if tt % 2 == 0 else "g_out1")

            _emit_wq = globals()["_EMIT_WQ"]
            prep_work(0, 0)
            prep_work(0, 1)
            prep_xbar(0, 0)
            prep_xbar(0, 1)
            wd_loaded = [False]

            def _load_wd():
                if wd_loaded[0]:
                    return
                wd_loaded[0] = True
                opB("sync", lambda e: e.dma_start(
                    out=wd_sb[:].rearrange("p (nc e) -> p nc e", nc=22),
                    in_=wdq_r),
                    reads=[bwdq], writes=[bb["wd"]], inc=16, grp="g_wdsb")

            for i in range(NT_B):
                for nck in range(5):
                    b_gate_nck(i, nck, False)
                    if nck == 2 and i + 1 < NT_B:
                        prep_work(i + 1, 0)
                _load_wd()
                b_gate_nck(i, 5, True)
            tr.finalize(nc, _st, block)
    return nc


def kernel(L0_attn_x, L0_attn_Wq, L0_attn_Wk, L0_attn_Wv, L0_attn_Wo,
           L0_attn_W_gamma, L0_attn_b_gamma, L0_attn_W_lambda, L0_attn_b_lambda,
           L0_ffn_w_gate, L0_ffn_w_up, L0_ffn_w_down):
    ident, icol4, ones1, ones16, oh4 = _host_consts()
    x = np.ascontiguousarray(np.asarray(L0_attn_x, np.float32))
    wqT = np.ascontiguousarray(np.asarray(L0_attn_Wq, np.float32).T)
    wkT = np.ascontiguousarray(np.asarray(L0_attn_Wk, np.float32).T)
    wvT = np.ascontiguousarray(np.asarray(L0_attn_Wv, np.float32).T)
    woT = np.ascontiguousarray(np.asarray(L0_attn_Wo, np.float32).T)
    wg = np.ascontiguousarray(np.asarray(L0_ffn_w_gate, np.float32))
    wu = np.ascontiguousarray(np.asarray(L0_ffn_w_up, np.float32))
    wd = np.ascontiguousarray(np.asarray(L0_ffn_w_down, np.float32))

    def bc64(v):
        return np.ascontiguousarray(
            np.broadcast_to(np.tile(np.asarray(v, np.float32), 4), (128, 64)))

    nc = build_nc(debug=False)
    in_maps = []
    for b in range(8):
        in_maps.append({
            "x": np.ascontiguousarray(x[b]),
            "wqT": wqT, "wkT": wkT, "wvT": wvT, "woT": woT,
            "wg_raw": wg, "wu_raw": wu, "wd_raw": wd,
            "wgam_in": bc64(L0_attn_W_gamma), "bgam_in": bc64(L0_attn_b_gamma),
            "wlam_in": bc64(L0_attn_W_lambda), "blam_in": bc64(L0_attn_b_lambda),
            "c_ident": ident, "c_icol4": icol4, "c_ones1": ones1,
            "c_ones16": ones16, "c_oh4": oh4,
        })
    res = run_bass_kernel_spmd(nc, in_maps, core_ids=list(range(8)))
    return np.stack([np.asarray(r["out"], np.float32) for r in res.results])


# version 22
# speedup vs baseline: 1.0137x; 1.0137x over previous
import numpy as np
import ml_dtypes
import concourse.bass as bass
import concourse.mybir as mybir
from concourse.bass_utils import run_bass_kernel_spmd

F32 = mybir.dt.float32
BF16 = mybir.dt.bfloat16
U8 = mybir.dt.uint8
I16 = mybir.dt.int16
AF = mybir.ActivationFunctionType
ALU = mybir.AluOpType
AX = mybir.AxisListType
NBF = ml_dtypes.bfloat16

T, D, H, DK = 4096, 1024, 16, 64
C = 128
TILE_A = 512
TILE_B = 1024
NT_A = T // TILE_A
NT_B = T // TILE_B
DI = 2816
NCK = DI // 256
EPS_LN = 1e-5
MAGIC = np.float32(1.5 * 2 ** 23)


class _Ev:
    __slots__ = ("op",)

    def __init__(self, op):
        self.op = op


class _Op:
    __slots__ = ("stream", "fn", "inc", "deps", "needed", "cum", "idx", "grp")

    def __init__(self, stream, fn, inc, deps, idx, grp):
        self.stream = stream
        self.fn = fn
        self.inc = inc
        self.deps = deps
        self.needed = False
        self.cum = 0
        self.idx = idx
        self.grp = grp


class Buf:
    __slots__ = ("writer", "readers", "name")

    def __init__(self, name=""):
        self.writer = None
        self.readers = []
        self.name = name


SERIALIZE = False


class Tracker:

    def __init__(self):
        self.streams = {k: [] for k in ("pe", "dve", "act", "gp", "sync")}
        self.nops = 0
        self.last = {k: None for k in self.streams}
        self.groups = {}
        self.glast = {}

    def op(self, stream, fn, reads=(), writes=(), inc=1, grp=None):
        assert (inc == 16) == (grp is not None), f"grp required iff DMA ({grp})"
        deps = []
        if SERIALIZE:
            deps = [o for o in self.last.values() if o is not None]
        for b in reads:
            if b.writer is not None:
                deps.append(b.writer)
        for b in writes:
            if b.writer is not None:
                deps.append(b.writer)
            deps.extend(b.readers)
        deps = [d for d in deps
                if d.inc == 16 or not (stream == "pe" and d.stream == "pe")]
        o = _Op(stream, fn, inc, deps, self.nops, grp)
        self.nops += 1
        self.streams[stream].append(o)
        self.last[stream] = o
        if grp is not None:
            self.groups.setdefault(grp, 0)
            self.glast[grp] = o
        for b in reads:
            b.readers.append(o)
        for b in writes:
            b.writer = o
            b.readers = []
        return o

    def barrier(self):
        bufs = []
        for s, ops in self.streams.items():
            b = Buf(f"bar_{s}")
            b.writer = ops[-1] if ops else None
            bufs.append(b)
        for g, o in self.glast.items():
            b = Buf(f"bar_{g}")
            b.writer = o
            bufs.append(b)
        return bufs

    def finalize(self, nc, st, block):
        stream_sems = {s: st.enter_context(nc.semaphore(f"s_{s}"))
                       for s in self.streams}
        group_sems = {g: st.enter_context(nc.semaphore(f"sg_{g}"))
                      for g in self.groups}
        for ops in self.streams.values():
            for o in ops:
                for d in o.deps:
                    d.needed = True
        grp_stream = {}
        for s, ops in self.streams.items():
            c = 0
            for o in ops:
                if o.inc == 16:
                    self.groups[o.grp] += 16
                    o.cum = self.groups[o.grp]
                    grp_stream[o.grp] = s
                else:
                    if o.needed:
                        c += 1
                    o.cum = c

        def emit(stream):
            ops = self.streams[stream]
            eng_attr = {"pe": "tensor", "dve": "vector", "act": "scalar",
                        "gp": "gpsimd", "sync": "sync"}[stream]

            def body(eng):
                last_wait = {}
                for o in ops:
                    for d in o.deps:
                        if d.inc == 16:
                            sem, key, v = group_sems[d.grp], ("g", d.grp), d.cum
                        else:
                            if not d.needed:
                                continue
                            sem, key, v = (stream_sems[d.stream], ("s", d.stream),
                                           d.cum)
                        if v <= 0 or last_wait.get(key, -1) >= v:
                            continue
                        eng.wait_ge(sem, v)
                        last_wait[key] = v
                    ins = o.fn(eng)
                    if o.inc == 16:
                        ins.then_inc(group_sems[o.grp], 16)
                    elif o.needed:
                        ins.then_inc(stream_sems[o.stream], 1)
                for g, s2 in grp_stream.items():
                    if s2 == stream and self.groups[g] > 0:
                        eng.wait_ge(group_sems[g], self.groups[g])

            getattr(block, eng_attr)(body)
        for s in ("sync", "gp", "pe", "dve", "act"):
            emit(s)


def make_quant(tr, SAfn, prefix, bufmk, nsets=2, maxw=2816, share_u=None,
               cbuf=None, pipe3=False):
    nst = 3 if pipe3 else nsets
    nxn = 2 if pipe3 else nsets
    nm = 1 if pipe3 else nsets
    sts = [SAfn(f"{prefix}_st{s}", [128, 3 * (maxw // 16)], F32)
           for s in range(nst)]
    bst = [bufmk(f"{prefix}_st{s}") for s in range(nst)]
    xns = [SAfn(f"{prefix}_xn{s}", [128, maxw], BF16) for s in range(nxn)]
    bxn = [bufmk(f"{prefix}_xn{s}") for s in range(nxn)]
    us = [(share_u[0] if share_u else SAfn(f"{prefix}_u{s}", [128, maxw], F32))
          for s in range(nxn)]
    bu = [(share_u[1] if share_u else bufmk(f"{prefix}_u{s}"))
          for s in range(nxn)]
    qls = [SAfn(f"{prefix}_ql{s}", [128, maxw], BF16) for s in range(nxn)]
    bql = [bufmk(f"{prefix}_ql{s}") for s in range(nxn)]
    ms = [SAfn(f"{prefix}_m{s}", [128, maxw], U8) for s in range(nm)]
    bm = [bufmk(f"{prefix}_m{s}") for s in range(nm)]

    def quant_piece(xap, width, rbufs, wbufs, op, sidx, stage=0):
        st_t, st_b = sts[sidx % nst], bst[sidx % nst]
        xn_full, xn_b = xns[sidx % nxn], bxn[sidx % nxn]
        u_full, u_b = us[sidx % nxn], bu[sidx % nxn]
        ql_full, ql_b = qls[sidx % nxn], bql[sidx % nxn]
        m_full, m_b = ms[sidx % nm], bm[sidx % nm]
        nb = width // 16
        nbm = st_t.shape[1] // 3
        amax = st_t[:, 0:nb]
        scl = st_t[:, nbm:nbm + nb]
        inv = st_t[:, 2 * nbm:2 * nbm + nb]
        x3 = xap.rearrange("p (a b) -> p a b", b=16)
        xn = xn_full[:, 0:width]
        u = u_full[:, 0:width]
        ql = ql_full[:, 0:width]
        m = m_full[:, 0:width]
        creads = [cbuf] if cbuf is not None else []
        if stage in (0, "a"):
            op("dve", lambda e: e.tensor_reduce(
                amax, x3, AX.X, ALU.max, apply_absolute_value=True),
                reads=list(rbufs), writes=[st_b])
            op("dve", lambda e: e.tensor_scalar(
                scl, amax, 1.0 / 6.0, 1e-12, ALU.mult, ALU.max),
                reads=[st_b], writes=[st_b])
            op("dve", lambda e: e.reciprocal(inv, scl),
               reads=[st_b], writes=[st_b])
        if stage in (0, "b"):
            op("dve", lambda e: e.tensor_tensor(
                xn.rearrange("p (a b) -> p a b", b=16), x3,
                inv.unsqueeze(2).broadcast_to([128, nb, 16]), ALU.mult),
                reads=list(rbufs) + [st_b], writes=[xn_b])
            op("act", lambda e: e.activation(
                u, xn, AF.Identity, scale=2.0, bias=float(MAGIC)),
                reads=[xn_b] + creads, writes=[u_b])
            op("act", lambda e: e.activation(
                ql, u, AF.Identity, scale=0.5, bias=-float(MAGIC) * 0.5),
                reads=[u_b] + creads, writes=[ql_b])
            op("act", lambda e: e.activation(u, xn, AF.Abs),
               reads=[xn_b], writes=[u_b])
        if stage in (0, "c"):
            op("dve", lambda e: e.tensor_scalar(
                xn.bitcast(I16), xn.bitcast(I16), 32, None, ALU.add),
                reads=[xn_b], writes=[xn_b])
            op("dve", lambda e: e.tensor_scalar(
                xn.bitcast(I16), xn.bitcast(I16), -64, None, ALU.bitwise_and),
                reads=[xn_b], writes=[xn_b])
            op("dve", lambda e: e.tensor_scalar(
                m, u, 0.9, None, ALU.is_lt),
                reads=[u_b], writes=[m_b])
            op("dve", lambda e: e.copy_predicated(xn, m, ql),
                reads=[m_b, ql_b, xn_b], writes=[xn_b])
            op("dve", lambda e: e.tensor_tensor(
                x3, xn.rearrange("p (a b) -> p a b", b=16),
                scl.unsqueeze(2).broadcast_to([128, nb, 16]), ALU.mult),
                reads=[xn_b, st_b], writes=list(wbufs))
    return quant_piece


def _host_consts():
    ident = np.eye(128, dtype=NBF)
    icol4 = np.zeros((128, 512), dtype=NBF)
    for b in range(4):
        icol4[:, b * 128:(b + 1) * 128] = ident
    ones1 = np.ones((1, 128), dtype=NBF)
    ones16 = np.ones((16, 128), dtype=NBF)
    oh4 = np.zeros((16, 64), dtype=NBF)
    for c in range(4):
        for h in range(16):
            oh4[h, c * 16 + h] = 1
    return ident, icol4, ones1, ones16, oh4


def build_nc(debug=False):
    nc = bass.Bass("TRN2", target_bir_lowering=False)
    tr = Tracker()

    x_in = nc.dram_tensor("x", [T, D], F32, kind="ExternalInput").ap()
    wqT = nc.dram_tensor("wqT", [D, D], F32, kind="ExternalInput").ap()
    wkT = nc.dram_tensor("wkT", [D, D], F32, kind="ExternalInput").ap()
    wvT = nc.dram_tensor("wvT", [D, D], F32, kind="ExternalInput").ap()
    woT = nc.dram_tensor("woT", [D, D], F32, kind="ExternalInput").ap()
    wg_raw = nc.dram_tensor("wg_raw", [DI, D], F32, kind="ExternalInput").ap()
    wu_raw = nc.dram_tensor("wu_raw", [DI, D], F32, kind="ExternalInput").ap()
    wd_raw = nc.dram_tensor("wd_raw", [D, DI], F32, kind="ExternalInput").ap()
    gbc = {}
    for nm in ("wgam", "bgam", "wlam", "blam"):
        gbc[nm] = nc.dram_tensor(nm + "_in", [128, 64], F32, kind="ExternalInput").ap()
    c_ident = nc.dram_tensor("c_ident", [128, 128], BF16, kind="ExternalInput").ap()
    c_icol4 = nc.dram_tensor("c_icol4", [128, 512], BF16, kind="ExternalInput").ap()
    c_ones1 = nc.dram_tensor("c_ones1", [1, 128], BF16, kind="ExternalInput").ap()
    c_ones16 = nc.dram_tensor("c_ones16", [16, 128], BF16, kind="ExternalInput").ap()
    c_oh4 = nc.dram_tensor("c_oh4", [16, 64], BF16, kind="ExternalInput").ap()

    out_d = nc.dram_tensor("out", [T, D], F32, kind="ExternalOutput").ap()
    x1_kind = "ExternalOutput" if debug else "Internal"
    x1_d = nc.dram_tensor("x1_d", [T, D], BF16, kind=x1_kind).ap()
    wgq_d = nc.dram_tensor("wgq_d", [D, DI], BF16, kind=x1_kind).ap()
    wuq_d = nc.dram_tensor("wuq_d", [D, DI], BF16, kind="Internal").ap()
    wdq_d = nc.dram_tensor("wdq_d", [DI, D], BF16, kind=x1_kind).ap()
    dbg = {}
    if debug:
        dbg["q_dbg"] = nc.dram_tensor("q_dbg", [128, 8 * 512], BF16, kind="ExternalOutput").ap()
        dbg["k_dbg"] = nc.dram_tensor("k_dbg", [TILE_A, D], BF16, kind="ExternalOutput").ap()
        dbg["y_dbg"] = nc.dram_tensor("y_dbg", [TILE_A, D], BF16, kind="ExternalOutput").ap()
        dbg["rho_dbg"] = nc.dram_tensor("rho_dbg", [16, TILE_A], BF16, kind="ExternalOutput").ap()
        dbg["xt_dbg"] = nc.dram_tensor("xt_dbg", [128, 4 * 1024], BF16, kind="ExternalOutput").ap()
        dbg["xn1F_dbg"] = nc.dram_tensor("xn1F_dbg", [128, 4096], BF16, kind="ExternalOutput").ap()
        dbg["wq_dbg"] = nc.dram_tensor("wq_dbg", [128, 8 * 1024], BF16, kind="ExternalOutput").ap()
        dbg["xn1_dbg"] = nc.dram_tensor("xn1_dbg", [128, 4 * 1024], BF16, kind="ExternalOutput").ap()

    x_r = x_in.rearrange("(a p) f -> p a f", p=128)
    x1_r = x1_d.rearrange("(a p) f -> p a f", p=128)
    out_r = out_d.rearrange("(a p) f -> p a f", p=128)
    wgq_r = wgq_d.rearrange("(dc p) n -> p dc n", p=128)
    wuq_r = wuq_d.rearrange("(dc p) n -> p dc n", p=128)
    wdq_r = wdq_d.rearrange("(nc p) e -> p nc e", p=128)

    from contextlib import ExitStack
    with ExitStack() as _st:
        def SB(name, shape, dtype):
            return _st.enter_context(nc.sbuf_tensor(name, shape, dtype))

        def PS(name, shape, dtype):
            return _st.enter_context(nc.psum_tensor(name, shape, dtype))

        ident = SB("ident", [128, 128], BF16)
        icol4 = SB("icol4", [128, 512], BF16)
        ones1 = SB("ones1", [1, 128], BF16)
        ones16 = SB("ones16", [16, 128], BF16)
        oh4 = SB("oh4", [16, 64], BF16)
        cmagic = SB("cmagic", [128, 1], F32)
        cmagic2 = SB("cmagic2", [128, 1], F32)
        wgam = SB("wgam", [128, 64], F32)
        bgam = SB("bgam", [128, 64], F32)
        wlam = SB("wlam", [128, 64], F32)
        blam = SB("blam", [128, 64], F32)
        ln2st = SB("ln2st", [128, 64], F32)
        pA = PS("pA", [128, 512], F32)
        pB = PS("pB", [128, 512], F32)
        pR1 = PS("pR1", [128, 512], F32)
        pR2 = PS("pR2", [128, 512], F32)
        pPT = PS("pPT", [128, 512], F32)
        pY = PS("pY", [128, 512], F32)
        pM = PS("pM", [128, 512], F32)
        pT = PS("pT", [128, 1024], BF16)
        block = _st.enter_context(nc.Block())
        bx1d = Buf("x1_d")
        bln2 = Buf("ln2st")
        nc.const_aps.register(F32, float(MAGIC), cmagic[:])
        nc.const_aps.register(F32, -float(MAGIC) * 0.5, cmagic2[:])
        bwgq6 = [Buf(f"wgq_d{j}") for j in range(6)]
        bwuq6 = [Buf(f"wuq_d{j}") for j in range(6)]
        bwdq = Buf("wdq_d")
        b_const = Buf("consts")
        BC = {nm: b_const for nm in ("ident", "icol4", "ones1", "ones16", "oh4",
                                     "wgam", "bgam", "wlam", "blam")}
        pbuf = {id(p): Buf(nm) for nm, p in
                [("pA", pA), ("pB", pB), ("pR1", pR1), ("pR2", pR2),
                 ("pPT", pPT), ("pY", pY), ("pM", pM), ("pT", pT)]}

        def P(p):
            return pbuf[id(p)]

        for src, dst, nm in [(c_ident, ident, "ident"), (c_icol4, icol4, "icol4"),
                             (c_ones1, ones1, "ones1"), (c_ones16, ones16, "ones16"),
                             (c_oh4, oh4, "oh4"), (gbc["wgam"], wgam, "wgam"),
                             (gbc["bgam"], bgam, "bgam"), (gbc["wlam"], wlam, "wlam"),
                             (gbc["blam"], blam, "blam")]:
            tr.op("sync", (lambda s_, d_: lambda e: e.dma_start(out=d_[:], in_=s_))(src, dst),
                  writes=[BC[nm]], inc=16, grp="g_const")
        tr.op("dve", lambda e: e.memset(cmagic[:], float(MAGIC)), writes=[BC["wgam"]])
        tr.op("dve", lambda e: e.memset(cmagic2[:], -float(MAGIC) * 0.5),
              writes=[BC["wgam"]])

        with ExitStack() as _stA:
            def SA(name, shape, dtype, **kw):
                return _stA.enter_context(nc.sbuf_tensor(name, shape, dtype, **kw))

            def SA2(name, shape, dtype, **kw):
                return [SA(name + "_0", shape, dtype, **kw),
                        SA(name + "_1", shape, dtype, **kw)]

            wq_sb = SA("wq_sb", [128, 8 * 1024], BF16)
            wk_sb = SA("wk_sb", [128, 8 * 1024], BF16)
            wv_sb = SA("wv_sb", [128, 8 * 1024], BF16)
            wo_sb = SA("wo_sb", [128, 8 * 1024], BF16)
            S_sb = SA("S_sb", [128, 512], BF16)
            xt2 = SA2("xt", [128, 4 * 1024], BF16)
            xn12 = [SA("xn1", [128, 4 * 1024], BF16)] * 2
            xn1F2 = SA2("xn1F", [128, 8 * 512], BF16, is_transpose_dest=True)
            qF2 = [SA("qF", [128, 8 * 512], BF16)] * 2
            vF2 = [SA("vF", [128, 8 * 512], BF16)] * 2
            ktok2 = [SA("ktok", [128, 4 * 1024], BF16)] * 2
            vtok2 = [SA("vtok", [128, 4 * 1024], BF16, is_transpose_dest=True)] * 2
            run42 = [SA("run4", [128, 64], F32)] * 2
            din_T2 = [SA("din_T", [128, 64], F32)] * 2
            dcol2 = [SA("dcol", [128, 64], F32)] * 2
            ya = SA("ya", [128, 4 * 1024], BF16)
            yF = SA("yF", [128, 8 * 512], BF16, is_transpose_dest=True)
            scr = SA("scr", [128, 1024], F32)
            lnst = SA("lnst", [128, 40], F32)
            kst = SA("kst", [128, 6 * 64], F32)
            rho4 = SA("rho4", [128, 64], BF16)
            rho_ht = SA("rho_ht", [16, 512], BF16)
            rho0_ht = SA("rho0_ht", [16, 512], BF16)
            rhoS = SA("rhoS", [16, 512], BF16)
            din_bf = SA("din_bf", [16, 512], BF16)
            rstg = SA("rstg", [16, 64], BF16)
            rho0_sep = [SA("rho0_sep0", [1, 2048], BF16),
                        SA("rho0_sep1", [1, 2048], BF16)]
            E0 = SA("E0", [128, 512], BF16)
            E1 = SA("E1", [128, 512], BF16)
            mtp0 = SA("mtp0", [128, 512], BF16)
            mtp1 = SA("mtp1", [128, 512], BF16)
            ku0 = SA("ku0", [128, 256], BF16)
            ku1 = SA("ku1", [128, 256], BF16)
            vw0 = SA("vw0", [128, 256], BF16)
            vw1 = SA("vw1", [128, 256], BF16)
            vws = SA("vws", [128, 16], F32)
            y2s0 = SA("y2s0", [128, 256], F32)
            y2s1 = SA("y2s1", [128, 256], F32)
            wtile2 = [SA("wtile0", [128, 1408], BF16),
                      SA("wtile1", [128, 1408], BF16)]
            wttr = SA("wttr", [128, 11 * 128], BF16, is_transpose_dest=True)

            singles = ("S", "ya", "yF", "scr", "lnst", "kst", "rho4", "w4",
                       "rho_ht", "rho0_ht", "rhoS", "din_bf",
                       "rstg", "rho0_sep0", "rho0_sep1", "E0", "E1",
                       "mtp0", "mtp1",
                       "ku0", "ku1", "vw0", "vw1", "vws", "y2s0", "y2s1",
                       "wtile0", "wtile1", "wttr")
            bw = {nm: Buf(nm) for nm in singles}
            doubles = ("xt", "xn1", "xn1F", "qF", "vF", "ktok", "vtok",
                       "run4", "din_T", "dcol")
            real_doubles = ("xt", "xn1F")
            bw2 = {nm: ([Buf(nm + "_0"), Buf(nm + "_1")] if nm in real_doubles
                        else [Buf(nm)] * 2) for nm in doubles}

            for src_, dst in [(wqT, wq_sb), (wkT, wk_sb), (wvT, wv_sb), (woT, wo_sb)]:
                tr.op("gp", (lambda s_, d_: lambda e: e.dma_start(
                    out=d_[:].rearrange("p (dc e) -> p dc e", dc=8),
                    in_=s_.rearrange("(dc p) e -> p dc e", p=128)))(src_, dst),
                    writes=[bw["w4"]], inc=16, grp="g_attnw")
            tr.op("dve", lambda e: e.memset(S_sb[:], 0.0), writes=[bw["S"]])
            tr.op("dve", lambda e: e.memset(rhoS[:], 0.0), writes=[bw["rhoS"]])

            ya3 = ya[:].rearrange("p (a f) -> p a f", a=4)
            yFb = yF[:].rearrange("p (cc dc f) -> p cc dc f", cc=4, dc=8)
            wsb3 = {nm: w[:].rearrange("p (dc e) -> p dc e", dc=8)
                    for nm, w in [("wq", wq_sb), ("wk", wk_sb),
                                  ("wv", wv_sb), ("wo", wo_sb)]}

            HGRP = [[0, 2, 4, 6], [8, 10, 12, 14],
                    [1, 3, 5, 7], [9, 11, 13, 15]]
            GBASE = [0, 0, 64, 64]
            GOFF = [0, 8, 1, 9]

            def Ssl(h):
                return S_sb[(h % 2) * 64:(h % 2) * 64 + 64,
                            (h // 2) * 64:(h // 2) * 64 + 64]

            def Sgq(g):
                return S_sb[GBASE[g]:GBASE[g] + 64,
                            (g % 2) * 256:(g % 2) * 256 + 256]

            def hsel4(ap16, g):
                st_ = GOFF[g] % 2
                lo = (GOFF[g] % 8) // 2 + (GOFF[g] // 8) * 4
                lo = 4 if GOFF[g] >= 8 else 0
                return ap16[:, st_::2, :][:, lo:lo + 4, :]

            def hslice(buf3, h, cols):
                return buf3[(h % 2) * 64:(h % 2) * 64 + 64, h // 2, cols]

            class TV:

                def __init__(self, i):
                    b = i % 2
                    self.i = i
                    self.xt3 = xt2[b][:].rearrange("p (a f) -> p a f", a=4)
                    self.xn13 = xn12[b][:].rearrange("p (a f) -> p a f", a=4)
                    self.xn1 = xn12[b]
                    self.xn1F = xn1F2[b]
                    self.xn1Fb = xn1F2[b][:].rearrange(
                        "p (tt dc f) -> p tt dc f", tt=4, dc=8)
                    self.qF = qF2[b]
                    self.vF = vF2[b]
                    self.qF3 = qF2[b][:].rearrange("p (a f) -> p a f", a=8)
                    self.vF3 = vF2[b][:].rearrange("p (a f) -> p a f", a=8)
                    self.ktok = ktok2[b]
                    self.ktok3 = ktok2[b][:].rearrange("p (a f) -> p a f", a=4)
                    self.vtok = vtok2[b]
                    self.vtokb = vtok2[b][:].rearrange(
                        "p (ec tt f) -> p ec tt f", ec=8, tt=4)
                    self.run4 = run42[b]
                    self.din_T = din_T2[b]
                    self.dcol = dcol2[b]
                    for nm in doubles:
                        setattr(self, "b_" + nm, bw2[nm][b])

            def prep_a(i):
                v = TV(i)
                tr.op("gp", (lambda t_, v_: lambda e: e.dma_start(
                    out=v_.xt3, in_=x_r[:, t_ * 4:(t_ + 1) * 4, :]))(i, v),
                    writes=[v.b_xt], inc=16, grp="g_xt%d" % (i % 2))
                for tt in range(4):
                    tr.op("act", (lambda tt_, v_: lambda e: e.activation(
                        scr[:], v_.xt3[:, tt_, :], AF.Square,
                        accum_out=lnst[:, tt_:tt_ + 1]))(tt, v),
                        reads=[v.b_xt], writes=[bw["scr"], bw["lnst"]])
                    tr.op("dve", (lambda tt_, v_: lambda e: e.reduce_sum(
                        lnst[:, 4 + tt_:5 + tt_], v_.xt3[:, tt_, :], AX.X))(tt, v),
                        reads=[v.b_xt], writes=[bw["lnst"]])
                tr.op("dve", lambda e: e.tensor_scalar(
                    lnst[:, 8:12], lnst[:, 4:8], 1.0 / D, None, ALU.mult),
                    reads=[bw["lnst"]], writes=[bw["lnst"]])
                tr.op("dve", lambda e: e.tensor_scalar(
                    lnst[:, 12:16], lnst[:, 0:4], 1.0 / D, EPS_LN, ALU.mult, ALU.add),
                    reads=[bw["lnst"]], writes=[bw["lnst"]])
                tr.op("dve", lambda e: e.tensor_tensor(
                    lnst[:, 16:20], lnst[:, 8:12], lnst[:, 8:12], ALU.mult),
                    reads=[bw["lnst"]], writes=[bw["lnst"]])
                tr.op("dve", lambda e: e.tensor_tensor(
                    lnst[:, 20:24], lnst[:, 12:16], lnst[:, 16:20], ALU.subtract),
                    reads=[bw["lnst"]], writes=[bw["lnst"]])
                tr.op("act", lambda e: e.activation(
                    lnst[:, 24:28], lnst[:, 20:24], AF.Ln),
                    reads=[bw["lnst"]], writes=[bw["lnst"]])
                tr.op("act", lambda e: e.activation(
                    lnst[:, 28:32], lnst[:, 24:28], AF.Exp, scale=-0.5),
                    reads=[bw["lnst"]], writes=[bw["lnst"]])
                tr.op("dve", lambda e: e.tensor_tensor(
                    lnst[:, 32:36], lnst[:, 8:12], lnst[:, 28:32], ALU.mult),
                    reads=[bw["lnst"]], writes=[bw["lnst"]])
                tr.op("dve", lambda e: e.tensor_scalar(
                    lnst[:, 36:40], lnst[:, 32:36], -1.0, None, ALU.mult),
                    reads=[bw["lnst"]], writes=[bw["lnst"]])
                for tt in range(4):
                    tr.op("dve", (lambda tt_, v_: lambda e: e.tensor_scalar(
                        v_.xn13[:, tt_, :], v_.xt3[:, tt_, :],
                        lnst[:, 28 + tt_:29 + tt_], lnst[:, 36 + tt_:37 + tt_],
                        ALU.mult, ALU.add))(tt, v),
                        reads=[v.b_xt, bw["lnst"]], writes=[v.b_xn1])
                tr.op("sync", (lambda v_: lambda e: e.dma_start(
                    out=v_.xn1F[:].rearrange("p (a f) -> p a f", a=32),
                    in_=v_.xn1[:].rearrange("p (a f) -> p a f", a=32),
                    transpose=True))(v),
                    reads=[v.b_xn1], writes=[v.b_xn1F], inc=16,
                    grp="g_xn1F%d" % (i % 2))

            def prep_b(i):
                v = TV(i)
                for dst3, dstb, wnm in [(v.qF3, v.b_qF, "wq"), (v.vF3, v.b_vF, "wv")]:
                    for ec in range(8):
                        ps = pA if ec % 2 == 0 else pB

                        def mk(ec_, ps_, wnm_, v_):
                            def f(e):
                                ins = None
                                for dc in range(8):
                                    ins = e.matmul(
                                        ps_[:],
                                        wsb3[wnm_][:, dc, ec_ * 128:(ec_ + 1) * 128],
                                        v_.xn1Fb[:, :, dc, :],
                                        start=(dc == 0), stop=(dc == 7))
                                return ins
                            return f
                        tr.op("pe", mk(ec, ps, wnm, v),
                              reads=[bw["w4"], v.b_xn1F], writes=[P(ps)])
                        tr.op("act", (lambda ec_, ps_, d3_: lambda e: e.copy(
                            d3_[:, ec_, :], ps_[:]))(ec, ps, dst3),
                            reads=[P(ps)], writes=[dstb])
                for tt in range(4):
                    for eh in range(2):
                        ps = pA if eh == 0 else pB

                        def mkk(tt_, eh_, ps_, v_):
                            def f(e):
                                ins = None
                                for dc in range(8):
                                    ins = e.matmul(
                                        ps_[:], v_.xn1Fb[:, tt_, dc, :],
                                        wsb3["wk"][:, dc, eh_ * 512:(eh_ + 1) * 512],
                                        start=(dc == 0), stop=(dc == 7))
                                return ins
                            return f
                        tr.op("pe", mkk(tt, eh, ps, v),
                              reads=[bw["w4"], v.b_xn1F], writes=[P(ps)])
                        tr.op("act", (lambda tt_, eh_, ps_, v_: lambda e: e.copy(
                            v_.ktok3[:, tt_, eh_ * 512:(eh_ + 1) * 512], ps_[:]))(
                                tt, eh, ps, v),
                            reads=[P(ps)], writes=[v.b_ktok])
                tr.op("sync", (lambda v_: lambda e: e.dma_start(
                    out=v_.vtok[:].rearrange("p (a f) -> p a f", a=32),
                    in_=v_.vF[:].rearrange("p (a f) -> p a f", a=32),
                    transpose=True))(v),
                    reads=[v.b_vF], writes=[v.b_vtok], inc=16,
                    grp="g_vtok%d" % (i % 2))
                for tt in range(4):
                    tr.op("act", (lambda tt_, v_: lambda e: e.activation(
                        scr[:], v_.ktok3[:, tt_, :], AF.Square))(tt, v),
                        reads=[v.b_ktok], writes=[bw["scr"]])
                    tr.op("dve", (lambda tt_: lambda e: e.tensor_reduce(
                        kst[:, tt_ * 16:(tt_ + 1) * 16],
                        scr[:].rearrange("p (a b) -> p a b", b=64),
                        AX.X, ALU.add))(tt),
                        reads=[bw["scr"]], writes=[bw["kst"]])
                    tr.op("dve", (lambda tt_, v_: lambda e: e.tensor_reduce(
                        kst[:, 64 + tt_ * 16:64 + (tt_ + 1) * 16],
                        v_.ktok3[:, tt_, :].rearrange("p (a b) -> p a b", b=64),
                        AX.X, ALU.add))(tt, v),
                        reads=[v.b_ktok], writes=[bw["kst"]])
                tr.op("dve", lambda e: e.tensor_scalar(
                    kst[:, 128:192], kst[:, 0:64], 1e-24, None, ALU.max),
                    reads=[bw["kst"]], writes=[bw["kst"]])
                tr.op("act", lambda e: e.activation(
                    kst[:, 128:192], kst[:, 128:192], AF.Ln),
                    reads=[bw["kst"]], writes=[bw["kst"]])
                tr.op("act", lambda e: e.activation(
                    kst[:, 128:192], kst[:, 128:192], AF.Exp, scale=-0.5),
                    reads=[bw["kst"]], writes=[bw["kst"]])
                tr.op("dve", lambda e: e.tensor_tensor(
                    kst[:, 192:256], kst[:, 64:128], kst[:, 128:192], ALU.mult),
                    reads=[bw["kst"]], writes=[bw["kst"]])
                tr.op("dve", lambda e: e.scalar_tensor_tensor(
                    kst[:, 256:320], kst[:, 192:256], 1.0 / DK, wgam[:],
                    ALU.mult, ALU.mult),
                    reads=[bw["kst"], BC["wgam"]], writes=[bw["kst"]])
                tr.op("dve", lambda e: e.tensor_tensor(
                    kst[:, 256:320], kst[:, 256:320], bgam[:], ALU.add),
                    reads=[bw["kst"], BC["bgam"]], writes=[bw["kst"]])
                tr.op("act", lambda e: e.activation(
                    kst[:, 320:384], kst[:, 256:320], AF.Sigmoid),
                    reads=[bw["kst"]], writes=[bw["kst"]])
                tr.op("dve", lambda e: e.scalar_tensor_tensor(
                    kst[:, 256:320], kst[:, 192:256], 1.0 / DK, wlam[:],
                    ALU.mult, ALU.mult),
                    reads=[bw["kst"], BC["wlam"]], writes=[bw["kst"]])
                tr.op("dve", lambda e: e.tensor_tensor(
                    kst[:, 256:320], kst[:, 256:320], blam[:], ALU.add),
                    reads=[bw["kst"], BC["blam"]], writes=[bw["kst"]])
                tr.op("act", lambda e: e.activation(
                    kst[:, 256:320], kst[:, 256:320], AF.Sigmoid),
                    reads=[bw["kst"]], writes=[bw["kst"]])
                tr.op("dve", lambda e: e.tensor_scalar(
                    kst[:, 256:320], kst[:, 256:320], -1.0, 1.0, ALU.mult, ALU.add),
                    reads=[bw["kst"]], writes=[bw["kst"]])
                tr.op("dve", lambda e: e.tensor_tensor(
                    rho4[:], kst[:, 256:320], kst[:, 320:384], ALU.mult),
                    reads=[bw["kst"]], writes=[bw["rho4"]])
                tr.op("dve", (lambda v_: lambda e: e.tensor_scalar(
                    v_.run4[:], rho4[:], -1.0, 1.0, ALU.mult, ALU.add))(v),
                    reads=[bw["rho4"]], writes=[v.b_run4])
                tr.op("dve", (lambda v_: lambda e: e.tensor_tensor(
                    v_.run4[:], v_.run4[:], kst[:, 128:192], ALU.mult))(v),
                    reads=[v.b_run4, bw["kst"]], writes=[v.b_run4])
                for tt in range(4):
                    tr.op("pe", (lambda tt_: lambda e: e.transpose(
                        pT[0:16, 0:128], rho4[:, tt_ * 16:(tt_ + 1) * 16], ident[:]))(tt),
                        reads=[bw["rho4"], BC["ident"]], writes=[P(pT)])
                    tr.op("act", (lambda tt_: lambda e: e.copy(
                        rho_ht[:, tt_ * 128:(tt_ + 1) * 128], pT[0:16, 0:128]))(tt),
                        reads=[P(pT)], writes=[bw["rho_ht"]])
                tr.op("dve", lambda e: e.tensor_copy(rho0_ht[:], rho_ht[:]),
                      reads=[bw["rho_ht"]], writes=[bw["rho0_ht"]])
                tr.op("dve", lambda e: e.memset(
                    rho0_ht[:].rearrange("p (a f) -> p a f", a=4)[:, :, 0:1], 0.0),
                    writes=[bw["rho0_ht"]])
                tr.op("dve", lambda e: e.tensor_copy(
                    rhoS[:].rearrange("p (a f) -> p a f", a=4)[:, :, 0:1],
                    rho_ht[:].rearrange("p (a f) -> p a f", a=4)[:, :, 0:1]),
                    reads=[bw["rho_ht"]], writes=[bw["rhoS"]])
                tr.op("dve", lambda e: e.tensor_tensor_scan(
                    din_bf[:], rho0_ht[:], rhoS[:], 0.0, ALU.mult, ALU.add),
                    reads=[bw["rho0_ht"], bw["rhoS"]], writes=[bw["din_bf"]])
                for tt in range(4):
                    tr.op("pe", (lambda tt_: lambda e: e.transpose(
                        pT[0:128, 128:144], din_bf[:, tt_ * 128:(tt_ + 1) * 128],
                        ident[0:16, 0:16]))(tt),
                        reads=[bw["din_bf"], BC["ident"]], writes=[P(pT)])
                    tr.op("act", (lambda tt_, v_: lambda e: e.copy(
                        v_.din_T[:, tt_ * 16:(tt_ + 1) * 16], pT[0:128, 128:144]))(
                            tt, v),
                        reads=[P(pT)], writes=[v.b_din_T])
                tr.op("dve", lambda e: e.tensor_tensor(
                    rstg[:].rearrange("p (c j) -> p c j", c=4),
                    oh4[:].rearrange("p (c j) -> p c j", c=4),
                    din_bf[:, 127::128].unsqueeze(2).broadcast_to([16, 4, 16]),
                    ALU.mult),
                    reads=[bw["din_bf"], BC["oh4"]], writes=[bw["rstg"]])
                tr.op("pe", lambda e: e.matmul(
                    pM[0:128, 256:320], ones16[:], rstg[:], start=True, stop=True),
                    reads=[BC["ones16"], bw["rstg"]], writes=[P(pM)])
                tr.op("dve", (lambda v_: lambda e: e.tensor_copy(
                    v_.dcol[:], pM[0:128, 256:320]))(v),
                    reads=[P(pM)], writes=[v.b_dcol])
                if debug and i == 0:
                    tr.op("sync", lambda e: e.dma_start(
                        out=dbg["rho_dbg"], in_=rho_ht[:]),
                        reads=[bw["rho_ht"]], inc=16, grp="g_dbg0")
                    tr.op("sync", (lambda v_: lambda e: e.dma_start(
                        out=dbg["xt_dbg"], in_=v_.xt3[:, :, :].rearrange(
                            "p a f -> p (a f)") if False else xt2[0][:]))(v),
                        reads=[v.b_xt], inc=16, grp="g_dbg1")
                    tr.op("sync", lambda e: e.dma_start(
                        out=dbg["xn1_dbg"], in_=xn12[0][:]),
                        reads=[v.b_xn1], inc=16, grp="g_dbg2")
                    tr.op("sync", lambda e: e.dma_start(
                        out=dbg["xn1F_dbg"], in_=xn1F2[0][:]),
                        reads=[v.b_xn1F], inc=16, grp="g_dbg6")
                    tr.op("sync", lambda e: e.dma_start(
                        out=dbg["wq_dbg"], in_=wq_sb[:]),
                        reads=[bw["w4"]], inc=16, grp="g_dbg7")
                    tr.op("sync", lambda e: e.dma_start(
                        out=dbg["q_dbg"], in_=qF2[0][:]),
                        reads=[v.b_qF], inc=16, grp="g_dbg3")
                    tr.op("sync", (lambda v_: lambda e: e.dma_start(
                        out=dbg["k_dbg"].rearrange("(a p) f -> p a f", p=128),
                        in_=v_.ktok3))(v),
                        reads=[v.b_ktok], inc=16, grp="g_dbg4")

            def chunks(i, mid=None):
                v = TV(i)
                def emit_bc_scan(cc, g):
                    sep = rho0_sep[cc % 2]
                    sepb = bw["rho0_sep%d" % (cc % 2)]
                    if g == 0:
                        tr.op("sync", (lambda cc_, sep_: lambda e: e.dma_start(
                            out=sep_[0:1, :].rearrange("q (h t) -> q h t", h=16),
                            in_=rho0_ht[:, cc_ * 128:(cc_ + 1) * 128]))(cc, sep),
                            reads=[bw["rho0_ht"]], writes=[sepb], inc=16,
                            grp="g_rsep%d" % (cc % 2))
                    pRho = pR1 if g % 2 == 0 else pR2
                    E_sb = E0 if g % 2 == 0 else E1
                    Ebuf = bw["E0"] if g % 2 == 0 else bw["E1"]

                    def mk_bc(g_, pRho_, sep_):
                        def f(e):
                            rhs = hsel4(sep_[0:1, :].rearrange(
                                "q (hh t) -> q hh t", hh=16), g_)
                            return e.matmul(pRho_[:], ones1[:], rhs,
                                            start=True, stop=True)
                        return f
                    tr.op("pe", mk_bc(g, pRho, sep),
                          reads=[BC["ones1"], sepb], writes=[P(pRho)])
                    tr.op("dve", (lambda E_, pRho_: lambda e: e.tensor_tensor_scan(
                        E_[:], pRho_[:], icol4[:], 0.0, ALU.mult, ALU.add))(
                            E_sb, pRho),
                        reads=[P(pRho), BC["icol4"]], writes=[Ebuf])

                emit_bc_scan(0, 0)
                for cc in range(4):
                    if cc == 2 and mid is not None:
                        mid()
                    for g in range(4):
                        if g < 3:
                            emit_bc_scan(cc, g + 1)
                        elif cc < 3:
                            emit_bc_scan(cc + 1, 0)
                        heads = HGRP[g]
                        E_sb = E0 if g % 2 == 0 else E1
                        Ebuf = bw["E0"] if g % 2 == 0 else bw["E1"]
                        mtp = mtp0 if g % 2 == 0 else mtp1
                        Mbuf = bw["mtp0"] if g % 2 == 0 else bw["mtp1"]
                        ku = ku0 if g % 2 == 0 else ku1
                        Kbuf = bw["ku0"] if g % 2 == 0 else bw["ku1"]
                        vw = vw0 if g % 2 == 0 else vw1
                        Vbuf = bw["vw0"] if g % 2 == 0 else bw["vw1"]
                        y2s = y2s0 if g % 2 == 0 else y2s1
                        Ybuf = bw["y2s0"] if g % 2 == 0 else bw["y2s1"]

                        def mk_pt(cc_, heads_, v_):
                            def f(e):
                                ins = None
                                for j, h in enumerate(heads_):
                                    cols = slice(cc_ * 128, (cc_ + 1) * 128)
                                    ins = e.matmul(
                                        pPT[:, j * 128:(j + 1) * 128],
                                        hslice(v_.vF3, h, cols),
                                        hslice(v_.qF3, h, cols),
                                        start=True, stop=True)
                                return ins
                            return f
                        tr.op("pe", mk_pt(cc, heads, v),
                              reads=[v.b_vF, v.b_qF], writes=[P(pPT)])
                        tr.op("dve", (lambda mtp_, E_: lambda e: e.tensor_tensor(
                            mtp_[:], E_[:], pPT[:], ALU.mult))(mtp, E_sb),
                            reads=[Ebuf, P(pPT)], writes=[Mbuf])
                        kt4 = hsel4(v.ktok3[:, cc, :].rearrange(
                            "p (hh w) -> p hh w", hh=16), g)
                        run_q = hsel4(v.run4[:].rearrange(
                            "p (c hh) -> p c hh", c=4)[:, cc, :].unsqueeze(2), g)
                        tr.op("dve", (lambda ku_, kt4_, rq_: lambda e: e.tensor_tensor(
                            ku_[:].rearrange("p (j w) -> p j w", j=4), kt4_,
                            rq_.broadcast_to([128, 4, 64]), ALU.mult))(ku, kt4, run_q),
                            reads=[v.b_ktok, v.b_run4], writes=[Kbuf])

                        def mk_vws(E_, rq_):
                            def f(e):
                                el = E_[:].rearrange(
                                    "p (j t) -> p j t", j=4)[:, :, 127:128]
                                return e.tensor_tensor(
                                    vws[:, 0:4].unsqueeze(2), el, rq_, ALU.mult)
                            return f
                        tr.op("dve", mk_vws(E_sb, run_q),
                              reads=[Ebuf, v.b_run4], writes=[bw["vws"]])

                        def mk_vw(cc_, g_, vw_, v_):
                            def f(e):
                                ec0 = (GOFF[g_] % 8) // 2 + (GOFF[g_] // 8) * 4
                                woff = (GOFF[g_] % 2) * 64
                                vt4 = v_.vtokb[:, ec0:ec0 + 4, cc_, woff:woff + 64]
                                return e.tensor_tensor(
                                    vw_[:].rearrange("p (j w) -> p j w", j=4), vt4,
                                    vws[:, 0:4].unsqueeze(2).broadcast_to(
                                        [128, 4, 64]), ALU.mult)
                            return f
                        tr.op("dve", mk_vw(cc, g, vw, v),
                              reads=[v.b_vtok, bw["vws"]], writes=[Vbuf])

                        def mk_y(cc_, g_, heads_, mtp_, ku_, v_):
                            def f(e):
                                ins = None
                                for j, h in enumerate(heads_):
                                    cols = slice(cc_ * 128, (cc_ + 1) * 128)
                                    e.matmul(pY[:, j * 64:(j + 1) * 64],
                                             mtp_[:, j * 128:(j + 1) * 128],
                                             ku_[:, j * 64:(j + 1) * 64],
                                             start=True, stop=True)
                                    ins = e.matmul(
                                        pY[:, 256 + j * 64:256 + (j + 1) * 64],
                                        hslice(v_.qF3, h, cols), Ssl(h),
                                        start=True, stop=True)
                                return ins
                            return f
                        tr.op("pe", mk_y(cc, g, heads, mtp, ku, v),
                              reads=[Mbuf, Kbuf, v.b_qF, bw["S"]], writes=[P(pY)])
                        def mk_y2s(cc_, g_, y2s_, v_):
                            def f(e):
                                dq = hsel4(v_.din_T[:].rearrange(
                                    "p (c hh) -> p c hh", c=4)[:, cc_, :]
                                    .unsqueeze(2), g_)
                                return e.tensor_tensor(
                                    y2s_[:].rearrange("p (j w) -> p j w", j=4),
                                    pY[:, 256:512].rearrange(
                                        "p (j w) -> p j w", j=4),
                                    dq.broadcast_to([128, 4, 64]), ALU.mult)
                            return f
                        tr.op("dve", mk_y2s(cc, g, y2s, v),
                              reads=[P(pY), v.b_din_T], writes=[Ybuf])

                        def mk_ya(cc_, g_, y2s_):
                            def f(e):
                                out = hsel4(ya3[:, cc_, :].rearrange(
                                    "p (hh w) -> p hh w", hh=16), g_)
                                return e.tensor_tensor(
                                    out, pY[0:128, 0:256].rearrange(
                                        "p (j w) -> p j w", j=4),
                                    y2s_[:].rearrange("p (j w) -> p j w", j=4),
                                    ALU.add)
                            return f
                        tr.op("dve", mk_ya(cc, g, y2s),
                              reads=[P(pY), Ybuf], writes=[bw["ya"]])

                        def mk_dl(cc_, g_, heads_, vw_, v_):
                            def f(e):
                                ins = None
                                rb_ = GBASE[g_]
                                tp = (0, 64) if rb_ else None
                                for j, h in enumerate(heads_):
                                    ins = e.matmul(
                                        pM[rb_:rb_ + 64, j * 64:(j + 1) * 64],
                                        vw_[:, j * 64:(j + 1) * 64],
                                        v_.ktok3[:, cc_, h * 64:(h + 1) * 64],
                                        start=True, stop=True, tile_position=tp)
                                return ins
                            return f
                        tr.op("pe", mk_dl(cc, g, heads, vw, v),
                              reads=[Vbuf, v.b_ktok], writes=[P(pM)])

                        def mk_st1(cc_, g_, v_):
                            def f(e):
                                rb_ = GBASE[g_]
                                sg = Sgq(g_)
                                dq = hsel4(v_.dcol[rb_:rb_ + 64, :].rearrange(
                                    "p (c hh) -> p c hh", c=4)[:, cc_, :].unsqueeze(2),
                                    g_)
                                return e.tensor_tensor(
                                    sg.rearrange("p (j w) -> p j w", j=4),
                                    sg.rearrange("p (j w) -> p j w", j=4),
                                    dq.broadcast_to([64, 4, 64]), ALU.mult)
                            return f

                        def mk_st2(g_):
                            def f(e):
                                rb_ = GBASE[g_]
                                return e.tensor_tensor(
                                    Sgq(g_), Sgq(g_), pM[rb_:rb_ + 64, 0:256],
                                    ALU.add)
                            return f
                        tr.op("dve", mk_st1(cc, g, v),
                              reads=[v.b_dcol, bw["S"]], writes=[bw["S"]])
                        tr.op("dve", mk_st2(g),
                              reads=[P(pM), bw["S"]], writes=[bw["S"]])
                        _emit_wq(1)
                        _emit_ln2(1)
                if debug and i == 0:
                    tr.op("sync", lambda e: e.dma_start(
                        out=dbg["y_dbg"].rearrange("(a p) f -> p a f", p=128),
                        in_=ya3),
                        reads=[bw["ya"]], inc=16, grp="g_dbg5")

            def finish(i):
                v = TV(i)
                tr.op("sync", lambda e: e.dma_start(
                    out=yF[:].rearrange("p (a f) -> p a f", a=32),
                    in_=ya[:].rearrange("p (a f) -> p a f", a=32),
                    transpose=True),
                    reads=[bw["ya"]], writes=[bw["yF"]], inc=16, grp="g_yF")
                for tt in range(4):
                    for eh in range(2):
                        ps = pA if eh == 0 else pB

                        def mko(tt_, eh_, ps_):
                            def f(e):
                                ins = None
                                for dc in range(8):
                                    ins = e.matmul(
                                        ps_[:], yFb[:, tt_, dc, :],
                                        wsb3["wo"][:, dc, eh_ * 512:(eh_ + 1) * 512],
                                        start=(dc == 0), stop=(dc == 7))
                                return ins
                            return f
                        tr.op("pe", mko(tt, eh, ps),
                              reads=[bw["w4"], bw["yF"]], writes=[P(ps)])
                        tr.op("dve", (lambda tt_, eh_, ps_, v_: lambda e: e.tensor_tensor(
                            v_.xt3[:, tt_, eh_ * 512:(eh_ + 1) * 512], ps_[:],
                            v_.xt3[:, tt_, eh_ * 512:(eh_ + 1) * 512], ALU.add))(
                                tt, eh, ps, v),
                            reads=[P(ps), v.b_xt], writes=[v.b_xt])
                tr.op("sync", (lambda t_, v_: lambda e: e.dma_start(
                    out=x1_r[:, t_ * 4:(t_ + 1) * 4, :], in_=v_.xt3))(i, v),
                    reads=[v.b_xt], writes=[bx1d], inc=16, grp="g_x1d")

            _ln2_pend = []

            def _emit_ln2(n):
                while n > 0 and _ln2_pend:
                    _ln2_pend.pop(0)()
                    n -= 1

            def ln2stats(i):
                v = TV(i)
                for tt in range(4):
                    a = i * 4 + tt

                    def s1(tt_=tt, a_=a, v_=v):
                        tr.op("act", lambda e: e.activation(
                            scr[:], v_.xt3[:, tt_, :], AF.Square,
                            accum_out=ln2st[:, a_:a_ + 1]),
                            reads=[v_.b_xt], writes=[bw["scr"], bln2])

                    def s2(tt_=tt, a_=a, v_=v):
                        tr.op("act", lambda e: e.activation(
                            scr[:], v_.xt3[:, tt_, :], AF.Identity,
                            accum_out=ln2st[:, 32 + a_:33 + a_]),
                            reads=[v_.b_xt], writes=[bw["scr"], bln2])
                    _ln2_pend.append(s1)
                    _ln2_pend.append(s2)

            quantA = make_quant(tr, SA, "qa", Buf, nsets=2, maxw=704,
                                share_u=None, cbuf=BC["wgam"], pipe3=True)
            _wq_raw_jobs = []

            def _mk_wq(raw_r, dst_r, nrows, width, nblk_tr, dram_buf0, lst):
                nhalf = 2 if width == 2816 else 1
                hw_ = width // nhalf
                pw = 704 if width == 2816 else 512
                nbh = nblk_tr // nhalf
                for rt in range(nrows):
                    if dram_buf0 is None:
                        dram_buf = (bwgq6 if dst_r is wgq_r else bwuq6)[
                            min(rt // 4, 5)]
                    else:
                        dram_buf = dram_buf0
                    for hh in range(nhalf):
                        lst.append(dict(rt=rt, hh=hh, raw_r=raw_r, dst_r=dst_r,
                                        hw=hw_, nbh=nbh, dram_buf=dram_buf,
                                        pw=pw))

            _jg, _ju, _jd = [], [], []
            _mk_wq(wg_raw, wgq_r, 22, 1024, 8, None, _jg)
            _mk_wq(wu_raw, wuq_r, 22, 1024, 8, None, _ju)
            _mk_wq(wd_raw, wdq_r, 8, 2816, 22, bwdq, _jd)
            _jobs = []
            for a, b2 in zip(_jg, _ju):
                _jobs.append(a)
                _jobs.append(b2)
            _jobs.extend(_jd)

            def _wq_load(jj):
                j = _jobs[jj]
                wt = wtile2[jj % 2]
                wtb = bw["wtile%d" % (jj % 2)]
                tr.op("gp", (lambda j_, wt_: lambda e: e.dma_start(
                    out=wt_[:, 0:j_["hw"]],
                    in_=j_["raw_r"][j_["rt"] * 128:(j_["rt"] + 1) * 128,
                                    j_["hh"] * j_["hw"]:(j_["hh"] + 1) * j_["hw"]]))(
                        j, wt),
                    writes=[wtb], inc=16, grp="g_wtile%d" % (jj % 2))

            def _wq_piece(jj, pc, stage):
                j = _jobs[jj]
                wt = wtile2[jj % 2]
                wtb = bw["wtile%d" % (jj % 2)]
                quantA(wt[:, pc * j["pw"]:(pc + 1) * j["pw"]], j["pw"],
                       [wtb], [wtb], tr.op, 2 * jj + pc, stage=stage)

            def _wq_store(jj):
                j = _jobs[jj]
                wt = wtile2[jj % 2]
                wtb = bw["wtile%d" % (jj % 2)]
                nbh_ = j["nbh"]
                tr.op("sync", (lambda j_, wt_, n_: lambda e: e.dma_start(
                    out=wttr[:, 0:n_ * 128].rearrange("p (a f) -> p a f", a=n_),
                    in_=wt_[:, 0:j_["hw"]].rearrange("p (a f) -> p a f", a=n_),
                    transpose=True))(j, wt, nbh_),
                    reads=[wtb], writes=[bw["wttr"]], inc=16, grp="g_wttr")
                tr.op("gp", (lambda j_, n_: lambda e: e.dma_start(
                    out=j_["dst_r"][:, j_["hh"] * n_:(j_["hh"] + 1) * n_,
                                    j_["rt"] * 128:(j_["rt"] + 1) * 128],
                    in_=wttr[:, 0:n_ * 128].rearrange("p (a f) -> p a f", a=n_)))(
                        j, nbh_),
                    reads=[bw["wttr"]], writes=[j["dram_buf"]], inc=16,
                    grp="g_wscr")

            NP = 2 * len(_jobs)
            _wqs = {"s": 0}

            def _emit_wq(nslots):
                for _ in range(nslots):
                    s = _wqs["s"]
                    if s > NP + 2:
                        return
                    _wqs["s"] += 1
                    if s == 0:
                        _wq_load(0)
                        _wq_load(1)
                        continue
                    p = s - 3
                    if 0 <= p < NP:
                        _wq_piece(p // 2, p % 2, "c")
                        if p % 2 == 1:
                            _wq_store(p // 2)
                            if p // 2 + 2 < len(_jobs):
                                _wq_load(p // 2 + 2)
                    p = s - 2
                    if 0 <= p < NP:
                        _wq_piece(p // 2, p % 2, "b")
                    p = s - 1
                    if 0 <= p < NP:
                        _wq_piece(p // 2, p % 2, "a")

            prep_a(0)
            prep_b(0)
            for i in range(NT_A):
                mid = (lambda i_=i: prep_a(i_ + 1)) if i + 1 < NT_A else None
                chunks(i, mid=mid)
                finish(i)
                if i + 1 < NT_A:
                    prep_b(i + 1)
                ln2stats(i)
            _emit_ln2(999)
            _emit_wq(999)
            globals()["_EMIT_WQ"] = lambda n: None

        bar = tr.barrier()
        with ExitStack() as _stB:
            def SBB(name, shape, dtype, **kw):
                return _stB.enter_context(nc.sbuf_tensor(name, shape, dtype, **kw))

            x1t = SBB("x1t", [128, 12 * 1024], BF16)
            xn2 = SBB("xn2", [128, 8 * 1024], BF16)
            xqF = SBB("xqF", [128, 8 * 1024], BF16, is_transpose_dest=True)
            h_t = SBB("h_t", [128, 8 * 2816], BF16)
            hqF0 = SBB("hqF0", [128, 22 * 128], BF16, is_transpose_dest=True)
            hqF1 = hqF0
            wgc0 = SBB("wgc0", [128, 8 * 512], BF16)
            wgc1 = SBB("wgc1", [128, 8 * 512], BF16)
            wuc0 = SBB("wuc0", [128, 8 * 512], BF16)
            wuc1 = SBB("wuc1", [128, 8 * 512], BF16)
            wd_sb = SBB("wd_sb", [128, 22 * 1024], BF16)
            sg0 = SBB("sg0", [128, 512], BF16)
            sg1 = SBB("sg1", [128, 512], BF16)
            ost0 = SBB("ost0", [128, 1024], F32)
            ost1 = SBB("ost1", [128, 1024], F32)
            lnst2 = SBB("lnst2", [128, 80], F32)
            bb = {nm: Buf(nm) for nm in
                  ("h_t", "hqF0", "hqF1", "wgc0", "wgc1",
                   "wuc0", "wuc1", "wd", "sg0", "sg1", "ost0", "ost1",
                   "lnst2")}
            bb["hqF1"] = bb["hqF0"]
            bb_x1tq = [Buf("x1tq0"), Buf("x1tq1"), Buf("x1tq2")]
            bb_xn2q = [Buf("xn2q0"), Buf("xn2q1")]
            bb_xqFq = [Buf("xqFq0"), Buf("xqFq1")]
            first = {s: True for s in tr.streams}

            _orig_op = tr.op

            def opB(stream, fn, reads=(), writes=(), inc=1, grp=None):
                if first[stream]:
                    first[stream] = False
                    reads = list(reads) + bar
                return _orig_op(stream, fn, reads=reads, writes=writes, inc=inc,
                                grp=grp)

            quantB = make_quant(tr, SBB, "qb", Buf, nsets=2, maxw=512,
                                cbuf=BC["wgam"], pipe3=True)

            qB_q = []
            qst = {"a": 0, "b": 0, "c": 0}

            def qB_push(xap, width, rb, wb):
                qB_q.append((xap, width, rb, wb))
                return len(qB_q) - 1

            def qB_step():
                if qst["b"] > qst["c"]:
                    p = qst["c"]
                    x, w, rb, wb = qB_q[p]
                    quantB(x, w, rb, wb, opB, p, stage="c")
                    qst["c"] += 1
                if qst["a"] > qst["b"]:
                    p = qst["b"]
                    x, w, rb, wb = qB_q[p]
                    quantB(x, w, rb, wb, opB, p, stage="b")
                    qst["b"] += 1
                if qst["a"] < len(qB_q):
                    p = qst["a"]
                    x, w, rb, wb = qB_q[p]
                    quantB(x, w, rb, wb, opB, p, stage="a")
                    qst["a"] += 1

            def qB_drain(pidx):
                while qst["c"] <= pidx:
                    qB_step()



            x1t3 = x1t[:].rearrange("p (a f) -> p a f", a=12)

            def slotq(i, q):
                return (2 * i + q) % 3
            xn23 = xn2[:].rearrange("p (a f) -> p a f", a=8)
            xqFb = xqF[:].rearrange("p (tt dc f) -> p tt dc f", tt=8, dc=8)
            ht3 = h_t[:].rearrange("p (a f) -> p a f", a=8)
            wd3 = wd_sb[:].rearrange("p (nc e) -> p nc e", nc=22)

            def prep_work(i, q):
                sq = slotq(i, q)
                bq = bb_x1tq[sq]
                opB("gp", (lambda t_, q_, sq_: lambda e: e.dma_start(
                    out=x1t3[:, sq_ * 4:(sq_ + 1) * 4, :],
                    in_=x1_r[:, t_ * 8 + q_ * 4:t_ * 8 + (q_ + 1) * 4, :]))(
                        i, q, sq),
                    reads=[bx1d], writes=[bq], inc=16, grp="g_x1tq%d" % sq)
                o = q * 4
                a0 = i * 8 + q * 4

                def L(base):
                    return lnst2[:, base + o:base + o + 4]
                opB("dve", (lambda a0_: lambda e: e.tensor_scalar(
                    L(16), ln2st[:, 32 + a0_:36 + a0_], 1.0 / D, None,
                    ALU.mult))(a0),
                    reads=[bln2], writes=[bb["lnst2"]])
                opB("dve", (lambda a0_: lambda e: e.tensor_scalar(
                    L(24), ln2st[:, a0_:a0_ + 4], 1.0 / D, EPS_LN,
                    ALU.mult, ALU.add))(a0),
                    reads=[bln2], writes=[bb["lnst2"]])
                opB("dve", lambda e: e.tensor_tensor(
                    L(32), L(16), L(16), ALU.mult),
                    reads=[bb["lnst2"]], writes=[bb["lnst2"]])
                opB("dve", lambda e: e.tensor_tensor(
                    L(40), L(24), L(32), ALU.subtract),
                    reads=[bb["lnst2"]], writes=[bb["lnst2"]])
                opB("act", lambda e: e.activation(L(48), L(40), AF.Ln),
                    reads=[bb["lnst2"]], writes=[bb["lnst2"]])
                opB("act", lambda e: e.activation(L(56), L(48), AF.Exp, scale=-0.5),
                    reads=[bb["lnst2"]], writes=[bb["lnst2"]])
                opB("dve", lambda e: e.tensor_tensor(
                    L(64), L(16), L(56), ALU.mult),
                    reads=[bb["lnst2"]], writes=[bb["lnst2"]])
                opB("dve", lambda e: e.tensor_scalar(
                    L(72), L(64), -1.0, None, ALU.mult),
                    reads=[bb["lnst2"]], writes=[bb["lnst2"]])
                for tt in range(q * 4, q * 4 + 4):
                    st_ = sq * 4 + (tt % 4)
                    opB("dve", (lambda tt_, st2_: lambda e: e.tensor_scalar(
                        xn23[:, tt_, :], x1t3[:, st2_, :],
                        lnst2[:, 56 + tt_:57 + tt_], lnst2[:, 72 + tt_:73 + tt_],
                        ALU.mult, ALU.add))(tt, st_),
                        reads=[bq, bb["lnst2"]], writes=[bb_xn2q[q]])
                last = None
                for tt in range(q * 4, q * 4 + 4):
                    for pc in range(2):
                        last = qB_push(xn23[:, tt, pc * 512:(pc + 1) * 512],
                                       512, [bb_xn2q[q]], [bb_xn2q[q]])
                        qB_step()
                prep_last[q] = last

            def prep_xbar(i, q):
                qB_drain(prep_last[q])
                opB("sync", (lambda q_: lambda e: e.dma_start(
                    out=xqF[:].rearrange("p (a f) -> p a f", a=64)[
                        :, q_ * 32:(q_ + 1) * 32, :],
                    in_=xn2[:].rearrange("p (a f) -> p a f", a=64)[
                        :, q_ * 32:(q_ + 1) * 32, :],
                    transpose=True))(q),
                    reads=[bb_xn2q[q]], writes=[bb_xqFq[q]], inc=16,
                    grp="g_xqFq%d" % q)

            prep_last = {}

            NKW = [512] * 5 + [256]
            NKOFF = [0, 512, 1024, 1536, 2048, 2560]

            def b_gate_nck(i, nck, fuse_tail):
                nw = NKW[nck]
                no = NKOFF[nck]
                wg_c = wgc0 if nck % 2 == 0 else wgc1
                wu_c = wuc0 if nck % 2 == 0 else wuc1
                Bg = bb["wgc0"] if nck % 2 == 0 else bb["wgc1"]
                Bu = bb["wuc0"] if nck % 2 == 0 else bb["wuc1"]
                opB("sync", (lambda no_, nw_, w_: lambda e: e.dma_start(
                    out=w_[:, 0:8 * nw_].rearrange("p (dc f) -> p dc f", dc=8),
                    in_=wgq_r[:, :, no_:no_ + nw_]))(no, nw, wg_c),
                    reads=[bwgq6[nck]], writes=[Bg], inc=16,
                    grp="g_wgc0" if nck % 2 == 0 else "g_wgc1")
                opB("sync", (lambda no_, nw_, w_: lambda e: e.dma_start(
                    out=w_[:, 0:8 * nw_].rearrange("p (dc f) -> p dc f", dc=8),
                    in_=wuq_r[:, :, no_:no_ + nw_]))(no, nw, wu_c),
                    reads=[bwuq6[nck]], writes=[Bu], inc=16,
                    grp="g_wuc0" if nck % 2 == 0 else "g_wuc1")
                for tt in range(8):
                    pg = pA if tt % 2 == 0 else pR1
                    pu = pB if tt % 2 == 0 else pR2
                    sg = sg0 if tt % 2 == 0 else sg1
                    Bs = bb["sg0"] if tt % 2 == 0 else bb["sg1"]

                    def mkg(tt_, nw_, w_, ps_):
                        def f(e):
                            ins = None
                            for dc in range(8):
                                ins = e.matmul(
                                    ps_[:, 0:nw_],
                                    xqFb[:, tt_, dc, :],
                                    w_[:, 0:8 * nw_].rearrange(
                                        "p (dc f) -> p dc f", dc=8)[:, dc, :],
                                    start=(dc == 0), stop=(dc == 7))
                            return ins
                        return f
                    opB("pe", mkg(tt, nw, wg_c, pg),
                        reads=[bb_xqFq[tt // 4], Bg], writes=[P(pg)])
                    opB("pe", mkg(tt, nw, wu_c, pu),
                        reads=[bb_xqFq[tt // 4], Bu], writes=[P(pu)])
                    opB("act", (lambda nw_, ps_, sg_: lambda e: e.activation(
                        sg_[:, 0:nw_], ps_[:, 0:nw_], AF.Silu))(nw, pg, sg),
                        reads=[P(pg)], writes=[Bs])
                    opB("dve", (lambda tt_, no_, nw_, ps_, sg_: lambda e: e.tensor_tensor(
                        ht3[:, tt_, no_:no_ + nw_], ps_[:, 0:nw_],
                        sg_[:, 0:nw_], ALU.mult))(tt, no, nw, pu, sg),
                        reads=[P(pu), Bs], writes=[bb["h_t"]])
                    hlast[tt] = qB_push(ht3[:, tt, no:no + nw], nw,
                                        [bb["h_t"]], [bb["h_t"]])
                    qB_step()
                    if fuse_tail and tt >= 1:
                        b_tail_tt(i, tt - 1)
                        if tt - 1 == 3 and i + 1 < NT_B:
                            prep_work(i + 1, 1)
                            prep_xbar(i + 1, 0)
                if fuse_tail:
                    b_tail_tt(i, 7)
                    if i + 1 < NT_B:
                        prep_xbar(i + 1, 1)

            hlast = {}

            def b_tail_tt(i, tt):
                qB_drain(hlast[tt])
                hqF = hqF0 if tt % 2 == 0 else hqF1
                Bh = bb["hqF0"] if tt % 2 == 0 else bb["hqF1"]
                opB("sync", (lambda tt_, hq_: lambda e: e.dma_start(
                    out=hq_[:].rearrange("p (a f) -> p a f", a=22),
                    in_=ht3[:, tt_, :].rearrange("p (a f) -> p a f", a=22),
                    transpose=True))(tt, hqF),
                    reads=[bb["h_t"]], writes=[Bh], inc=16,
                    grp="g_hqF0" if tt % 2 == 0 else "g_hqF1")
                hq3 = hqF[:].rearrange("p (a f) -> p a f", a=22)
                pdd = (pPT, pY)
                for eh in range(2):
                    pd = pdd[eh]

                    def mkd(tt_, eh_, hq3_, pd_):
                        def f(e):
                            ins = None
                            for ncc in range(22):
                                ins = e.matmul(
                                    pd_[:], hq3_[:, ncc, :],
                                    wd3[:, ncc, eh_ * 512:(eh_ + 1) * 512],
                                    start=(ncc == 0), stop=(ncc == 21))
                            return ins
                        return f
                    opB("pe", mkd(tt, eh, hq3, pd),
                        reads=[Bh, bb["wd"]], writes=[P(pd)])
                ost = ost0 if tt % 2 == 0 else ost1
                Bo = bb["ost0"] if tt % 2 == 0 else bb["ost1"]
                st_ = slotq(i, tt // 4) * 4 + (tt % 4)
                for eh in range(2):
                    pd = pdd[eh]
                    opB("dve", (lambda st2_, eh_, pd_, o_: lambda e: e.tensor_tensor(
                        o_[:, eh_ * 512:(eh_ + 1) * 512], pd_[:],
                        x1t3[:, st2_, eh_ * 512:(eh_ + 1) * 512], ALU.add))(
                            st_, eh, pd, ost),
                        reads=[P(pd), bb_x1tq[slotq(i, tt // 4)]], writes=[Bo])
                opB("sync", (lambda t_, tt_, o_: lambda e: e.dma_start(
                    out=out_r[:, t_ * 8 + tt_, :], in_=o_[:]))(i, tt, ost),
                    reads=[Bo], inc=16,
                    grp="g_out0" if tt % 2 == 0 else "g_out1")

            _emit_wq = globals()["_EMIT_WQ"]
            prep_work(0, 0)
            prep_work(0, 1)
            prep_xbar(0, 0)
            prep_xbar(0, 1)
            wd_loaded = [False]

            def _load_wd():
                if wd_loaded[0]:
                    return
                wd_loaded[0] = True
                opB("sync", lambda e: e.dma_start(
                    out=wd_sb[:].rearrange("p (nc e) -> p nc e", nc=22),
                    in_=wdq_r),
                    reads=[bwdq], writes=[bb["wd"]], inc=16, grp="g_wdsb")

            for i in range(NT_B):
                for nck in range(5):
                    b_gate_nck(i, nck, False)
                    if nck == 2 and i + 1 < NT_B:
                        prep_work(i + 1, 0)
                _load_wd()
                b_gate_nck(i, 5, True)
            tr.finalize(nc, _st, block)
    return nc


def kernel(L0_attn_x, L0_attn_Wq, L0_attn_Wk, L0_attn_Wv, L0_attn_Wo,
           L0_attn_W_gamma, L0_attn_b_gamma, L0_attn_W_lambda, L0_attn_b_lambda,
           L0_ffn_w_gate, L0_ffn_w_up, L0_ffn_w_down):
    ident, icol4, ones1, ones16, oh4 = _host_consts()
    x = np.ascontiguousarray(np.asarray(L0_attn_x, np.float32))
    wqT = np.ascontiguousarray(np.asarray(L0_attn_Wq, np.float32).T)
    wkT = np.ascontiguousarray(np.asarray(L0_attn_Wk, np.float32).T)
    wvT = np.ascontiguousarray(np.asarray(L0_attn_Wv, np.float32).T)
    woT = np.ascontiguousarray(np.asarray(L0_attn_Wo, np.float32).T)
    wg = np.ascontiguousarray(np.asarray(L0_ffn_w_gate, np.float32))
    wu = np.ascontiguousarray(np.asarray(L0_ffn_w_up, np.float32))
    wd = np.ascontiguousarray(np.asarray(L0_ffn_w_down, np.float32))

    def bc64(v):
        return np.ascontiguousarray(
            np.broadcast_to(np.tile(np.asarray(v, np.float32), 4), (128, 64)))

    nc = build_nc(debug=False)
    in_maps = []
    for b in range(8):
        in_maps.append({
            "x": np.ascontiguousarray(x[b]),
            "wqT": wqT, "wkT": wkT, "wvT": wvT, "woT": woT,
            "wg_raw": wg, "wu_raw": wu, "wd_raw": wd,
            "wgam_in": bc64(L0_attn_W_gamma), "bgam_in": bc64(L0_attn_b_gamma),
            "wlam_in": bc64(L0_attn_W_lambda), "blam_in": bc64(L0_attn_b_lambda),
            "c_ident": ident, "c_icol4": icol4, "c_ones1": ones1,
            "c_ones16": ones16, "c_oh4": oh4,
        })
    res = run_bass_kernel_spmd(nc, in_maps, core_ids=list(range(8)))
    return np.stack([np.asarray(r["out"], np.float32) for r in res.results])


# version 23
# speedup vs baseline: 1.0164x; 1.0027x over previous
import numpy as np
import ml_dtypes
import concourse.bass as bass
import concourse.mybir as mybir
from concourse.bass_utils import run_bass_kernel_spmd

F32 = mybir.dt.float32
BF16 = mybir.dt.bfloat16
U8 = mybir.dt.uint8
I16 = mybir.dt.int16
AF = mybir.ActivationFunctionType
ALU = mybir.AluOpType
AX = mybir.AxisListType
NBF = ml_dtypes.bfloat16

T, D, H, DK = 4096, 1024, 16, 64
C = 128
TILE_A = 512
TILE_B = 1024
NT_A = T // TILE_A
NT_B = T // TILE_B
DI = 2816
NCK = DI // 256
EPS_LN = 1e-5
MAGIC = np.float32(1.5 * 2 ** 23)


class _Ev:
    __slots__ = ("op",)

    def __init__(self, op):
        self.op = op


class _Op:
    __slots__ = ("stream", "fn", "inc", "deps", "needed", "cum", "idx", "grp")

    def __init__(self, stream, fn, inc, deps, idx, grp):
        self.stream = stream
        self.fn = fn
        self.inc = inc
        self.deps = deps
        self.needed = False
        self.cum = 0
        self.idx = idx
        self.grp = grp


class Buf:
    __slots__ = ("writer", "readers", "name")

    def __init__(self, name=""):
        self.writer = None
        self.readers = []
        self.name = name


SERIALIZE = False


class Tracker:

    def __init__(self):
        self.streams = {k: [] for k in ("pe", "dve", "act", "gp", "sync")}
        self.nops = 0
        self.last = {k: None for k in self.streams}
        self.groups = {}
        self.glast = {}

    def op(self, stream, fn, reads=(), writes=(), inc=1, grp=None):
        assert (inc == 16) == (grp is not None), f"grp required iff DMA ({grp})"
        deps = []
        if SERIALIZE:
            deps = [o for o in self.last.values() if o is not None]
        for b in reads:
            if b.writer is not None:
                deps.append(b.writer)
        for b in writes:
            if b.writer is not None:
                deps.append(b.writer)
            deps.extend(b.readers)
        deps = [d for d in deps
                if d.inc == 16 or not (stream == "pe" and d.stream == "pe")]
        o = _Op(stream, fn, inc, deps, self.nops, grp)
        self.nops += 1
        self.streams[stream].append(o)
        self.last[stream] = o
        if grp is not None:
            self.groups.setdefault(grp, 0)
            self.glast[grp] = o
        for b in reads:
            b.readers.append(o)
        for b in writes:
            b.writer = o
            b.readers = []
        return o

    def barrier(self):
        bufs = []
        for s, ops in self.streams.items():
            b = Buf(f"bar_{s}")
            b.writer = ops[-1] if ops else None
            bufs.append(b)
        for g, o in self.glast.items():
            b = Buf(f"bar_{g}")
            b.writer = o
            bufs.append(b)
        return bufs

    def finalize(self, nc, st, block):
        stream_sems = {s: st.enter_context(nc.semaphore(f"s_{s}"))
                       for s in self.streams}
        group_sems = {g: st.enter_context(nc.semaphore(f"sg_{g}"))
                      for g in self.groups}
        for ops in self.streams.values():
            for o in ops:
                for d in o.deps:
                    d.needed = True
        grp_stream = {}
        for s, ops in self.streams.items():
            c = 0
            for o in ops:
                if o.inc == 16:
                    self.groups[o.grp] += 16
                    o.cum = self.groups[o.grp]
                    grp_stream[o.grp] = s
                else:
                    if o.needed:
                        c += 1
                    o.cum = c

        def emit(stream):
            ops = self.streams[stream]
            eng_attr = {"pe": "tensor", "dve": "vector", "act": "scalar",
                        "gp": "gpsimd", "sync": "sync"}[stream]

            def body(eng):
                last_wait = {}
                for o in ops:
                    for d in o.deps:
                        if d.inc == 16:
                            sem, key, v = group_sems[d.grp], ("g", d.grp), d.cum
                        else:
                            if not d.needed:
                                continue
                            sem, key, v = (stream_sems[d.stream], ("s", d.stream),
                                           d.cum)
                        if v <= 0 or last_wait.get(key, -1) >= v:
                            continue
                        eng.wait_ge(sem, v)
                        last_wait[key] = v
                    ins = o.fn(eng)
                    if o.inc == 16:
                        ins.then_inc(group_sems[o.grp], 16)
                    elif o.needed:
                        ins.then_inc(stream_sems[o.stream], 1)
                for g, s2 in grp_stream.items():
                    if s2 == stream and self.groups[g] > 0:
                        eng.wait_ge(group_sems[g], self.groups[g])

            getattr(block, eng_attr)(body)
        for s in ("sync", "gp", "pe", "dve", "act"):
            emit(s)


def make_quant(tr, SAfn, prefix, bufmk, nsets=2, maxw=2816, share_u=None,
               cbuf=None, pipe3=False):
    nst = 3 if pipe3 else nsets
    nxn = 2 if pipe3 else nsets
    nm = 1 if pipe3 else nsets
    sts = [SAfn(f"{prefix}_st{s}", [128, 3 * (maxw // 16)], F32)
           for s in range(nst)]
    bst = [bufmk(f"{prefix}_st{s}") for s in range(nst)]
    xns = [SAfn(f"{prefix}_xn{s}", [128, maxw], BF16) for s in range(nxn)]
    bxn = [bufmk(f"{prefix}_xn{s}") for s in range(nxn)]
    us = [(share_u[0] if share_u else SAfn(f"{prefix}_u{s}", [128, maxw], F32))
          for s in range(nxn)]
    bu = [(share_u[1] if share_u else bufmk(f"{prefix}_u{s}"))
          for s in range(nxn)]
    qls = [SAfn(f"{prefix}_ql{s}", [128, maxw], BF16) for s in range(nxn)]
    bql = [bufmk(f"{prefix}_ql{s}") for s in range(nxn)]
    ms = [SAfn(f"{prefix}_m{s}", [128, maxw], U8) for s in range(nm)]
    bm = [bufmk(f"{prefix}_m{s}") for s in range(nm)]

    def quant_piece(xap, width, rbufs, wbufs, op, sidx, stage=0):
        st_t, st_b = sts[sidx % nst], bst[sidx % nst]
        xn_full, xn_b = xns[sidx % nxn], bxn[sidx % nxn]
        u_full, u_b = us[sidx % nxn], bu[sidx % nxn]
        ql_full, ql_b = qls[sidx % nxn], bql[sidx % nxn]
        m_full, m_b = ms[sidx % nm], bm[sidx % nm]
        nb = width // 16
        nbm = st_t.shape[1] // 3
        amax = st_t[:, 0:nb]
        scl = st_t[:, nbm:nbm + nb]
        inv = st_t[:, 2 * nbm:2 * nbm + nb]
        x3 = xap.rearrange("p (a b) -> p a b", b=16)
        xn = xn_full[:, 0:width]
        u = u_full[:, 0:width]
        ql = ql_full[:, 0:width]
        m = m_full[:, 0:width]
        creads = [cbuf] if cbuf is not None else []
        if stage in (0, "a"):
            op("dve", lambda e: e.tensor_reduce(
                amax, x3, AX.X, ALU.max, apply_absolute_value=True),
                reads=list(rbufs), writes=[st_b])
            op("dve", lambda e: e.tensor_scalar(
                scl, amax, 1.0 / 6.0, 1e-12, ALU.mult, ALU.max),
                reads=[st_b], writes=[st_b])
            op("dve", lambda e: e.reciprocal(inv, scl),
               reads=[st_b], writes=[st_b])
        if stage in (0, "b"):
            op("dve", lambda e: e.tensor_tensor(
                xn.rearrange("p (a b) -> p a b", b=16), x3,
                inv.unsqueeze(2).broadcast_to([128, nb, 16]), ALU.mult),
                reads=list(rbufs) + [st_b], writes=[xn_b])
            op("act", lambda e: e.activation(
                u, xn, AF.Identity, scale=2.0, bias=float(MAGIC)),
                reads=[xn_b] + creads, writes=[u_b])
            op("act", lambda e: e.activation(
                ql, u, AF.Identity, scale=0.5, bias=-float(MAGIC) * 0.5),
                reads=[u_b] + creads, writes=[ql_b])
            op("act", lambda e: e.activation(u, xn, AF.Abs),
               reads=[xn_b], writes=[u_b])
        if stage in (0, "c"):
            op("dve", lambda e: e.tensor_scalar(
                xn.bitcast(I16), xn.bitcast(I16), 32, None, ALU.add),
                reads=[xn_b], writes=[xn_b])
            op("dve", lambda e: e.tensor_scalar(
                xn.bitcast(I16), xn.bitcast(I16), -64, None, ALU.bitwise_and),
                reads=[xn_b], writes=[xn_b])
            op("dve", lambda e: e.tensor_scalar(
                m, u, 0.9, None, ALU.is_lt),
                reads=[u_b], writes=[m_b])
            op("dve", lambda e: e.copy_predicated(xn, m, ql),
                reads=[m_b, ql_b, xn_b], writes=[xn_b])
            op("dve", lambda e: e.tensor_tensor(
                x3, xn.rearrange("p (a b) -> p a b", b=16),
                scl.unsqueeze(2).broadcast_to([128, nb, 16]), ALU.mult),
                reads=[xn_b, st_b], writes=list(wbufs))
    return quant_piece


def _host_consts():
    ident = np.eye(128, dtype=NBF)
    icol4 = np.zeros((128, 512), dtype=NBF)
    for b in range(4):
        icol4[:, b * 128:(b + 1) * 128] = ident
    ones1 = np.ones((1, 128), dtype=NBF)
    ones16 = np.ones((16, 128), dtype=NBF)
    oh4 = np.zeros((16, 64), dtype=NBF)
    for c in range(4):
        for h in range(16):
            oh4[h, c * 16 + h] = 1
    return ident, icol4, ones1, ones16, oh4


def build_nc(debug=False):
    nc = bass.Bass("TRN2", target_bir_lowering=False)
    tr = Tracker()

    x_in = nc.dram_tensor("x", [T, D], F32, kind="ExternalInput").ap()
    wqT = nc.dram_tensor("wqT", [D, D], F32, kind="ExternalInput").ap()
    wkT = nc.dram_tensor("wkT", [D, D], F32, kind="ExternalInput").ap()
    wvT = nc.dram_tensor("wvT", [D, D], F32, kind="ExternalInput").ap()
    woT = nc.dram_tensor("woT", [D, D], F32, kind="ExternalInput").ap()
    wg_raw = nc.dram_tensor("wg_raw", [DI, D], F32, kind="ExternalInput").ap()
    wu_raw = nc.dram_tensor("wu_raw", [DI, D], F32, kind="ExternalInput").ap()
    wd_raw = nc.dram_tensor("wd_raw", [D, DI], F32, kind="ExternalInput").ap()
    gbc = {}
    for nm in ("wgam", "bgam", "wlam", "blam"):
        gbc[nm] = nc.dram_tensor(nm + "_in", [128, 64], F32, kind="ExternalInput").ap()
    c_ident = nc.dram_tensor("c_ident", [128, 128], BF16, kind="ExternalInput").ap()
    c_icol4 = nc.dram_tensor("c_icol4", [128, 512], BF16, kind="ExternalInput").ap()
    c_ones1 = nc.dram_tensor("c_ones1", [1, 128], BF16, kind="ExternalInput").ap()
    c_ones16 = nc.dram_tensor("c_ones16", [16, 128], BF16, kind="ExternalInput").ap()
    c_oh4 = nc.dram_tensor("c_oh4", [16, 64], BF16, kind="ExternalInput").ap()

    out_d = nc.dram_tensor("out", [T, D], F32, kind="ExternalOutput").ap()
    x1_kind = "ExternalOutput" if debug else "Internal"
    x1_d = nc.dram_tensor("x1_d", [T, D], BF16, kind=x1_kind).ap()
    wgq_d = nc.dram_tensor("wgq_d", [D, DI], BF16, kind=x1_kind).ap()
    wuq_d = nc.dram_tensor("wuq_d", [D, DI], BF16, kind="Internal").ap()
    wdq_d = nc.dram_tensor("wdq_d", [DI, D], BF16, kind=x1_kind).ap()
    dbg = {}
    if debug:
        dbg["q_dbg"] = nc.dram_tensor("q_dbg", [128, 8 * 512], BF16, kind="ExternalOutput").ap()
        dbg["k_dbg"] = nc.dram_tensor("k_dbg", [TILE_A, D], BF16, kind="ExternalOutput").ap()
        dbg["y_dbg"] = nc.dram_tensor("y_dbg", [TILE_A, D], BF16, kind="ExternalOutput").ap()
        dbg["rho_dbg"] = nc.dram_tensor("rho_dbg", [16, TILE_A], BF16, kind="ExternalOutput").ap()
        dbg["xt_dbg"] = nc.dram_tensor("xt_dbg", [128, 4 * 1024], BF16, kind="ExternalOutput").ap()
        dbg["xn1F_dbg"] = nc.dram_tensor("xn1F_dbg", [128, 4096], BF16, kind="ExternalOutput").ap()
        dbg["wq_dbg"] = nc.dram_tensor("wq_dbg", [128, 8 * 1024], BF16, kind="ExternalOutput").ap()
        dbg["xn1_dbg"] = nc.dram_tensor("xn1_dbg", [128, 4 * 1024], BF16, kind="ExternalOutput").ap()

    x_r = x_in.rearrange("(a p) f -> p a f", p=128)
    x1_r = x1_d.rearrange("(a p) f -> p a f", p=128)
    out_r = out_d.rearrange("(a p) f -> p a f", p=128)
    wgq_r = wgq_d.rearrange("(dc p) n -> p dc n", p=128)
    wuq_r = wuq_d.rearrange("(dc p) n -> p dc n", p=128)
    wdq_r = wdq_d.rearrange("(nc p) e -> p nc e", p=128)

    from contextlib import ExitStack
    with ExitStack() as _st:
        def SB(name, shape, dtype):
            return _st.enter_context(nc.sbuf_tensor(name, shape, dtype))

        def PS(name, shape, dtype):
            return _st.enter_context(nc.psum_tensor(name, shape, dtype))

        ident = SB("ident", [128, 128], BF16)
        icol4 = SB("icol4", [128, 512], BF16)
        ones1 = SB("ones1", [1, 128], BF16)
        ones16 = SB("ones16", [16, 128], BF16)
        oh4 = SB("oh4", [16, 64], BF16)
        cmagic = SB("cmagic", [128, 1], F32)
        cmagic2 = SB("cmagic2", [128, 1], F32)
        wgam = SB("wgam", [128, 64], F32)
        bgam = SB("bgam", [128, 64], F32)
        wlam = SB("wlam", [128, 64], F32)
        blam = SB("blam", [128, 64], F32)
        ln2st = SB("ln2st", [128, 64], F32)
        pA = PS("pA", [128, 512], F32)
        pB = PS("pB", [128, 512], F32)
        pR1 = PS("pR1", [128, 512], F32)
        pR2 = PS("pR2", [128, 512], F32)
        pPT = PS("pPT", [128, 512], F32)
        pY = PS("pY", [128, 512], F32)
        pM = PS("pM", [128, 512], F32)
        pT = PS("pT", [128, 1024], BF16)
        block = _st.enter_context(nc.Block())
        bx1d = Buf("x1_d")
        bln2 = Buf("ln2st")
        nc.const_aps.register(F32, float(MAGIC), cmagic[:])
        nc.const_aps.register(F32, -float(MAGIC) * 0.5, cmagic2[:])
        bwgq6 = [Buf(f"wgq_d{j}") for j in range(6)]
        bwuq6 = [Buf(f"wuq_d{j}") for j in range(6)]
        bwdq = Buf("wdq_d")
        b_const = Buf("consts")
        BC = {nm: b_const for nm in ("ident", "icol4", "ones1", "ones16", "oh4",
                                     "wgam", "bgam", "wlam", "blam")}
        pbuf = {id(p): Buf(nm) for nm, p in
                [("pA", pA), ("pB", pB), ("pR1", pR1), ("pR2", pR2),
                 ("pPT", pPT), ("pY", pY), ("pM", pM), ("pT", pT)]}

        def P(p):
            return pbuf[id(p)]

        for src, dst, nm in [(c_ident, ident, "ident"), (c_icol4, icol4, "icol4"),
                             (c_ones1, ones1, "ones1"), (c_ones16, ones16, "ones16"),
                             (c_oh4, oh4, "oh4"), (gbc["wgam"], wgam, "wgam"),
                             (gbc["bgam"], bgam, "bgam"), (gbc["wlam"], wlam, "wlam"),
                             (gbc["blam"], blam, "blam")]:
            tr.op("sync", (lambda s_, d_: lambda e: e.dma_start(out=d_[:], in_=s_))(src, dst),
                  writes=[BC[nm]], inc=16, grp="g_const")
        tr.op("dve", lambda e: e.memset(cmagic[:], float(MAGIC)), writes=[BC["wgam"]])
        tr.op("dve", lambda e: e.memset(cmagic2[:], -float(MAGIC) * 0.5),
              writes=[BC["wgam"]])

        with ExitStack() as _stA:
            def SA(name, shape, dtype, **kw):
                return _stA.enter_context(nc.sbuf_tensor(name, shape, dtype, **kw))

            def SA2(name, shape, dtype, **kw):
                return [SA(name + "_0", shape, dtype, **kw),
                        SA(name + "_1", shape, dtype, **kw)]

            wq_sb = SA("wq_sb", [128, 8 * 1024], BF16)
            wk_sb = SA("wk_sb", [128, 8 * 1024], BF16)
            wv_sb = SA("wv_sb", [128, 8 * 1024], BF16)
            wo_sb = SA("wo_sb", [128, 8 * 1024], BF16)
            S_sb = SA("S_sb", [128, 512], BF16)
            xt2 = SA2("xt", [128, 4 * 1024], BF16)
            xn12 = [SA("xn1", [128, 4 * 1024], BF16)] * 2
            xn1F2 = SA2("xn1F", [128, 8 * 512], BF16, is_transpose_dest=True)
            qF2 = [SA("qF", [128, 8 * 512], BF16)] * 2
            vF2 = [SA("vF", [128, 8 * 512], BF16)] * 2
            ktok2 = [SA("ktok", [128, 4 * 1024], BF16)] * 2
            vtok2 = [SA("vtok", [128, 4 * 1024], BF16, is_transpose_dest=True)] * 2
            run42 = [SA("run4", [128, 64], F32)] * 2
            din_T2 = [SA("din_T", [128, 64], F32)] * 2
            dcol2 = [SA("dcol", [128, 64], F32)] * 2
            ya = SA("ya", [128, 4 * 1024], BF16)
            yF = SA("yF", [128, 8 * 512], BF16, is_transpose_dest=True)
            scr = SA("scr", [128, 1024], F32)
            lnst = SA("lnst", [128, 40], F32)
            kst = SA("kst", [128, 6 * 64], F32)
            rho4 = SA("rho4", [128, 64], BF16)
            rho_ht = SA("rho_ht", [16, 512], BF16)
            rho0_ht = SA("rho0_ht", [16, 512], BF16)
            rhoS = SA("rhoS", [16, 512], BF16)
            din_bf = SA("din_bf", [16, 512], BF16)
            rstg = SA("rstg", [16, 64], BF16)
            rho0_sep = [SA("rho0_sep0", [1, 2048], BF16),
                        SA("rho0_sep1", [1, 2048], BF16)]
            E0 = SA("E0", [128, 512], BF16)
            E1 = SA("E1", [128, 512], BF16)
            mtp0 = SA("mtp0", [128, 512], BF16)
            mtp1 = SA("mtp1", [128, 512], BF16)
            ku0 = SA("ku0", [128, 256], BF16)
            ku1 = SA("ku1", [128, 256], BF16)
            vw0 = SA("vw0", [128, 256], BF16)
            vw1 = SA("vw1", [128, 256], BF16)
            vws = SA("vws", [128, 16], F32)
            y2s0 = SA("y2s0", [128, 256], F32)
            y2s1 = SA("y2s1", [128, 256], F32)
            wtile2 = [SA("wtile0", [128, 1408], BF16),
                      SA("wtile1", [128, 1408], BF16)]
            wttr = SA("wttr", [128, 11 * 128], BF16, is_transpose_dest=True)

            singles = ("S", "ya", "yF", "scr", "lnst", "kst", "rho4", "w4",
                       "rho_ht", "rho0_ht", "rhoS", "din_bf",
                       "rstg", "rho0_sep0", "rho0_sep1", "E0", "E1",
                       "mtp0", "mtp1",
                       "ku0", "ku1", "vw0", "vw1", "vws", "y2s0", "y2s1",
                       "wtile0", "wtile1", "wttr")
            bw = {nm: Buf(nm) for nm in singles}
            doubles = ("xt", "xn1", "xn1F", "qF", "vF", "ktok", "vtok",
                       "run4", "din_T", "dcol")
            real_doubles = ("xt", "xn1F")
            bw2 = {nm: ([Buf(nm + "_0"), Buf(nm + "_1")] if nm in real_doubles
                        else [Buf(nm)] * 2) for nm in doubles}

            for src_, dst in [(wqT, wq_sb), (wkT, wk_sb), (wvT, wv_sb), (woT, wo_sb)]:
                tr.op("gp", (lambda s_, d_: lambda e: e.dma_start(
                    out=d_[:].rearrange("p (dc e) -> p dc e", dc=8),
                    in_=s_.rearrange("(dc p) e -> p dc e", p=128)))(src_, dst),
                    writes=[bw["w4"]], inc=16, grp="g_attnw")
            tr.op("dve", lambda e: e.memset(S_sb[:], 0.0), writes=[bw["S"]])
            tr.op("dve", lambda e: e.memset(rhoS[:], 0.0), writes=[bw["rhoS"]])

            ya3 = ya[:].rearrange("p (a f) -> p a f", a=4)
            yFb = yF[:].rearrange("p (cc dc f) -> p cc dc f", cc=4, dc=8)
            wsb3 = {nm: w[:].rearrange("p (dc e) -> p dc e", dc=8)
                    for nm, w in [("wq", wq_sb), ("wk", wk_sb),
                                  ("wv", wv_sb), ("wo", wo_sb)]}

            HGRP = [[0, 2, 4, 6], [8, 10, 12, 14],
                    [1, 3, 5, 7], [9, 11, 13, 15]]
            GBASE = [0, 0, 64, 64]
            GOFF = [0, 8, 1, 9]

            def Ssl(h):
                return S_sb[(h % 2) * 64:(h % 2) * 64 + 64,
                            (h // 2) * 64:(h // 2) * 64 + 64]

            def Sgq(g):
                return S_sb[GBASE[g]:GBASE[g] + 64,
                            (g % 2) * 256:(g % 2) * 256 + 256]

            def hsel4(ap16, g):
                st_ = GOFF[g] % 2
                lo = (GOFF[g] % 8) // 2 + (GOFF[g] // 8) * 4
                lo = 4 if GOFF[g] >= 8 else 0
                return ap16[:, st_::2, :][:, lo:lo + 4, :]

            def hslice(buf3, h, cols):
                return buf3[(h % 2) * 64:(h % 2) * 64 + 64, h // 2, cols]

            class TV:

                def __init__(self, i):
                    b = i % 2
                    self.i = i
                    self.xt3 = xt2[b][:].rearrange("p (a f) -> p a f", a=4)
                    self.xn13 = xn12[b][:].rearrange("p (a f) -> p a f", a=4)
                    self.xn1 = xn12[b]
                    self.xn1F = xn1F2[b]
                    self.xn1Fb = xn1F2[b][:].rearrange(
                        "p (tt dc f) -> p tt dc f", tt=4, dc=8)
                    self.qF = qF2[b]
                    self.vF = vF2[b]
                    self.qF3 = qF2[b][:].rearrange("p (a f) -> p a f", a=8)
                    self.vF3 = vF2[b][:].rearrange("p (a f) -> p a f", a=8)
                    self.ktok = ktok2[b]
                    self.ktok3 = ktok2[b][:].rearrange("p (a f) -> p a f", a=4)
                    self.vtok = vtok2[b]
                    self.vtokb = vtok2[b][:].rearrange(
                        "p (ec tt f) -> p ec tt f", ec=8, tt=4)
                    self.run4 = run42[b]
                    self.din_T = din_T2[b]
                    self.dcol = dcol2[b]
                    for nm in doubles:
                        setattr(self, "b_" + nm, bw2[nm][b])

            def prep_a(i):
                v = TV(i)
                tr.op("gp", (lambda t_, v_: lambda e: e.dma_start(
                    out=v_.xt3, in_=x_r[:, t_ * 4:(t_ + 1) * 4, :]))(i, v),
                    writes=[v.b_xt], inc=16, grp="g_xt%d" % (i % 2))
                for tt in range(4):
                    tr.op("act", (lambda tt_, v_: lambda e: e.activation(
                        scr[:], v_.xt3[:, tt_, :], AF.Square,
                        accum_out=lnst[:, tt_:tt_ + 1]))(tt, v),
                        reads=[v.b_xt], writes=[bw["scr"], bw["lnst"]])
                    tr.op("dve", (lambda tt_, v_: lambda e: e.reduce_sum(
                        lnst[:, 4 + tt_:5 + tt_], v_.xt3[:, tt_, :], AX.X))(tt, v),
                        reads=[v.b_xt], writes=[bw["lnst"]])
                tr.op("dve", lambda e: e.tensor_scalar(
                    lnst[:, 8:12], lnst[:, 4:8], 1.0 / D, None, ALU.mult),
                    reads=[bw["lnst"]], writes=[bw["lnst"]])
                tr.op("dve", lambda e: e.tensor_scalar(
                    lnst[:, 12:16], lnst[:, 0:4], 1.0 / D, EPS_LN, ALU.mult, ALU.add),
                    reads=[bw["lnst"]], writes=[bw["lnst"]])
                tr.op("dve", lambda e: e.tensor_tensor(
                    lnst[:, 16:20], lnst[:, 8:12], lnst[:, 8:12], ALU.mult),
                    reads=[bw["lnst"]], writes=[bw["lnst"]])
                tr.op("dve", lambda e: e.tensor_tensor(
                    lnst[:, 20:24], lnst[:, 12:16], lnst[:, 16:20], ALU.subtract),
                    reads=[bw["lnst"]], writes=[bw["lnst"]])
                tr.op("act", lambda e: e.activation(
                    lnst[:, 24:28], lnst[:, 20:24], AF.Ln),
                    reads=[bw["lnst"]], writes=[bw["lnst"]])
                tr.op("act", lambda e: e.activation(
                    lnst[:, 28:32], lnst[:, 24:28], AF.Exp, scale=-0.5),
                    reads=[bw["lnst"]], writes=[bw["lnst"]])
                tr.op("dve", lambda e: e.tensor_tensor(
                    lnst[:, 32:36], lnst[:, 8:12], lnst[:, 28:32], ALU.mult),
                    reads=[bw["lnst"]], writes=[bw["lnst"]])
                tr.op("dve", lambda e: e.tensor_scalar(
                    lnst[:, 36:40], lnst[:, 32:36], -1.0, None, ALU.mult),
                    reads=[bw["lnst"]], writes=[bw["lnst"]])
                for tt in range(4):
                    tr.op("dve", (lambda tt_, v_: lambda e: e.tensor_scalar(
                        v_.xn13[:, tt_, :], v_.xt3[:, tt_, :],
                        lnst[:, 28 + tt_:29 + tt_], lnst[:, 36 + tt_:37 + tt_],
                        ALU.mult, ALU.add))(tt, v),
                        reads=[v.b_xt, bw["lnst"]], writes=[v.b_xn1])
                tr.op("sync", (lambda v_: lambda e: e.dma_start(
                    out=v_.xn1F[:].rearrange("p (a f) -> p a f", a=32),
                    in_=v_.xn1[:].rearrange("p (a f) -> p a f", a=32),
                    transpose=True))(v),
                    reads=[v.b_xn1], writes=[v.b_xn1F], inc=16,
                    grp="g_xn1F%d" % (i % 2))

            def prep_b(i):
                v = TV(i)
                for dst3, dstb, wnm in [(v.qF3, v.b_qF, "wq"), (v.vF3, v.b_vF, "wv")]:
                    for ec in range(8):
                        ps = pA if ec % 2 == 0 else pB

                        def mk(ec_, ps_, wnm_, v_):
                            def f(e):
                                ins = None
                                for dc in range(8):
                                    ins = e.matmul(
                                        ps_[:],
                                        wsb3[wnm_][:, dc, ec_ * 128:(ec_ + 1) * 128],
                                        v_.xn1Fb[:, :, dc, :],
                                        start=(dc == 0), stop=(dc == 7))
                                return ins
                            return f
                        tr.op("pe", mk(ec, ps, wnm, v),
                              reads=[bw["w4"], v.b_xn1F], writes=[P(ps)])
                        tr.op("act", (lambda ec_, ps_, d3_: lambda e: e.copy(
                            d3_[:, ec_, :], ps_[:]))(ec, ps, dst3),
                            reads=[P(ps)], writes=[dstb])
                for tt in range(4):
                    for eh in range(2):
                        ps = pA if eh == 0 else pB

                        def mkk(tt_, eh_, ps_, v_):
                            def f(e):
                                ins = None
                                for dc in range(8):
                                    ins = e.matmul(
                                        ps_[:], v_.xn1Fb[:, tt_, dc, :],
                                        wsb3["wk"][:, dc, eh_ * 512:(eh_ + 1) * 512],
                                        start=(dc == 0), stop=(dc == 7))
                                return ins
                            return f
                        tr.op("pe", mkk(tt, eh, ps, v),
                              reads=[bw["w4"], v.b_xn1F], writes=[P(ps)])
                        tr.op("act", (lambda tt_, eh_, ps_, v_: lambda e: e.copy(
                            v_.ktok3[:, tt_, eh_ * 512:(eh_ + 1) * 512], ps_[:]))(
                                tt, eh, ps, v),
                            reads=[P(ps)], writes=[v.b_ktok])
                tr.op("sync", (lambda v_: lambda e: e.dma_start(
                    out=v_.vtok[:].rearrange("p (a f) -> p a f", a=32),
                    in_=v_.vF[:].rearrange("p (a f) -> p a f", a=32),
                    transpose=True))(v),
                    reads=[v.b_vF], writes=[v.b_vtok], inc=16,
                    grp="g_vtok%d" % (i % 2))
                for tt in range(4):
                    tr.op("act", (lambda tt_, v_: lambda e: e.activation(
                        scr[:], v_.ktok3[:, tt_, :], AF.Square))(tt, v),
                        reads=[v.b_ktok], writes=[bw["scr"]])
                    tr.op("dve", (lambda tt_: lambda e: e.tensor_reduce(
                        kst[:, tt_ * 16:(tt_ + 1) * 16],
                        scr[:].rearrange("p (a b) -> p a b", b=64),
                        AX.X, ALU.add))(tt),
                        reads=[bw["scr"]], writes=[bw["kst"]])
                    tr.op("dve", (lambda tt_, v_: lambda e: e.tensor_reduce(
                        kst[:, 64 + tt_ * 16:64 + (tt_ + 1) * 16],
                        v_.ktok3[:, tt_, :].rearrange("p (a b) -> p a b", b=64),
                        AX.X, ALU.add))(tt, v),
                        reads=[v.b_ktok], writes=[bw["kst"]])
                tr.op("dve", lambda e: e.tensor_scalar(
                    kst[:, 128:192], kst[:, 0:64], 1e-24, None, ALU.max),
                    reads=[bw["kst"]], writes=[bw["kst"]])
                tr.op("act", lambda e: e.activation(
                    kst[:, 128:192], kst[:, 128:192], AF.Ln),
                    reads=[bw["kst"]], writes=[bw["kst"]])
                tr.op("act", lambda e: e.activation(
                    kst[:, 128:192], kst[:, 128:192], AF.Exp, scale=-0.5),
                    reads=[bw["kst"]], writes=[bw["kst"]])
                tr.op("dve", lambda e: e.tensor_tensor(
                    kst[:, 192:256], kst[:, 64:128], kst[:, 128:192], ALU.mult),
                    reads=[bw["kst"]], writes=[bw["kst"]])
                tr.op("dve", lambda e: e.scalar_tensor_tensor(
                    kst[:, 256:320], kst[:, 192:256], 1.0 / DK, wgam[:],
                    ALU.mult, ALU.mult),
                    reads=[bw["kst"], BC["wgam"]], writes=[bw["kst"]])
                tr.op("dve", lambda e: e.tensor_tensor(
                    kst[:, 256:320], kst[:, 256:320], bgam[:], ALU.add),
                    reads=[bw["kst"], BC["bgam"]], writes=[bw["kst"]])
                tr.op("act", lambda e: e.activation(
                    kst[:, 320:384], kst[:, 256:320], AF.Sigmoid),
                    reads=[bw["kst"]], writes=[bw["kst"]])
                tr.op("dve", lambda e: e.scalar_tensor_tensor(
                    kst[:, 256:320], kst[:, 192:256], 1.0 / DK, wlam[:],
                    ALU.mult, ALU.mult),
                    reads=[bw["kst"], BC["wlam"]], writes=[bw["kst"]])
                tr.op("dve", lambda e: e.tensor_tensor(
                    kst[:, 256:320], kst[:, 256:320], blam[:], ALU.add),
                    reads=[bw["kst"], BC["blam"]], writes=[bw["kst"]])
                tr.op("act", lambda e: e.activation(
                    kst[:, 256:320], kst[:, 256:320], AF.Sigmoid),
                    reads=[bw["kst"]], writes=[bw["kst"]])
                tr.op("dve", lambda e: e.tensor_scalar(
                    kst[:, 256:320], kst[:, 256:320], -1.0, 1.0, ALU.mult, ALU.add),
                    reads=[bw["kst"]], writes=[bw["kst"]])
                tr.op("dve", lambda e: e.tensor_tensor(
                    rho4[:], kst[:, 256:320], kst[:, 320:384], ALU.mult),
                    reads=[bw["kst"]], writes=[bw["rho4"]])
                tr.op("dve", (lambda v_: lambda e: e.tensor_scalar(
                    v_.run4[:], rho4[:], -1.0, 1.0, ALU.mult, ALU.add))(v),
                    reads=[bw["rho4"]], writes=[v.b_run4])
                tr.op("dve", (lambda v_: lambda e: e.tensor_tensor(
                    v_.run4[:], v_.run4[:], kst[:, 128:192], ALU.mult))(v),
                    reads=[v.b_run4, bw["kst"]], writes=[v.b_run4])
                for tt in range(4):
                    tr.op("pe", (lambda tt_: lambda e: e.transpose(
                        pT[0:16, 0:128], rho4[:, tt_ * 16:(tt_ + 1) * 16], ident[:]))(tt),
                        reads=[bw["rho4"], BC["ident"]], writes=[P(pT)])
                    tr.op("act", (lambda tt_: lambda e: e.copy(
                        rho_ht[:, tt_ * 128:(tt_ + 1) * 128], pT[0:16, 0:128]))(tt),
                        reads=[P(pT)], writes=[bw["rho_ht"]])
                tr.op("dve", lambda e: e.tensor_copy(rho0_ht[:], rho_ht[:]),
                      reads=[bw["rho_ht"]], writes=[bw["rho0_ht"]])
                tr.op("dve", lambda e: e.memset(
                    rho0_ht[:].rearrange("p (a f) -> p a f", a=4)[:, :, 0:1], 0.0),
                    writes=[bw["rho0_ht"]])
                tr.op("dve", lambda e: e.tensor_copy(
                    rhoS[:].rearrange("p (a f) -> p a f", a=4)[:, :, 0:1],
                    rho_ht[:].rearrange("p (a f) -> p a f", a=4)[:, :, 0:1]),
                    reads=[bw["rho_ht"]], writes=[bw["rhoS"]])
                tr.op("dve", lambda e: e.tensor_tensor_scan(
                    din_bf[:], rho0_ht[:], rhoS[:], 0.0, ALU.mult, ALU.add),
                    reads=[bw["rho0_ht"], bw["rhoS"]], writes=[bw["din_bf"]])
                for tt in range(4):
                    tr.op("pe", (lambda tt_: lambda e: e.transpose(
                        pT[0:128, 128:144], din_bf[:, tt_ * 128:(tt_ + 1) * 128],
                        ident[0:16, 0:16]))(tt),
                        reads=[bw["din_bf"], BC["ident"]], writes=[P(pT)])
                    tr.op("act", (lambda tt_, v_: lambda e: e.copy(
                        v_.din_T[:, tt_ * 16:(tt_ + 1) * 16], pT[0:128, 128:144]))(
                            tt, v),
                        reads=[P(pT)], writes=[v.b_din_T])
                tr.op("dve", lambda e: e.tensor_tensor(
                    rstg[:].rearrange("p (c j) -> p c j", c=4),
                    oh4[:].rearrange("p (c j) -> p c j", c=4),
                    din_bf[:, 127::128].unsqueeze(2).broadcast_to([16, 4, 16]),
                    ALU.mult),
                    reads=[bw["din_bf"], BC["oh4"]], writes=[bw["rstg"]])
                tr.op("pe", lambda e: e.matmul(
                    pM[0:128, 256:320], ones16[:], rstg[:], start=True, stop=True),
                    reads=[BC["ones16"], bw["rstg"]], writes=[P(pM)])
                tr.op("dve", (lambda v_: lambda e: e.tensor_copy(
                    v_.dcol[:], pM[0:128, 256:320]))(v),
                    reads=[P(pM)], writes=[v.b_dcol])
                if debug and i == 0:
                    tr.op("sync", lambda e: e.dma_start(
                        out=dbg["rho_dbg"], in_=rho_ht[:]),
                        reads=[bw["rho_ht"]], inc=16, grp="g_dbg0")
                    tr.op("sync", (lambda v_: lambda e: e.dma_start(
                        out=dbg["xt_dbg"], in_=v_.xt3[:, :, :].rearrange(
                            "p a f -> p (a f)") if False else xt2[0][:]))(v),
                        reads=[v.b_xt], inc=16, grp="g_dbg1")
                    tr.op("sync", lambda e: e.dma_start(
                        out=dbg["xn1_dbg"], in_=xn12[0][:]),
                        reads=[v.b_xn1], inc=16, grp="g_dbg2")
                    tr.op("sync", lambda e: e.dma_start(
                        out=dbg["xn1F_dbg"], in_=xn1F2[0][:]),
                        reads=[v.b_xn1F], inc=16, grp="g_dbg6")
                    tr.op("sync", lambda e: e.dma_start(
                        out=dbg["wq_dbg"], in_=wq_sb[:]),
                        reads=[bw["w4"]], inc=16, grp="g_dbg7")
                    tr.op("sync", lambda e: e.dma_start(
                        out=dbg["q_dbg"], in_=qF2[0][:]),
                        reads=[v.b_qF], inc=16, grp="g_dbg3")
                    tr.op("sync", (lambda v_: lambda e: e.dma_start(
                        out=dbg["k_dbg"].rearrange("(a p) f -> p a f", p=128),
                        in_=v_.ktok3))(v),
                        reads=[v.b_ktok], inc=16, grp="g_dbg4")

            def chunks(i, mid=None):
                v = TV(i)
                def emit_bc_scan(cc, g):
                    sep = rho0_sep[cc % 2]
                    sepb = bw["rho0_sep%d" % (cc % 2)]
                    if g == 0:
                        tr.op("sync", (lambda cc_, sep_: lambda e: e.dma_start(
                            out=sep_[0:1, :].rearrange("q (h t) -> q h t", h=16),
                            in_=rho0_ht[:, cc_ * 128:(cc_ + 1) * 128]))(cc, sep),
                            reads=[bw["rho0_ht"]], writes=[sepb], inc=16,
                            grp="g_rsep%d" % (cc % 2))
                    pRho = pR1 if g % 2 == 0 else pR2
                    E_sb = E0 if g % 2 == 0 else E1
                    Ebuf = bw["E0"] if g % 2 == 0 else bw["E1"]

                    def mk_bc(g_, pRho_, sep_):
                        def f(e):
                            rhs = hsel4(sep_[0:1, :].rearrange(
                                "q (hh t) -> q hh t", hh=16), g_)
                            return e.matmul(pRho_[:], ones1[:], rhs,
                                            start=True, stop=True)
                        return f
                    tr.op("pe", mk_bc(g, pRho, sep),
                          reads=[BC["ones1"], sepb], writes=[P(pRho)])
                    tr.op("dve", (lambda E_, pRho_: lambda e: e.tensor_tensor_scan(
                        E_[:], pRho_[:], icol4[:], 0.0, ALU.mult, ALU.add))(
                            E_sb, pRho),
                        reads=[P(pRho), BC["icol4"]], writes=[Ebuf])

                emit_bc_scan(0, 0)
                for cc in range(4):
                    if cc == 2 and mid is not None:
                        mid()
                    for g in range(4):
                        if g < 3:
                            emit_bc_scan(cc, g + 1)
                        elif cc < 3:
                            emit_bc_scan(cc + 1, 0)
                        heads = HGRP[g]
                        E_sb = E0 if g % 2 == 0 else E1
                        Ebuf = bw["E0"] if g % 2 == 0 else bw["E1"]
                        mtp = mtp0 if g % 2 == 0 else mtp1
                        Mbuf = bw["mtp0"] if g % 2 == 0 else bw["mtp1"]
                        ku = ku0 if g % 2 == 0 else ku1
                        Kbuf = bw["ku0"] if g % 2 == 0 else bw["ku1"]
                        vw = vw0 if g % 2 == 0 else vw1
                        Vbuf = bw["vw0"] if g % 2 == 0 else bw["vw1"]
                        y2s = y2s0 if g % 2 == 0 else y2s1
                        Ybuf = bw["y2s0"] if g % 2 == 0 else bw["y2s1"]
                        pPTg = pPT if g % 2 == 0 else pB
                        pYg = pY if g % 2 == 0 else pA

                        def mk_pt(cc_, heads_, v_, pPT_):
                            def f(e):
                                ins = None
                                for j, h in enumerate(heads_):
                                    cols = slice(cc_ * 128, (cc_ + 1) * 128)
                                    ins = e.matmul(
                                        pPT_[:, j * 128:(j + 1) * 128],
                                        hslice(v_.vF3, h, cols),
                                        hslice(v_.qF3, h, cols),
                                        start=True, stop=True)
                                return ins
                            return f
                        tr.op("pe", mk_pt(cc, heads, v, pPTg),
                              reads=[v.b_vF, v.b_qF], writes=[P(pPTg)])
                        tr.op("dve", (lambda mtp_, E_, pPT_: lambda e: e.tensor_tensor(
                            mtp_[:], E_[:], pPT_[:], ALU.mult))(mtp, E_sb, pPTg),
                            reads=[Ebuf, P(pPTg)], writes=[Mbuf])
                        kt4 = hsel4(v.ktok3[:, cc, :].rearrange(
                            "p (hh w) -> p hh w", hh=16), g)
                        run_q = hsel4(v.run4[:].rearrange(
                            "p (c hh) -> p c hh", c=4)[:, cc, :].unsqueeze(2), g)
                        tr.op("dve", (lambda ku_, kt4_, rq_: lambda e: e.tensor_tensor(
                            ku_[:].rearrange("p (j w) -> p j w", j=4), kt4_,
                            rq_.broadcast_to([128, 4, 64]), ALU.mult))(ku, kt4, run_q),
                            reads=[v.b_ktok, v.b_run4], writes=[Kbuf])

                        def mk_vws(E_, rq_):
                            def f(e):
                                el = E_[:].rearrange(
                                    "p (j t) -> p j t", j=4)[:, :, 127:128]
                                return e.tensor_tensor(
                                    vws[:, 0:4].unsqueeze(2), el, rq_, ALU.mult)
                            return f
                        tr.op("dve", mk_vws(E_sb, run_q),
                              reads=[Ebuf, v.b_run4], writes=[bw["vws"]])

                        def mk_vw(cc_, g_, vw_, v_):
                            def f(e):
                                ec0 = (GOFF[g_] % 8) // 2 + (GOFF[g_] // 8) * 4
                                woff = (GOFF[g_] % 2) * 64
                                vt4 = v_.vtokb[:, ec0:ec0 + 4, cc_, woff:woff + 64]
                                return e.tensor_tensor(
                                    vw_[:].rearrange("p (j w) -> p j w", j=4), vt4,
                                    vws[:, 0:4].unsqueeze(2).broadcast_to(
                                        [128, 4, 64]), ALU.mult)
                            return f
                        tr.op("dve", mk_vw(cc, g, vw, v),
                              reads=[v.b_vtok, bw["vws"]], writes=[Vbuf])

                        def mk_y(cc_, g_, heads_, mtp_, ku_, v_, pY_):
                            def f(e):
                                ins = None
                                for j, h in enumerate(heads_):
                                    cols = slice(cc_ * 128, (cc_ + 1) * 128)
                                    e.matmul(pY_[:, j * 64:(j + 1) * 64],
                                             mtp_[:, j * 128:(j + 1) * 128],
                                             ku_[:, j * 64:(j + 1) * 64],
                                             start=True, stop=True)
                                    ins = e.matmul(
                                        pY_[:, 256 + j * 64:256 + (j + 1) * 64],
                                        hslice(v_.qF3, h, cols), Ssl(h),
                                        start=True, stop=True)
                                return ins
                            return f
                        tr.op("pe", mk_y(cc, g, heads, mtp, ku, v, pYg),
                              reads=[Mbuf, Kbuf, v.b_qF, bw["S"]], writes=[P(pYg)])
                        def mk_y2s(cc_, g_, y2s_, v_, pY_):
                            def f(e):
                                dq = hsel4(v_.din_T[:].rearrange(
                                    "p (c hh) -> p c hh", c=4)[:, cc_, :]
                                    .unsqueeze(2), g_)
                                return e.tensor_tensor(
                                    y2s_[:].rearrange("p (j w) -> p j w", j=4),
                                    pY_[:, 256:512].rearrange(
                                        "p (j w) -> p j w", j=4),
                                    dq.broadcast_to([128, 4, 64]), ALU.mult)
                            return f
                        tr.op("dve", mk_y2s(cc, g, y2s, v, pYg),
                              reads=[P(pYg), v.b_din_T], writes=[Ybuf])

                        def mk_ya(cc_, g_, y2s_, pY_):
                            def f(e):
                                out = hsel4(ya3[:, cc_, :].rearrange(
                                    "p (hh w) -> p hh w", hh=16), g_)
                                return e.tensor_tensor(
                                    out, pY_[0:128, 0:256].rearrange(
                                        "p (j w) -> p j w", j=4),
                                    y2s_[:].rearrange("p (j w) -> p j w", j=4),
                                    ALU.add)
                            return f
                        tr.op("dve", mk_ya(cc, g, y2s, pYg),
                              reads=[P(pYg), Ybuf], writes=[bw["ya"]])

                        def mk_dl(cc_, g_, heads_, vw_, v_):
                            def f(e):
                                ins = None
                                rb_ = GBASE[g_]
                                tp = (0, 64) if rb_ else None
                                for j, h in enumerate(heads_):
                                    ins = e.matmul(
                                        pM[rb_:rb_ + 64, j * 64:(j + 1) * 64],
                                        vw_[:, j * 64:(j + 1) * 64],
                                        v_.ktok3[:, cc_, h * 64:(h + 1) * 64],
                                        start=True, stop=True, tile_position=tp)
                                return ins
                            return f
                        tr.op("pe", mk_dl(cc, g, heads, vw, v),
                              reads=[Vbuf, v.b_ktok], writes=[P(pM)])

                        def mk_st1(cc_, g_, v_):
                            def f(e):
                                rb_ = GBASE[g_]
                                sg = Sgq(g_)
                                dq = hsel4(v_.dcol[rb_:rb_ + 64, :].rearrange(
                                    "p (c hh) -> p c hh", c=4)[:, cc_, :].unsqueeze(2),
                                    g_)
                                return e.tensor_tensor(
                                    sg.rearrange("p (j w) -> p j w", j=4),
                                    sg.rearrange("p (j w) -> p j w", j=4),
                                    dq.broadcast_to([64, 4, 64]), ALU.mult)
                            return f

                        def mk_st2(g_):
                            def f(e):
                                rb_ = GBASE[g_]
                                return e.tensor_tensor(
                                    Sgq(g_), Sgq(g_), pM[rb_:rb_ + 64, 0:256],
                                    ALU.add)
                            return f
                        tr.op("dve", mk_st1(cc, g, v),
                              reads=[v.b_dcol, bw["S"]], writes=[bw["S"]])
                        tr.op("dve", mk_st2(g),
                              reads=[P(pM), bw["S"]], writes=[bw["S"]])
                        _emit_wq(1)
                        _emit_ln2(1)
                if debug and i == 0:
                    tr.op("sync", lambda e: e.dma_start(
                        out=dbg["y_dbg"].rearrange("(a p) f -> p a f", p=128),
                        in_=ya3),
                        reads=[bw["ya"]], inc=16, grp="g_dbg5")

            def finish(i):
                v = TV(i)
                tr.op("sync", lambda e: e.dma_start(
                    out=yF[:].rearrange("p (a f) -> p a f", a=32),
                    in_=ya[:].rearrange("p (a f) -> p a f", a=32),
                    transpose=True),
                    reads=[bw["ya"]], writes=[bw["yF"]], inc=16, grp="g_yF")
                for tt in range(4):
                    for eh in range(2):
                        ps = pA if eh == 0 else pB

                        def mko(tt_, eh_, ps_):
                            def f(e):
                                ins = None
                                for dc in range(8):
                                    ins = e.matmul(
                                        ps_[:], yFb[:, tt_, dc, :],
                                        wsb3["wo"][:, dc, eh_ * 512:(eh_ + 1) * 512],
                                        start=(dc == 0), stop=(dc == 7))
                                return ins
                            return f
                        tr.op("pe", mko(tt, eh, ps),
                              reads=[bw["w4"], bw["yF"]], writes=[P(ps)])
                        tr.op("dve", (lambda tt_, eh_, ps_, v_: lambda e: e.tensor_tensor(
                            v_.xt3[:, tt_, eh_ * 512:(eh_ + 1) * 512], ps_[:],
                            v_.xt3[:, tt_, eh_ * 512:(eh_ + 1) * 512], ALU.add))(
                                tt, eh, ps, v),
                            reads=[P(ps), v.b_xt], writes=[v.b_xt])
                tr.op("sync", (lambda t_, v_: lambda e: e.dma_start(
                    out=x1_r[:, t_ * 4:(t_ + 1) * 4, :], in_=v_.xt3))(i, v),
                    reads=[v.b_xt], writes=[bx1d], inc=16, grp="g_x1d")

            _ln2_pend = []

            def _emit_ln2(n):
                while n > 0 and _ln2_pend:
                    _ln2_pend.pop(0)()
                    n -= 1

            def ln2stats(i):
                v = TV(i)
                for tt in range(4):
                    a = i * 4 + tt

                    def s1(tt_=tt, a_=a, v_=v):
                        tr.op("act", lambda e: e.activation(
                            scr[:], v_.xt3[:, tt_, :], AF.Square,
                            accum_out=ln2st[:, a_:a_ + 1]),
                            reads=[v_.b_xt], writes=[bw["scr"], bln2])

                    def s2(tt_=tt, a_=a, v_=v):
                        tr.op("act", lambda e: e.activation(
                            scr[:], v_.xt3[:, tt_, :], AF.Identity,
                            accum_out=ln2st[:, 32 + a_:33 + a_]),
                            reads=[v_.b_xt], writes=[bw["scr"], bln2])
                    _ln2_pend.append(s1)
                    _ln2_pend.append(s2)

            quantA = make_quant(tr, SA, "qa", Buf, nsets=2, maxw=704,
                                share_u=None, cbuf=BC["wgam"], pipe3=True)
            _wq_raw_jobs = []

            def _mk_wq(raw_r, dst_r, nrows, width, nblk_tr, dram_buf0, lst):
                nhalf = 2 if width == 2816 else 1
                hw_ = width // nhalf
                pw = 704 if width == 2816 else 512
                nbh = nblk_tr // nhalf
                for rt in range(nrows):
                    if dram_buf0 is None:
                        dram_buf = (bwgq6 if dst_r is wgq_r else bwuq6)[
                            min(rt // 4, 5)]
                    else:
                        dram_buf = dram_buf0
                    for hh in range(nhalf):
                        lst.append(dict(rt=rt, hh=hh, raw_r=raw_r, dst_r=dst_r,
                                        hw=hw_, nbh=nbh, dram_buf=dram_buf,
                                        pw=pw))

            _jg, _ju, _jd = [], [], []
            _mk_wq(wg_raw, wgq_r, 22, 1024, 8, None, _jg)
            _mk_wq(wu_raw, wuq_r, 22, 1024, 8, None, _ju)
            _mk_wq(wd_raw, wdq_r, 8, 2816, 22, bwdq, _jd)
            _jobs = []
            for a, b2 in zip(_jg, _ju):
                _jobs.append(a)
                _jobs.append(b2)
            _jobs.extend(_jd)

            def _wq_load(jj):
                j = _jobs[jj]
                wt = wtile2[jj % 2]
                wtb = bw["wtile%d" % (jj % 2)]
                tr.op("gp", (lambda j_, wt_: lambda e: e.dma_start(
                    out=wt_[:, 0:j_["hw"]],
                    in_=j_["raw_r"][j_["rt"] * 128:(j_["rt"] + 1) * 128,
                                    j_["hh"] * j_["hw"]:(j_["hh"] + 1) * j_["hw"]]))(
                        j, wt),
                    writes=[wtb], inc=16, grp="g_wtile%d" % (jj % 2))

            def _wq_piece(jj, pc, stage):
                j = _jobs[jj]
                wt = wtile2[jj % 2]
                wtb = bw["wtile%d" % (jj % 2)]
                quantA(wt[:, pc * j["pw"]:(pc + 1) * j["pw"]], j["pw"],
                       [wtb], [wtb], tr.op, 2 * jj + pc, stage=stage)

            def _wq_store(jj):
                j = _jobs[jj]
                wt = wtile2[jj % 2]
                wtb = bw["wtile%d" % (jj % 2)]
                nbh_ = j["nbh"]
                tr.op("sync", (lambda j_, wt_, n_: lambda e: e.dma_start(
                    out=wttr[:, 0:n_ * 128].rearrange("p (a f) -> p a f", a=n_),
                    in_=wt_[:, 0:j_["hw"]].rearrange("p (a f) -> p a f", a=n_),
                    transpose=True))(j, wt, nbh_),
                    reads=[wtb], writes=[bw["wttr"]], inc=16, grp="g_wttr")
                tr.op("gp", (lambda j_, n_: lambda e: e.dma_start(
                    out=j_["dst_r"][:, j_["hh"] * n_:(j_["hh"] + 1) * n_,
                                    j_["rt"] * 128:(j_["rt"] + 1) * 128],
                    in_=wttr[:, 0:n_ * 128].rearrange("p (a f) -> p a f", a=n_)))(
                        j, nbh_),
                    reads=[bw["wttr"]], writes=[j["dram_buf"]], inc=16,
                    grp="g_wscr")

            NP = 2 * len(_jobs)
            _wqs = {"s": 0}

            def _emit_wq(nslots):
                for _ in range(nslots):
                    s = _wqs["s"]
                    if s > NP + 2:
                        return
                    _wqs["s"] += 1
                    if s == 0:
                        _wq_load(0)
                        _wq_load(1)
                        continue
                    p = s - 3
                    if 0 <= p < NP:
                        _wq_piece(p // 2, p % 2, "c")
                        if p % 2 == 1:
                            _wq_store(p // 2)
                            if p // 2 + 2 < len(_jobs):
                                _wq_load(p // 2 + 2)
                    p = s - 2
                    if 0 <= p < NP:
                        _wq_piece(p // 2, p % 2, "b")
                    p = s - 1
                    if 0 <= p < NP:
                        _wq_piece(p // 2, p % 2, "a")

            prep_a(0)
            prep_b(0)
            for i in range(NT_A):
                mid = (lambda i_=i: prep_a(i_ + 1)) if i + 1 < NT_A else None
                chunks(i, mid=mid)
                finish(i)
                if i + 1 < NT_A:
                    prep_b(i + 1)
                ln2stats(i)
            _emit_ln2(999)
            _emit_wq(999)
            globals()["_EMIT_WQ"] = lambda n: None

        bar = tr.barrier()
        with ExitStack() as _stB:
            def SBB(name, shape, dtype, **kw):
                return _stB.enter_context(nc.sbuf_tensor(name, shape, dtype, **kw))

            x1t = SBB("x1t", [128, 12 * 1024], BF16)
            xn2 = SBB("xn2", [128, 8 * 1024], BF16)
            xqF = SBB("xqF", [128, 8 * 1024], BF16, is_transpose_dest=True)
            h_t = SBB("h_t", [128, 8 * 2816], BF16)
            hqF0 = SBB("hqF0", [128, 22 * 128], BF16, is_transpose_dest=True)
            hqF1 = hqF0
            wgc0 = SBB("wgc0", [128, 8 * 512], BF16)
            wgc1 = SBB("wgc1", [128, 8 * 512], BF16)
            wuc0 = SBB("wuc0", [128, 8 * 512], BF16)
            wuc1 = SBB("wuc1", [128, 8 * 512], BF16)
            wd_sb = SBB("wd_sb", [128, 22 * 1024], BF16)
            sg0 = SBB("sg0", [128, 512], BF16)
            sg1 = SBB("sg1", [128, 512], BF16)
            ost0 = SBB("ost0", [128, 1024], F32)
            ost1 = SBB("ost1", [128, 1024], F32)
            lnst2 = SBB("lnst2", [128, 80], F32)
            bb = {nm: Buf(nm) for nm in
                  ("h_t", "hqF0", "hqF1", "wgc0", "wgc1",
                   "wuc0", "wuc1", "wd", "sg0", "sg1", "ost0", "ost1",
                   "lnst2")}
            bb["hqF1"] = bb["hqF0"]
            bb_x1tq = [Buf("x1tq0"), Buf("x1tq1"), Buf("x1tq2")]
            bb_xn2q = [Buf("xn2q0"), Buf("xn2q1")]
            bb_xqFq = [Buf("xqFq0"), Buf("xqFq1")]
            first = {s: True for s in tr.streams}

            _orig_op = tr.op

            def opB(stream, fn, reads=(), writes=(), inc=1, grp=None):
                if first[stream]:
                    first[stream] = False
                    reads = list(reads) + bar
                return _orig_op(stream, fn, reads=reads, writes=writes, inc=inc,
                                grp=grp)

            quantB = make_quant(tr, SBB, "qb", Buf, nsets=2, maxw=512,
                                cbuf=BC["wgam"], pipe3=True)

            qB_q = []
            qst = {"a": 0, "b": 0, "c": 0}

            def qB_push(xap, width, rb, wb):
                qB_q.append((xap, width, rb, wb))
                return len(qB_q) - 1

            def qB_step():
                if qst["b"] > qst["c"]:
                    p = qst["c"]
                    x, w, rb, wb = qB_q[p]
                    quantB(x, w, rb, wb, opB, p, stage="c")
                    qst["c"] += 1
                if qst["a"] > qst["b"]:
                    p = qst["b"]
                    x, w, rb, wb = qB_q[p]
                    quantB(x, w, rb, wb, opB, p, stage="b")
                    qst["b"] += 1
                if qst["a"] < len(qB_q):
                    p = qst["a"]
                    x, w, rb, wb = qB_q[p]
                    quantB(x, w, rb, wb, opB, p, stage="a")
                    qst["a"] += 1

            def qB_drain(pidx):
                while qst["c"] <= pidx:
                    qB_step()



            x1t3 = x1t[:].rearrange("p (a f) -> p a f", a=12)

            def slotq(i, q):
                return (2 * i + q) % 3
            xn23 = xn2[:].rearrange("p (a f) -> p a f", a=8)
            xqFb = xqF[:].rearrange("p (tt dc f) -> p tt dc f", tt=8, dc=8)
            ht3 = h_t[:].rearrange("p (a f) -> p a f", a=8)
            wd3 = wd_sb[:].rearrange("p (nc e) -> p nc e", nc=22)

            def prep_work(i, q):
                sq = slotq(i, q)
                bq = bb_x1tq[sq]
                opB("gp", (lambda t_, q_, sq_: lambda e: e.dma_start(
                    out=x1t3[:, sq_ * 4:(sq_ + 1) * 4, :],
                    in_=x1_r[:, t_ * 8 + q_ * 4:t_ * 8 + (q_ + 1) * 4, :]))(
                        i, q, sq),
                    reads=[bx1d], writes=[bq], inc=16, grp="g_x1tq%d" % sq)
                o = q * 4
                a0 = i * 8 + q * 4

                def L(base):
                    return lnst2[:, base + o:base + o + 4]
                opB("dve", (lambda a0_: lambda e: e.tensor_scalar(
                    L(16), ln2st[:, 32 + a0_:36 + a0_], 1.0 / D, None,
                    ALU.mult))(a0),
                    reads=[bln2], writes=[bb["lnst2"]])
                opB("dve", (lambda a0_: lambda e: e.tensor_scalar(
                    L(24), ln2st[:, a0_:a0_ + 4], 1.0 / D, EPS_LN,
                    ALU.mult, ALU.add))(a0),
                    reads=[bln2], writes=[bb["lnst2"]])
                opB("dve", lambda e: e.tensor_tensor(
                    L(32), L(16), L(16), ALU.mult),
                    reads=[bb["lnst2"]], writes=[bb["lnst2"]])
                opB("dve", lambda e: e.tensor_tensor(
                    L(40), L(24), L(32), ALU.subtract),
                    reads=[bb["lnst2"]], writes=[bb["lnst2"]])
                opB("act", lambda e: e.activation(L(48), L(40), AF.Ln),
                    reads=[bb["lnst2"]], writes=[bb["lnst2"]])
                opB("act", lambda e: e.activation(L(56), L(48), AF.Exp, scale=-0.5),
                    reads=[bb["lnst2"]], writes=[bb["lnst2"]])
                opB("dve", lambda e: e.tensor_tensor(
                    L(64), L(16), L(56), ALU.mult),
                    reads=[bb["lnst2"]], writes=[bb["lnst2"]])
                opB("dve", lambda e: e.tensor_scalar(
                    L(72), L(64), -1.0, None, ALU.mult),
                    reads=[bb["lnst2"]], writes=[bb["lnst2"]])
                for tt in range(q * 4, q * 4 + 4):
                    st_ = sq * 4 + (tt % 4)
                    opB("dve", (lambda tt_, st2_: lambda e: e.tensor_scalar(
                        xn23[:, tt_, :], x1t3[:, st2_, :],
                        lnst2[:, 56 + tt_:57 + tt_], lnst2[:, 72 + tt_:73 + tt_],
                        ALU.mult, ALU.add))(tt, st_),
                        reads=[bq, bb["lnst2"]], writes=[bb_xn2q[q]])
                last = None
                for tt in range(q * 4, q * 4 + 4):
                    for pc in range(2):
                        last = qB_push(xn23[:, tt, pc * 512:(pc + 1) * 512],
                                       512, [bb_xn2q[q]], [bb_xn2q[q]])
                        qB_step()
                prep_last[q] = last

            def prep_xbar(i, q):
                qB_drain(prep_last[q])
                opB("sync", (lambda q_: lambda e: e.dma_start(
                    out=xqF[:].rearrange("p (a f) -> p a f", a=64)[
                        :, q_ * 32:(q_ + 1) * 32, :],
                    in_=xn2[:].rearrange("p (a f) -> p a f", a=64)[
                        :, q_ * 32:(q_ + 1) * 32, :],
                    transpose=True))(q),
                    reads=[bb_xn2q[q]], writes=[bb_xqFq[q]], inc=16,
                    grp="g_xqFq%d" % q)

            prep_last = {}

            NKW = [512] * 5 + [256]
            NKOFF = [0, 512, 1024, 1536, 2048, 2560]

            def b_gate_nck(i, nck, fuse_tail):
                nw = NKW[nck]
                no = NKOFF[nck]
                wg_c = wgc0 if nck % 2 == 0 else wgc1
                wu_c = wuc0 if nck % 2 == 0 else wuc1
                Bg = bb["wgc0"] if nck % 2 == 0 else bb["wgc1"]
                Bu = bb["wuc0"] if nck % 2 == 0 else bb["wuc1"]
                opB("sync", (lambda no_, nw_, w_: lambda e: e.dma_start(
                    out=w_[:, 0:8 * nw_].rearrange("p (dc f) -> p dc f", dc=8),
                    in_=wgq_r[:, :, no_:no_ + nw_]))(no, nw, wg_c),
                    reads=[bwgq6[nck]], writes=[Bg], inc=16,
                    grp="g_wgc0" if nck % 2 == 0 else "g_wgc1")
                opB("sync", (lambda no_, nw_, w_: lambda e: e.dma_start(
                    out=w_[:, 0:8 * nw_].rearrange("p (dc f) -> p dc f", dc=8),
                    in_=wuq_r[:, :, no_:no_ + nw_]))(no, nw, wu_c),
                    reads=[bwuq6[nck]], writes=[Bu], inc=16,
                    grp="g_wuc0" if nck % 2 == 0 else "g_wuc1")
                for tt in range(8):
                    pg = pA if tt % 2 == 0 else pR1
                    pu = pB if tt % 2 == 0 else pR2
                    sg = sg0 if tt % 2 == 0 else sg1
                    Bs = bb["sg0"] if tt % 2 == 0 else bb["sg1"]

                    def mkg(tt_, nw_, w_, ps_):
                        def f(e):
                            ins = None
                            for dc in range(8):
                                ins = e.matmul(
                                    ps_[:, 0:nw_],
                                    xqFb[:, tt_, dc, :],
                                    w_[:, 0:8 * nw_].rearrange(
                                        "p (dc f) -> p dc f", dc=8)[:, dc, :],
                                    start=(dc == 0), stop=(dc == 7))
                            return ins
                        return f
                    opB("pe", mkg(tt, nw, wg_c, pg),
                        reads=[bb_xqFq[tt // 4], Bg], writes=[P(pg)])
                    opB("pe", mkg(tt, nw, wu_c, pu),
                        reads=[bb_xqFq[tt // 4], Bu], writes=[P(pu)])
                    opB("act", (lambda nw_, ps_, sg_: lambda e: e.activation(
                        sg_[:, 0:nw_], ps_[:, 0:nw_], AF.Silu))(nw, pg, sg),
                        reads=[P(pg)], writes=[Bs])
                    opB("dve", (lambda tt_, no_, nw_, ps_, sg_: lambda e: e.tensor_tensor(
                        ht3[:, tt_, no_:no_ + nw_], ps_[:, 0:nw_],
                        sg_[:, 0:nw_], ALU.mult))(tt, no, nw, pu, sg),
                        reads=[P(pu), Bs], writes=[bb["h_t"]])
                    hlast[tt] = qB_push(ht3[:, tt, no:no + nw], nw,
                                        [bb["h_t"]], [bb["h_t"]])
                    qB_step()
                    if fuse_tail and tt >= 1:
                        b_tail_tt(i, tt - 1)
                        if tt - 1 == 3 and i + 1 < NT_B:
                            prep_work(i + 1, 1)
                            prep_xbar(i + 1, 0)
                if fuse_tail:
                    b_tail_tt(i, 7)
                    if i + 1 < NT_B:
                        prep_xbar(i + 1, 1)

            hlast = {}

            def b_tail_tt(i, tt):
                qB_drain(hlast[tt])
                hqF = hqF0 if tt % 2 == 0 else hqF1
                Bh = bb["hqF0"] if tt % 2 == 0 else bb["hqF1"]
                opB("sync", (lambda tt_, hq_: lambda e: e.dma_start(
                    out=hq_[:].rearrange("p (a f) -> p a f", a=22),
                    in_=ht3[:, tt_, :].rearrange("p (a f) -> p a f", a=22),
                    transpose=True))(tt, hqF),
                    reads=[bb["h_t"]], writes=[Bh], inc=16,
                    grp="g_hqF0" if tt % 2 == 0 else "g_hqF1")
                hq3 = hqF[:].rearrange("p (a f) -> p a f", a=22)
                pdd = (pPT, pY)
                for eh in range(2):
                    pd = pdd[eh]

                    def mkd(tt_, eh_, hq3_, pd_):
                        def f(e):
                            ins = None
                            for ncc in range(22):
                                ins = e.matmul(
                                    pd_[:], hq3_[:, ncc, :],
                                    wd3[:, ncc, eh_ * 512:(eh_ + 1) * 512],
                                    start=(ncc == 0), stop=(ncc == 21))
                            return ins
                        return f
                    opB("pe", mkd(tt, eh, hq3, pd),
                        reads=[Bh, bb["wd"]], writes=[P(pd)])
                ost = ost0 if tt % 2 == 0 else ost1
                Bo = bb["ost0"] if tt % 2 == 0 else bb["ost1"]
                st_ = slotq(i, tt // 4) * 4 + (tt % 4)
                for eh in range(2):
                    pd = pdd[eh]
                    opB("dve", (lambda st2_, eh_, pd_, o_: lambda e: e.tensor_tensor(
                        o_[:, eh_ * 512:(eh_ + 1) * 512], pd_[:],
                        x1t3[:, st2_, eh_ * 512:(eh_ + 1) * 512], ALU.add))(
                            st_, eh, pd, ost),
                        reads=[P(pd), bb_x1tq[slotq(i, tt // 4)]], writes=[Bo])
                opB("sync", (lambda t_, tt_, o_: lambda e: e.dma_start(
                    out=out_r[:, t_ * 8 + tt_, :], in_=o_[:]))(i, tt, ost),
                    reads=[Bo], inc=16,
                    grp="g_out0" if tt % 2 == 0 else "g_out1")

            _emit_wq = globals()["_EMIT_WQ"]
            prep_work(0, 0)
            prep_work(0, 1)
            prep_xbar(0, 0)
            prep_xbar(0, 1)
            wd_loaded = [False]

            def _load_wd():
                if wd_loaded[0]:
                    return
                wd_loaded[0] = True
                opB("sync", lambda e: e.dma_start(
                    out=wd_sb[:].rearrange("p (nc e) -> p nc e", nc=22),
                    in_=wdq_r),
                    reads=[bwdq], writes=[bb["wd"]], inc=16, grp="g_wdsb")

            for i in range(NT_B):
                for nck in range(5):
                    b_gate_nck(i, nck, False)
                    if nck == 2 and i + 1 < NT_B:
                        prep_work(i + 1, 0)
                _load_wd()
                b_gate_nck(i, 5, True)
            tr.finalize(nc, _st, block)
    return nc


def kernel(L0_attn_x, L0_attn_Wq, L0_attn_Wk, L0_attn_Wv, L0_attn_Wo,
           L0_attn_W_gamma, L0_attn_b_gamma, L0_attn_W_lambda, L0_attn_b_lambda,
           L0_ffn_w_gate, L0_ffn_w_up, L0_ffn_w_down):
    ident, icol4, ones1, ones16, oh4 = _host_consts()
    x = np.ascontiguousarray(np.asarray(L0_attn_x, np.float32))
    wqT = np.ascontiguousarray(np.asarray(L0_attn_Wq, np.float32).T)
    wkT = np.ascontiguousarray(np.asarray(L0_attn_Wk, np.float32).T)
    wvT = np.ascontiguousarray(np.asarray(L0_attn_Wv, np.float32).T)
    woT = np.ascontiguousarray(np.asarray(L0_attn_Wo, np.float32).T)
    wg = np.ascontiguousarray(np.asarray(L0_ffn_w_gate, np.float32))
    wu = np.ascontiguousarray(np.asarray(L0_ffn_w_up, np.float32))
    wd = np.ascontiguousarray(np.asarray(L0_ffn_w_down, np.float32))

    def bc64(v):
        return np.ascontiguousarray(
            np.broadcast_to(np.tile(np.asarray(v, np.float32), 4), (128, 64)))

    nc = build_nc(debug=False)
    in_maps = []
    for b in range(8):
        in_maps.append({
            "x": np.ascontiguousarray(x[b]),
            "wqT": wqT, "wkT": wkT, "wvT": wvT, "woT": woT,
            "wg_raw": wg, "wu_raw": wu, "wd_raw": wd,
            "wgam_in": bc64(L0_attn_W_gamma), "bgam_in": bc64(L0_attn_b_gamma),
            "wlam_in": bc64(L0_attn_W_lambda), "blam_in": bc64(L0_attn_b_lambda),
            "c_ident": ident, "c_icol4": icol4, "c_ones1": ones1,
            "c_ones16": ones16, "c_oh4": oh4,
        })
    res = run_bass_kernel_spmd(nc, in_maps, core_ids=list(range(8)))
    return np.stack([np.asarray(r["out"], np.float32) for r in res.results])


# version 24
# speedup vs baseline: 1.0921x; 1.0745x over previous
import numpy as np
import ml_dtypes
import concourse.bass as bass
import concourse.mybir as mybir
from concourse.bass_utils import run_bass_kernel_spmd

F32 = mybir.dt.float32
BF16 = mybir.dt.bfloat16
U8 = mybir.dt.uint8
I16 = mybir.dt.int16
AF = mybir.ActivationFunctionType
ALU = mybir.AluOpType
AX = mybir.AxisListType
NBF = ml_dtypes.bfloat16

T, D, H, DK = 4096, 1024, 16, 64
C = 128
TILE_A = 512
TILE_B = 1024
NT_A = T // TILE_A
NT_B = T // TILE_B
DI = 2816
NCK = DI // 256
EPS_LN = 1e-5
MAGIC = np.float32(1.5 * 2 ** 23)


class _Ev:
    __slots__ = ("op",)

    def __init__(self, op):
        self.op = op


class _Op:
    __slots__ = ("stream", "fn", "inc", "deps", "needed", "cum", "idx", "grp")

    def __init__(self, stream, fn, inc, deps, idx, grp):
        self.stream = stream
        self.fn = fn
        self.inc = inc
        self.deps = deps
        self.needed = False
        self.cum = 0
        self.idx = idx
        self.grp = grp


class Buf:
    __slots__ = ("writer", "readers", "name")

    def __init__(self, name=""):
        self.writer = None
        self.readers = []
        self.name = name


SERIALIZE = False


class Tracker:

    def __init__(self):
        self.streams = {k: [] for k in ("pe", "dve", "act", "gp", "sync")}
        self.nops = 0
        self.last = {k: None for k in self.streams}
        self.groups = {}
        self.glast = {}

    def op(self, stream, fn, reads=(), writes=(), inc=1, grp=None):
        assert (inc == 16) == (grp is not None), f"grp required iff DMA ({grp})"
        deps = []
        if SERIALIZE:
            deps = [o for o in self.last.values() if o is not None]
        for b in reads:
            if b.writer is not None:
                deps.append(b.writer)
        for b in writes:
            if b.writer is not None:
                deps.append(b.writer)
            deps.extend(b.readers)
        deps = [d for d in deps
                if d.inc == 16 or not (stream == "pe" and d.stream == "pe")]
        o = _Op(stream, fn, inc, deps, self.nops, grp)
        self.nops += 1
        self.streams[stream].append(o)
        self.last[stream] = o
        if grp is not None:
            self.groups.setdefault(grp, 0)
            self.glast[grp] = o
        for b in reads:
            b.readers.append(o)
        for b in writes:
            b.writer = o
            b.readers = []
        return o

    def barrier(self):
        bufs = []
        for s, ops in self.streams.items():
            b = Buf(f"bar_{s}")
            b.writer = ops[-1] if ops else None
            bufs.append(b)
        for g, o in self.glast.items():
            b = Buf(f"bar_{g}")
            b.writer = o
            bufs.append(b)
        return bufs

    def finalize(self, nc, st, block):
        stream_sems = {s: st.enter_context(nc.semaphore(f"s_{s}"))
                       for s in self.streams}
        group_sems = {g: st.enter_context(nc.semaphore(f"sg_{g}"))
                      for g in self.groups}
        for ops in self.streams.values():
            for o in ops:
                for d in o.deps:
                    d.needed = True
        grp_stream = {}
        for s, ops in self.streams.items():
            c = 0
            for o in ops:
                if o.inc == 16:
                    self.groups[o.grp] += 16
                    o.cum = self.groups[o.grp]
                    grp_stream[o.grp] = s
                else:
                    if o.needed:
                        c += 1
                    o.cum = c

        def emit(stream):
            ops = self.streams[stream]
            eng_attr = {"pe": "tensor", "dve": "vector", "act": "scalar",
                        "gp": "gpsimd", "sync": "sync"}[stream]

            def body(eng):
                last_wait = {}
                for o in ops:
                    for d in o.deps:
                        if d.inc == 16:
                            sem, key, v = group_sems[d.grp], ("g", d.grp), d.cum
                        else:
                            if not d.needed:
                                continue
                            sem, key, v = (stream_sems[d.stream], ("s", d.stream),
                                           d.cum)
                        if v <= 0 or last_wait.get(key, -1) >= v:
                            continue
                        eng.wait_ge(sem, v)
                        last_wait[key] = v
                    ins = o.fn(eng)
                    if o.inc == 16:
                        ins.then_inc(group_sems[o.grp], 16)
                    elif o.needed:
                        ins.then_inc(stream_sems[o.stream], 1)
                for g, s2 in grp_stream.items():
                    if s2 == stream and self.groups[g] > 0:
                        eng.wait_ge(group_sems[g], self.groups[g])

            getattr(block, eng_attr)(body)
        for s in ("sync", "gp", "pe", "dve", "act"):
            emit(s)


def make_quant(tr, SAfn, prefix, bufmk, nsets=2, maxw=2816, share_u=None,
               cbuf=None, pipe3=False):
    nst = 3 if pipe3 else nsets
    nxn = 2 if pipe3 else nsets
    nm = 1 if pipe3 else nsets
    sts = [SAfn(f"{prefix}_st{s}", [128, 3 * (maxw // 16)], F32)
           for s in range(nst)]
    bst = [bufmk(f"{prefix}_st{s}") for s in range(nst)]
    xns = [SAfn(f"{prefix}_xn{s}", [128, maxw], BF16) for s in range(nxn)]
    bxn = [bufmk(f"{prefix}_xn{s}") for s in range(nxn)]
    us = [(share_u[0] if share_u else SAfn(f"{prefix}_u{s}", [128, maxw], F32))
          for s in range(nxn)]
    bu = [(share_u[1] if share_u else bufmk(f"{prefix}_u{s}"))
          for s in range(nxn)]
    qls = [SAfn(f"{prefix}_ql{s}", [128, maxw], BF16) for s in range(nxn)]
    bql = [bufmk(f"{prefix}_ql{s}") for s in range(nxn)]
    ms = [SAfn(f"{prefix}_m{s}", [128, maxw], U8) for s in range(nm)]
    bm = [bufmk(f"{prefix}_m{s}") for s in range(nm)]

    def quant_piece(xap, width, rbufs, wbufs, op, sidx, stage=0):
        st_t, st_b = sts[sidx % nst], bst[sidx % nst]
        xn_full, xn_b = xns[sidx % nxn], bxn[sidx % nxn]
        u_full, u_b = us[sidx % nxn], bu[sidx % nxn]
        ql_full, ql_b = qls[sidx % nxn], bql[sidx % nxn]
        m_full, m_b = ms[sidx % nm], bm[sidx % nm]
        nb = width // 16
        nbm = st_t.shape[1] // 3
        amax = st_t[:, 0:nb]
        scl = st_t[:, nbm:nbm + nb]
        inv = st_t[:, 2 * nbm:2 * nbm + nb]
        x3 = xap.rearrange("p (a b) -> p a b", b=16)
        xn = xn_full[:, 0:width]
        u = u_full[:, 0:width]
        ql = ql_full[:, 0:width]
        m = m_full[:, 0:width]
        creads = [cbuf] if cbuf is not None else []
        if stage in (0, "a"):
            op("dve", lambda e: e.tensor_reduce(
                amax, x3, AX.X, ALU.max, apply_absolute_value=True),
                reads=list(rbufs), writes=[st_b])
            op("dve", lambda e: e.tensor_scalar(
                scl, amax, 1.0 / 6.0, 1e-12, ALU.mult, ALU.max),
                reads=[st_b], writes=[st_b])
            op("dve", lambda e: e.reciprocal(inv, scl),
               reads=[st_b], writes=[st_b])
        if stage in (0, "b"):
            op("dve", lambda e: e.tensor_tensor(
                xn.rearrange("p (a b) -> p a b", b=16), x3,
                inv.unsqueeze(2).broadcast_to([128, nb, 16]), ALU.mult),
                reads=list(rbufs) + [st_b], writes=[xn_b])
            op("act", lambda e: e.activation(
                u, xn, AF.Identity, scale=2.0, bias=float(MAGIC)),
                reads=[xn_b] + creads, writes=[u_b])
            op("act", lambda e: e.activation(
                ql, u, AF.Identity, scale=0.5, bias=-float(MAGIC) * 0.5),
                reads=[u_b] + creads, writes=[ql_b])
            op("act", lambda e: e.activation(u, xn, AF.Abs),
               reads=[xn_b], writes=[u_b])
        if stage in (0, "c"):
            op("dve", lambda e: e.tensor_scalar(
                xn.bitcast(I16), xn.bitcast(I16), 32, None, ALU.add),
                reads=[xn_b], writes=[xn_b])
            op("dve", lambda e: e.tensor_scalar(
                xn.bitcast(I16), xn.bitcast(I16), -64, None, ALU.bitwise_and),
                reads=[xn_b], writes=[xn_b])
            op("dve", lambda e: e.tensor_scalar(
                m, u, 0.9, None, ALU.is_lt),
                reads=[u_b], writes=[m_b])
            op("dve", lambda e: e.copy_predicated(xn, m, ql),
                reads=[m_b, ql_b, xn_b], writes=[xn_b])
            op("dve", lambda e: e.tensor_tensor(
                x3, xn.rearrange("p (a b) -> p a b", b=16),
                scl.unsqueeze(2).broadcast_to([128, nb, 16]), ALU.mult),
                reads=[xn_b, st_b], writes=list(wbufs))
    return quant_piece


def _host_consts():
    ident = np.eye(128, dtype=NBF)
    icol4 = np.zeros((128, 512), dtype=NBF)
    for b in range(4):
        icol4[:, b * 128:(b + 1) * 128] = ident
    ones1 = np.ones((1, 128), dtype=NBF)
    ones16 = np.ones((16, 128), dtype=NBF)
    oh4 = np.zeros((16, 64), dtype=NBF)
    for c in range(4):
        for h in range(16):
            oh4[h, c * 16 + h] = 1
    return ident, icol4, ones1, ones16, oh4


def build_nc(debug=False):
    nc = bass.Bass("TRN2", target_bir_lowering=False)
    tr = Tracker()

    x_in = nc.dram_tensor("x", [T, D], F32, kind="ExternalInput").ap()
    wqT = nc.dram_tensor("wqT", [D, D], F32, kind="ExternalInput").ap()
    wkT = nc.dram_tensor("wkT", [D, D], F32, kind="ExternalInput").ap()
    wvT = nc.dram_tensor("wvT", [D, D], F32, kind="ExternalInput").ap()
    woT = nc.dram_tensor("woT", [D, D], F32, kind="ExternalInput").ap()
    wg_raw = nc.dram_tensor("wg_raw", [DI, D], F32, kind="ExternalInput").ap()
    wu_raw = nc.dram_tensor("wu_raw", [DI, D], F32, kind="ExternalInput").ap()
    wd_raw = nc.dram_tensor("wd_raw", [D, DI], F32, kind="ExternalInput").ap()
    gbc = {}
    for nm in ("wgam", "bgam", "wlam", "blam"):
        gbc[nm] = nc.dram_tensor(nm + "_in", [128, 64], F32, kind="ExternalInput").ap()
    c_ident = nc.dram_tensor("c_ident", [128, 128], BF16, kind="ExternalInput").ap()
    c_icol4 = nc.dram_tensor("c_icol4", [128, 512], BF16, kind="ExternalInput").ap()
    c_ones1 = nc.dram_tensor("c_ones1", [1, 128], BF16, kind="ExternalInput").ap()
    c_ones16 = nc.dram_tensor("c_ones16", [16, 128], BF16, kind="ExternalInput").ap()
    c_oh4 = nc.dram_tensor("c_oh4", [16, 64], BF16, kind="ExternalInput").ap()

    out_d = nc.dram_tensor("out", [T, D], F32, kind="ExternalOutput").ap()
    x1_kind = "ExternalOutput" if debug else "Internal"
    x1_d = nc.dram_tensor("x1_d", [T, D], BF16, kind=x1_kind).ap()
    wgq_d = nc.dram_tensor("wgq_d", [D, DI], BF16, kind=x1_kind).ap()
    wuq_d = nc.dram_tensor("wuq_d", [D, DI], BF16, kind="Internal").ap()
    wdq_d = nc.dram_tensor("wdq_d", [DI, D], BF16, kind=x1_kind).ap()
    dbg = {}
    if debug:
        dbg["q_dbg"] = nc.dram_tensor("q_dbg", [128, 8 * 512], BF16, kind="ExternalOutput").ap()
        dbg["k_dbg"] = nc.dram_tensor("k_dbg", [TILE_A, D], BF16, kind="ExternalOutput").ap()
        dbg["y_dbg"] = nc.dram_tensor("y_dbg", [TILE_A, D], BF16, kind="ExternalOutput").ap()
        dbg["rho_dbg"] = nc.dram_tensor("rho_dbg", [16, TILE_A], BF16, kind="ExternalOutput").ap()
        dbg["xt_dbg"] = nc.dram_tensor("xt_dbg", [128, 4 * 1024], BF16, kind="ExternalOutput").ap()
        dbg["xn1F_dbg"] = nc.dram_tensor("xn1F_dbg", [128, 4096], BF16, kind="ExternalOutput").ap()
        dbg["wq_dbg"] = nc.dram_tensor("wq_dbg", [128, 8 * 1024], BF16, kind="ExternalOutput").ap()
        dbg["xn1_dbg"] = nc.dram_tensor("xn1_dbg", [128, 4 * 1024], BF16, kind="ExternalOutput").ap()

    x_r = x_in.rearrange("(a p) f -> p a f", p=128)
    x1_r = x1_d.rearrange("(a p) f -> p a f", p=128)
    out_r = out_d.rearrange("(a p) f -> p a f", p=128)
    wgq_r = wgq_d.rearrange("(dc p) n -> p dc n", p=128)
    wuq_r = wuq_d.rearrange("(dc p) n -> p dc n", p=128)
    wdq_r = wdq_d.rearrange("(nc p) e -> p nc e", p=128)

    from contextlib import ExitStack
    with ExitStack() as _st:
        def SB(name, shape, dtype):
            return _st.enter_context(nc.sbuf_tensor(name, shape, dtype))

        def PS(name, shape, dtype):
            return _st.enter_context(nc.psum_tensor(name, shape, dtype))

        ident = SB("ident", [128, 128], BF16)
        icol4 = SB("icol4", [128, 512], BF16)
        ones1 = SB("ones1", [1, 128], BF16)
        ones16 = SB("ones16", [16, 128], BF16)
        oh4 = SB("oh4", [16, 64], BF16)
        cmagic = SB("cmagic", [128, 1], F32)
        cmagic2 = SB("cmagic2", [128, 1], F32)
        wgam = SB("wgam", [128, 64], F32)
        bgam = SB("bgam", [128, 64], F32)
        wlam = SB("wlam", [128, 64], F32)
        blam = SB("blam", [128, 64], F32)
        ln2st = SB("ln2st", [128, 64], F32)
        pA = PS("pA", [128, 512], F32)
        pB = PS("pB", [128, 512], F32)
        pR1 = PS("pR1", [128, 512], F32)
        pR2 = PS("pR2", [128, 512], F32)
        pPT = PS("pPT", [128, 512], F32)
        pY = PS("pY", [128, 512], F32)
        pM = PS("pM", [128, 512], F32)
        pT = PS("pT", [128, 1024], BF16)
        block = _st.enter_context(nc.Block())
        bx1d = Buf("x1_d")
        bln2 = Buf("ln2st")
        nc.const_aps.register(F32, float(MAGIC), cmagic[:])
        nc.const_aps.register(F32, -float(MAGIC) * 0.5, cmagic2[:])
        bwgq6 = [Buf(f"wgq_d{j}") for j in range(6)]
        bwuq6 = [Buf(f"wuq_d{j}") for j in range(6)]
        bwdq = Buf("wdq_d")
        b_const = Buf("consts")
        BC = {nm: b_const for nm in ("ident", "icol4", "ones1", "ones16", "oh4",
                                     "wgam", "bgam", "wlam", "blam")}
        pbuf = {id(p): Buf(nm) for nm, p in
                [("pA", pA), ("pB", pB), ("pR1", pR1), ("pR2", pR2),
                 ("pPT", pPT), ("pY", pY), ("pM", pM), ("pT", pT)]}

        def P(p):
            return pbuf[id(p)]

        for src, dst, nm in [(c_ident, ident, "ident"), (c_icol4, icol4, "icol4"),
                             (c_ones1, ones1, "ones1"), (c_ones16, ones16, "ones16"),
                             (c_oh4, oh4, "oh4"), (gbc["wgam"], wgam, "wgam"),
                             (gbc["bgam"], bgam, "bgam"), (gbc["wlam"], wlam, "wlam"),
                             (gbc["blam"], blam, "blam")]:
            tr.op("sync", (lambda s_, d_: lambda e: e.dma_start(out=d_[:], in_=s_))(src, dst),
                  writes=[BC[nm]], inc=16, grp="g_const")
        tr.op("dve", lambda e: e.memset(cmagic[:], float(MAGIC)), writes=[BC["wgam"]])
        tr.op("dve", lambda e: e.memset(cmagic2[:], -float(MAGIC) * 0.5),
              writes=[BC["wgam"]])

        with ExitStack() as _stA:
            def SA(name, shape, dtype, **kw):
                return _stA.enter_context(nc.sbuf_tensor(name, shape, dtype, **kw))

            def SA2(name, shape, dtype, **kw):
                return [SA(name + "_0", shape, dtype, **kw),
                        SA(name + "_1", shape, dtype, **kw)]

            wq_sb = SA("wq_sb", [128, 8 * 1024], BF16)
            wk_sb = SA("wk_sb", [128, 8 * 1024], BF16)
            wv_sb = SA("wv_sb", [128, 8 * 1024], BF16)
            wo_sb = SA("wo_sb", [128, 8 * 1024], BF16)
            S_sb = SA("S_sb", [128, 512], BF16)
            xt2 = SA2("xt", [128, 4 * 1024], BF16)
            xn12 = [SA("xn1", [128, 4 * 1024], BF16)] * 2
            xn1F2 = SA2("xn1F", [128, 8 * 512], BF16, is_transpose_dest=True)
            qF2 = [SA("qF", [128, 8 * 512], BF16)] * 2
            vF2 = [SA("vF", [128, 8 * 512], BF16)] * 2
            ktok2 = [SA("ktok", [128, 4 * 1024], BF16)] * 2
            vtok2 = [SA("vtok", [128, 4 * 1024], BF16, is_transpose_dest=True)] * 2
            run42 = [SA("run4", [128, 64], F32)] * 2
            din_T2 = [SA("din_T", [128, 64], F32)] * 2
            dcol2 = [SA("dcol", [128, 64], F32)] * 2
            ya = SA("ya", [128, 4 * 1024], BF16)
            yF = SA("yF", [128, 8 * 512], BF16, is_transpose_dest=True)
            scr = SA("scr", [128, 1024], F32)
            lnst = SA("lnst", [128, 40], F32)
            kst = SA("kst", [128, 6 * 64], F32)
            rho4 = SA("rho4", [128, 64], BF16)
            rho_ht = SA("rho_ht", [16, 512], BF16)
            rho0_ht = SA("rho0_ht", [16, 512], BF16)
            rhoS = SA("rhoS", [16, 512], BF16)
            din_bf = SA("din_bf", [16, 512], BF16)
            rstg = SA("rstg", [16, 64], BF16)
            rho0_sep = [SA("rho0_sep0", [1, 2048], BF16),
                        SA("rho0_sep1", [1, 2048], BF16)]
            E0 = SA("E0", [128, 512], BF16)
            E1 = SA("E1", [128, 512], BF16)
            mtp0 = SA("mtp0", [128, 512], BF16)
            mtp1 = SA("mtp1", [128, 512], BF16)
            ku0 = SA("ku0", [128, 256], BF16)
            ku1 = SA("ku1", [128, 256], BF16)
            vw0 = SA("vw0", [128, 256], BF16)
            vw1 = SA("vw1", [128, 256], BF16)
            vws = SA("vws", [128, 16], F32)
            y2s0 = SA("y2s0", [128, 256], F32)
            y2s1 = SA("y2s1", [128, 256], F32)
            wtile2 = [SA("wtile0", [128, 1408], BF16),
                      SA("wtile1", [128, 1408], BF16)]
            wttr = SA("wttr", [128, 11 * 128], BF16, is_transpose_dest=True)

            singles = ("S", "ya", "yF", "scr", "lnst", "kst", "rho4", "w4",
                       "rho_ht", "rho0_ht", "rhoS", "din_bf",
                       "rstg", "rho0_sep0", "rho0_sep1", "E0", "E1",
                       "mtp0", "mtp1",
                       "ku0", "ku1", "vw0", "vw1", "vws", "y2s0", "y2s1",
                       "wtile0", "wtile1", "wttr")
            bw = {nm: Buf(nm) for nm in singles}
            doubles = ("xt", "xn1", "xn1F", "qF", "vF", "ktok", "vtok",
                       "run4", "din_T", "dcol")
            real_doubles = ("xt", "xn1F")
            bw2 = {nm: ([Buf(nm + "_0"), Buf(nm + "_1")] if nm in real_doubles
                        else [Buf(nm)] * 2) for nm in doubles}

            for src_, dst in [(wqT, wq_sb), (wkT, wk_sb), (wvT, wv_sb), (woT, wo_sb)]:
                tr.op("gp", (lambda s_, d_: lambda e: e.dma_start(
                    out=d_[:].rearrange("p (dc e) -> p dc e", dc=8),
                    in_=s_.rearrange("(dc p) e -> p dc e", p=128)))(src_, dst),
                    writes=[bw["w4"]], inc=16, grp="g_attnw")
            tr.op("dve", lambda e: e.memset(S_sb[:], 0.0), writes=[bw["S"]])
            tr.op("dve", lambda e: e.memset(rhoS[:], 0.0), writes=[bw["rhoS"]])

            ya3 = ya[:].rearrange("p (a f) -> p a f", a=4)
            yFb = yF[:].rearrange("p (cc dc f) -> p cc dc f", cc=4, dc=8)
            wsb3 = {nm: w[:].rearrange("p (dc e) -> p dc e", dc=8)
                    for nm, w in [("wq", wq_sb), ("wk", wk_sb),
                                  ("wv", wv_sb), ("wo", wo_sb)]}

            HGRP = [[0, 2, 4, 6], [8, 10, 12, 14],
                    [1, 3, 5, 7], [9, 11, 13, 15]]
            GBASE = [0, 0, 64, 64]
            GOFF = [0, 8, 1, 9]

            def Ssl(h):
                return S_sb[(h % 2) * 64:(h % 2) * 64 + 64,
                            (h // 2) * 64:(h // 2) * 64 + 64]

            def Sgq(g):
                return S_sb[GBASE[g]:GBASE[g] + 64,
                            (g % 2) * 256:(g % 2) * 256 + 256]

            def hsel4(ap16, g):
                st_ = GOFF[g] % 2
                lo = (GOFF[g] % 8) // 2 + (GOFF[g] // 8) * 4
                lo = 4 if GOFF[g] >= 8 else 0
                return ap16[:, st_::2, :][:, lo:lo + 4, :]

            def hslice(buf3, h, cols):
                return buf3[(h % 2) * 64:(h % 2) * 64 + 64, h // 2, cols]

            class TV:

                def __init__(self, i):
                    b = i % 2
                    self.i = i
                    self.xt3 = xt2[b][:].rearrange("p (a f) -> p a f", a=4)
                    self.xn13 = xn12[b][:].rearrange("p (a f) -> p a f", a=4)
                    self.xn1 = xn12[b]
                    self.xn1F = xn1F2[b]
                    self.xn1Fb = xn1F2[b][:].rearrange(
                        "p (tt dc f) -> p tt dc f", tt=4, dc=8)
                    self.qF = qF2[b]
                    self.vF = vF2[b]
                    self.qF3 = qF2[b][:].rearrange("p (a f) -> p a f", a=8)
                    self.vF3 = vF2[b][:].rearrange("p (a f) -> p a f", a=8)
                    self.ktok = ktok2[b]
                    self.ktok3 = ktok2[b][:].rearrange("p (a f) -> p a f", a=4)
                    self.vtok = vtok2[b]
                    self.vtokb = vtok2[b][:].rearrange(
                        "p (ec tt f) -> p ec tt f", ec=8, tt=4)
                    self.run4 = run42[b]
                    self.din_T = din_T2[b]
                    self.dcol = dcol2[b]
                    for nm in doubles:
                        setattr(self, "b_" + nm, bw2[nm][b])

            def prep_a(i):
                v = TV(i)
                tr.op("gp", (lambda t_, v_: lambda e: e.dma_start(
                    out=v_.xt3, in_=x_r[:, t_ * 4:(t_ + 1) * 4, :]))(i, v),
                    writes=[v.b_xt], inc=16, grp="g_xt%d" % (i % 2))
                for tt in range(4):
                    tr.op("act", (lambda tt_, v_: lambda e: e.activation(
                        scr[:], v_.xt3[:, tt_, :], AF.Square,
                        accum_out=lnst[:, tt_:tt_ + 1]))(tt, v),
                        reads=[v.b_xt], writes=[bw["scr"], bw["lnst"]])
                    tr.op("dve", (lambda tt_, v_: lambda e: e.reduce_sum(
                        lnst[:, 4 + tt_:5 + tt_], v_.xt3[:, tt_, :], AX.X))(tt, v),
                        reads=[v.b_xt], writes=[bw["lnst"]])
                tr.op("dve", lambda e: e.tensor_scalar(
                    lnst[:, 8:12], lnst[:, 4:8], 1.0 / D, None, ALU.mult),
                    reads=[bw["lnst"]], writes=[bw["lnst"]])
                tr.op("dve", lambda e: e.tensor_scalar(
                    lnst[:, 12:16], lnst[:, 0:4], 1.0 / D, EPS_LN, ALU.mult, ALU.add),
                    reads=[bw["lnst"]], writes=[bw["lnst"]])
                tr.op("dve", lambda e: e.tensor_tensor(
                    lnst[:, 16:20], lnst[:, 8:12], lnst[:, 8:12], ALU.mult),
                    reads=[bw["lnst"]], writes=[bw["lnst"]])
                tr.op("dve", lambda e: e.tensor_tensor(
                    lnst[:, 20:24], lnst[:, 12:16], lnst[:, 16:20], ALU.subtract),
                    reads=[bw["lnst"]], writes=[bw["lnst"]])
                tr.op("act", lambda e: e.activation(
                    lnst[:, 24:28], lnst[:, 20:24], AF.Ln),
                    reads=[bw["lnst"]], writes=[bw["lnst"]])
                tr.op("act", lambda e: e.activation(
                    lnst[:, 28:32], lnst[:, 24:28], AF.Exp, scale=-0.5),
                    reads=[bw["lnst"]], writes=[bw["lnst"]])
                tr.op("dve", lambda e: e.tensor_tensor(
                    lnst[:, 32:36], lnst[:, 8:12], lnst[:, 28:32], ALU.mult),
                    reads=[bw["lnst"]], writes=[bw["lnst"]])
                tr.op("dve", lambda e: e.tensor_scalar(
                    lnst[:, 36:40], lnst[:, 32:36], -1.0, None, ALU.mult),
                    reads=[bw["lnst"]], writes=[bw["lnst"]])
                for tt in range(4):
                    tr.op("dve", (lambda tt_, v_: lambda e: e.tensor_scalar(
                        v_.xn13[:, tt_, :], v_.xt3[:, tt_, :],
                        lnst[:, 28 + tt_:29 + tt_], lnst[:, 36 + tt_:37 + tt_],
                        ALU.mult, ALU.add))(tt, v),
                        reads=[v.b_xt, bw["lnst"]], writes=[v.b_xn1])
                tr.op("sync", (lambda v_: lambda e: e.dma_start(
                    out=v_.xn1F[:].rearrange("p (a f) -> p a f", a=32),
                    in_=v_.xn1[:].rearrange("p (a f) -> p a f", a=32),
                    transpose=True))(v),
                    reads=[v.b_xn1], writes=[v.b_xn1F], inc=16,
                    grp="g_xn1F%d" % (i % 2))

            def prep_b(i):
                v = TV(i)
                for dst3, dstb, wnm in [(v.qF3, v.b_qF, "wq"), (v.vF3, v.b_vF, "wv")]:
                    for ec in range(8):
                        ps = pA if ec % 2 == 0 else pB

                        def mk(ec_, ps_, wnm_, v_):
                            def f(e):
                                ins = None
                                for dc in range(8):
                                    ins = e.matmul(
                                        ps_[:],
                                        wsb3[wnm_][:, dc, ec_ * 128:(ec_ + 1) * 128],
                                        v_.xn1Fb[:, :, dc, :],
                                        start=(dc == 0), stop=(dc == 7))
                                return ins
                            return f
                        tr.op("pe", mk(ec, ps, wnm, v),
                              reads=[bw["w4"], v.b_xn1F], writes=[P(ps)])
                        tr.op("act", (lambda ec_, ps_, d3_: lambda e: e.copy(
                            d3_[:, ec_, :], ps_[:]))(ec, ps, dst3),
                            reads=[P(ps)], writes=[dstb])
                for tt in range(4):
                    for eh in range(2):
                        ps = pA if eh == 0 else pB

                        def mkk(tt_, eh_, ps_, v_):
                            def f(e):
                                ins = None
                                for dc in range(8):
                                    ins = e.matmul(
                                        ps_[:], v_.xn1Fb[:, tt_, dc, :],
                                        wsb3["wk"][:, dc, eh_ * 512:(eh_ + 1) * 512],
                                        start=(dc == 0), stop=(dc == 7))
                                return ins
                            return f
                        tr.op("pe", mkk(tt, eh, ps, v),
                              reads=[bw["w4"], v.b_xn1F], writes=[P(ps)])
                        tr.op("act", (lambda tt_, eh_, ps_, v_: lambda e: e.copy(
                            v_.ktok3[:, tt_, eh_ * 512:(eh_ + 1) * 512], ps_[:]))(
                                tt, eh, ps, v),
                            reads=[P(ps)], writes=[v.b_ktok])
                tr.op("sync", (lambda v_: lambda e: e.dma_start(
                    out=v_.vtok[:].rearrange("p (a f) -> p a f", a=32),
                    in_=v_.vF[:].rearrange("p (a f) -> p a f", a=32),
                    transpose=True))(v),
                    reads=[v.b_vF], writes=[v.b_vtok], inc=16,
                    grp="g_vtok%d" % (i % 2))
                for tt in range(4):
                    tr.op("act", (lambda tt_, v_: lambda e: e.activation(
                        scr[:], v_.ktok3[:, tt_, :], AF.Square))(tt, v),
                        reads=[v.b_ktok], writes=[bw["scr"]])
                    tr.op("dve", (lambda tt_: lambda e: e.tensor_reduce(
                        kst[:, tt_ * 16:(tt_ + 1) * 16],
                        scr[:].rearrange("p (a b) -> p a b", b=64),
                        AX.X, ALU.add))(tt),
                        reads=[bw["scr"]], writes=[bw["kst"]])
                    tr.op("dve", (lambda tt_, v_: lambda e: e.tensor_reduce(
                        kst[:, 64 + tt_ * 16:64 + (tt_ + 1) * 16],
                        v_.ktok3[:, tt_, :].rearrange("p (a b) -> p a b", b=64),
                        AX.X, ALU.add))(tt, v),
                        reads=[v.b_ktok], writes=[bw["kst"]])
                tr.op("dve", lambda e: e.tensor_scalar(
                    kst[:, 128:192], kst[:, 0:64], 1e-24, None, ALU.max),
                    reads=[bw["kst"]], writes=[bw["kst"]])
                tr.op("act", lambda e: e.activation(
                    kst[:, 128:192], kst[:, 128:192], AF.Ln),
                    reads=[bw["kst"]], writes=[bw["kst"]])
                tr.op("act", lambda e: e.activation(
                    kst[:, 128:192], kst[:, 128:192], AF.Exp, scale=-0.5),
                    reads=[bw["kst"]], writes=[bw["kst"]])
                tr.op("dve", lambda e: e.tensor_tensor(
                    kst[:, 192:256], kst[:, 64:128], kst[:, 128:192], ALU.mult),
                    reads=[bw["kst"]], writes=[bw["kst"]])
                tr.op("dve", lambda e: e.scalar_tensor_tensor(
                    kst[:, 256:320], kst[:, 192:256], 1.0 / DK, wgam[:],
                    ALU.mult, ALU.mult),
                    reads=[bw["kst"], BC["wgam"]], writes=[bw["kst"]])
                tr.op("dve", lambda e: e.tensor_tensor(
                    kst[:, 256:320], kst[:, 256:320], bgam[:], ALU.add),
                    reads=[bw["kst"], BC["bgam"]], writes=[bw["kst"]])
                tr.op("act", lambda e: e.activation(
                    kst[:, 320:384], kst[:, 256:320], AF.Sigmoid),
                    reads=[bw["kst"]], writes=[bw["kst"]])
                tr.op("dve", lambda e: e.scalar_tensor_tensor(
                    kst[:, 256:320], kst[:, 192:256], 1.0 / DK, wlam[:],
                    ALU.mult, ALU.mult),
                    reads=[bw["kst"], BC["wlam"]], writes=[bw["kst"]])
                tr.op("dve", lambda e: e.tensor_tensor(
                    kst[:, 256:320], kst[:, 256:320], blam[:], ALU.add),
                    reads=[bw["kst"], BC["blam"]], writes=[bw["kst"]])
                tr.op("act", lambda e: e.activation(
                    kst[:, 256:320], kst[:, 256:320], AF.Sigmoid),
                    reads=[bw["kst"]], writes=[bw["kst"]])
                tr.op("dve", lambda e: e.tensor_scalar(
                    kst[:, 256:320], kst[:, 256:320], -1.0, 1.0, ALU.mult, ALU.add),
                    reads=[bw["kst"]], writes=[bw["kst"]])
                tr.op("dve", lambda e: e.tensor_tensor(
                    rho4[:], kst[:, 256:320], kst[:, 320:384], ALU.mult),
                    reads=[bw["kst"]], writes=[bw["rho4"]])
                tr.op("dve", (lambda v_: lambda e: e.tensor_scalar(
                    v_.run4[:], rho4[:], -1.0, 1.0, ALU.mult, ALU.add))(v),
                    reads=[bw["rho4"]], writes=[v.b_run4])
                tr.op("dve", (lambda v_: lambda e: e.tensor_tensor(
                    v_.run4[:], v_.run4[:], kst[:, 128:192], ALU.mult))(v),
                    reads=[v.b_run4, bw["kst"]], writes=[v.b_run4])
                for tt in range(4):
                    tr.op("pe", (lambda tt_: lambda e: e.transpose(
                        pT[0:16, 0:128], rho4[:, tt_ * 16:(tt_ + 1) * 16], ident[:]))(tt),
                        reads=[bw["rho4"], BC["ident"]], writes=[P(pT)])
                    tr.op("act", (lambda tt_: lambda e: e.copy(
                        rho_ht[:, tt_ * 128:(tt_ + 1) * 128], pT[0:16, 0:128]))(tt),
                        reads=[P(pT)], writes=[bw["rho_ht"]])
                tr.op("dve", lambda e: e.tensor_copy(rho0_ht[:], rho_ht[:]),
                      reads=[bw["rho_ht"]], writes=[bw["rho0_ht"]])
                tr.op("dve", lambda e: e.memset(
                    rho0_ht[:].rearrange("p (a f) -> p a f", a=4)[:, :, 0:1], 0.0),
                    writes=[bw["rho0_ht"]])
                tr.op("dve", lambda e: e.tensor_copy(
                    rhoS[:].rearrange("p (a f) -> p a f", a=4)[:, :, 0:1],
                    rho_ht[:].rearrange("p (a f) -> p a f", a=4)[:, :, 0:1]),
                    reads=[bw["rho_ht"]], writes=[bw["rhoS"]])
                tr.op("dve", lambda e: e.tensor_tensor_scan(
                    din_bf[:], rho0_ht[:], rhoS[:], 0.0, ALU.mult, ALU.add),
                    reads=[bw["rho0_ht"], bw["rhoS"]], writes=[bw["din_bf"]])
                for tt in range(4):
                    tr.op("pe", (lambda tt_: lambda e: e.transpose(
                        pT[0:128, 128:144], din_bf[:, tt_ * 128:(tt_ + 1) * 128],
                        ident[0:16, 0:16]))(tt),
                        reads=[bw["din_bf"], BC["ident"]], writes=[P(pT)])
                    tr.op("act", (lambda tt_, v_: lambda e: e.copy(
                        v_.din_T[:, tt_ * 16:(tt_ + 1) * 16], pT[0:128, 128:144]))(
                            tt, v),
                        reads=[P(pT)], writes=[v.b_din_T])
                tr.op("dve", lambda e: e.tensor_tensor(
                    rstg[:].rearrange("p (c j) -> p c j", c=4),
                    oh4[:].rearrange("p (c j) -> p c j", c=4),
                    din_bf[:, 127::128].unsqueeze(2).broadcast_to([16, 4, 16]),
                    ALU.mult),
                    reads=[bw["din_bf"], BC["oh4"]], writes=[bw["rstg"]])
                tr.op("pe", lambda e: e.matmul(
                    pM[0:128, 256:320], ones16[:], rstg[:], start=True, stop=True),
                    reads=[BC["ones16"], bw["rstg"]], writes=[P(pM)])
                tr.op("dve", (lambda v_: lambda e: e.tensor_copy(
                    v_.dcol[:], pM[0:128, 256:320]))(v),
                    reads=[P(pM)], writes=[v.b_dcol])
                if debug and i == 0:
                    tr.op("sync", lambda e: e.dma_start(
                        out=dbg["rho_dbg"], in_=rho_ht[:]),
                        reads=[bw["rho_ht"]], inc=16, grp="g_dbg0")
                    tr.op("sync", (lambda v_: lambda e: e.dma_start(
                        out=dbg["xt_dbg"], in_=v_.xt3[:, :, :].rearrange(
                            "p a f -> p (a f)") if False else xt2[0][:]))(v),
                        reads=[v.b_xt], inc=16, grp="g_dbg1")
                    tr.op("sync", lambda e: e.dma_start(
                        out=dbg["xn1_dbg"], in_=xn12[0][:]),
                        reads=[v.b_xn1], inc=16, grp="g_dbg2")
                    tr.op("sync", lambda e: e.dma_start(
                        out=dbg["xn1F_dbg"], in_=xn1F2[0][:]),
                        reads=[v.b_xn1F], inc=16, grp="g_dbg6")
                    tr.op("sync", lambda e: e.dma_start(
                        out=dbg["wq_dbg"], in_=wq_sb[:]),
                        reads=[bw["w4"]], inc=16, grp="g_dbg7")
                    tr.op("sync", lambda e: e.dma_start(
                        out=dbg["q_dbg"], in_=qF2[0][:]),
                        reads=[v.b_qF], inc=16, grp="g_dbg3")
                    tr.op("sync", (lambda v_: lambda e: e.dma_start(
                        out=dbg["k_dbg"].rearrange("(a p) f -> p a f", p=128),
                        in_=v_.ktok3))(v),
                        reads=[v.b_ktok], inc=16, grp="g_dbg4")

            def chunks(i, mid=None):
                v = TV(i)
                def emit_bc_scan(cc, g):
                    sep = rho0_sep[cc % 2]
                    sepb = bw["rho0_sep%d" % (cc % 2)]
                    if g == 0:
                        tr.op("sync", (lambda cc_, sep_: lambda e: e.dma_start(
                            out=sep_[0:1, :].rearrange("q (h t) -> q h t", h=16),
                            in_=rho0_ht[:, cc_ * 128:(cc_ + 1) * 128]))(cc, sep),
                            reads=[bw["rho0_ht"]], writes=[sepb], inc=16,
                            grp="g_rsep%d" % (cc % 2))
                    pRho = pR1 if g % 2 == 0 else pR2
                    E_sb = E0 if g % 2 == 0 else E1
                    Ebuf = bw["E0"] if g % 2 == 0 else bw["E1"]

                    def mk_bc(g_, pRho_, sep_):
                        def f(e):
                            rhs = hsel4(sep_[0:1, :].rearrange(
                                "q (hh t) -> q hh t", hh=16), g_)
                            return e.matmul(pRho_[:], ones1[:], rhs,
                                            start=True, stop=True)
                        return f
                    tr.op("pe", mk_bc(g, pRho, sep),
                          reads=[BC["ones1"], sepb], writes=[P(pRho)])
                    tr.op("dve", (lambda E_, pRho_: lambda e: e.tensor_tensor_scan(
                        E_[:], pRho_[:], icol4[:], 0.0, ALU.mult, ALU.add))(
                            E_sb, pRho),
                        reads=[P(pRho), BC["icol4"]], writes=[Ebuf])

                emit_bc_scan(0, 0)
                for cc in range(4):
                    if cc == 2 and mid is not None:
                        mid()
                    for g in range(4):
                        if g < 3:
                            emit_bc_scan(cc, g + 1)
                        elif cc < 3:
                            emit_bc_scan(cc + 1, 0)
                        heads = HGRP[g]
                        E_sb = E0 if g % 2 == 0 else E1
                        Ebuf = bw["E0"] if g % 2 == 0 else bw["E1"]
                        mtp = mtp0 if g % 2 == 0 else mtp1
                        Mbuf = bw["mtp0"] if g % 2 == 0 else bw["mtp1"]
                        ku = ku0 if g % 2 == 0 else ku1
                        Kbuf = bw["ku0"] if g % 2 == 0 else bw["ku1"]
                        vw = vw0 if g % 2 == 0 else vw1
                        Vbuf = bw["vw0"] if g % 2 == 0 else bw["vw1"]
                        y2s = y2s0 if g % 2 == 0 else y2s1
                        Ybuf = bw["y2s0"] if g % 2 == 0 else bw["y2s1"]
                        pPTg = pPT if g % 2 == 0 else pB
                        pYg = pY if g % 2 == 0 else pA

                        def mk_pt(cc_, heads_, v_, pPT_):
                            def f(e):
                                ins = None
                                for j, h in enumerate(heads_):
                                    cols = slice(cc_ * 128, (cc_ + 1) * 128)
                                    ins = e.matmul(
                                        pPT_[:, j * 128:(j + 1) * 128],
                                        hslice(v_.vF3, h, cols),
                                        hslice(v_.qF3, h, cols),
                                        start=True, stop=True)
                                return ins
                            return f
                        tr.op("pe", mk_pt(cc, heads, v, pPTg),
                              reads=[v.b_vF, v.b_qF], writes=[P(pPTg)])
                        tr.op("dve", (lambda mtp_, E_, pPT_: lambda e: e.tensor_tensor(
                            mtp_[:], E_[:], pPT_[:], ALU.mult))(mtp, E_sb, pPTg),
                            reads=[Ebuf, P(pPTg)], writes=[Mbuf])
                        kt4 = hsel4(v.ktok3[:, cc, :].rearrange(
                            "p (hh w) -> p hh w", hh=16), g)
                        run_q = hsel4(v.run4[:].rearrange(
                            "p (c hh) -> p c hh", c=4)[:, cc, :].unsqueeze(2), g)
                        tr.op("dve", (lambda ku_, kt4_, rq_: lambda e: e.tensor_tensor(
                            ku_[:].rearrange("p (j w) -> p j w", j=4), kt4_,
                            rq_.broadcast_to([128, 4, 64]), ALU.mult))(ku, kt4, run_q),
                            reads=[v.b_ktok, v.b_run4], writes=[Kbuf])

                        def mk_vws(E_, rq_):
                            def f(e):
                                el = E_[:].rearrange(
                                    "p (j t) -> p j t", j=4)[:, :, 127:128]
                                return e.tensor_tensor(
                                    vws[:, 0:4].unsqueeze(2), el, rq_, ALU.mult)
                            return f
                        tr.op("dve", mk_vws(E_sb, run_q),
                              reads=[Ebuf, v.b_run4], writes=[bw["vws"]])

                        def mk_vw(cc_, g_, vw_, v_):
                            def f(e):
                                ec0 = (GOFF[g_] % 8) // 2 + (GOFF[g_] // 8) * 4
                                woff = (GOFF[g_] % 2) * 64
                                vt4 = v_.vtokb[:, ec0:ec0 + 4, cc_, woff:woff + 64]
                                return e.tensor_tensor(
                                    vw_[:].rearrange("p (j w) -> p j w", j=4), vt4,
                                    vws[:, 0:4].unsqueeze(2).broadcast_to(
                                        [128, 4, 64]), ALU.mult)
                            return f
                        tr.op("dve", mk_vw(cc, g, vw, v),
                              reads=[v.b_vtok, bw["vws"]], writes=[Vbuf])

                        def mk_y(cc_, g_, heads_, mtp_, ku_, v_, pY_):
                            def f(e):
                                ins = None
                                for j, h in enumerate(heads_):
                                    cols = slice(cc_ * 128, (cc_ + 1) * 128)
                                    e.matmul(pY_[:, j * 64:(j + 1) * 64],
                                             mtp_[:, j * 128:(j + 1) * 128],
                                             ku_[:, j * 64:(j + 1) * 64],
                                             start=True, stop=True)
                                    ins = e.matmul(
                                        pY_[:, 256 + j * 64:256 + (j + 1) * 64],
                                        hslice(v_.qF3, h, cols), Ssl(h),
                                        start=True, stop=True)
                                return ins
                            return f
                        tr.op("pe", mk_y(cc, g, heads, mtp, ku, v, pYg),
                              reads=[Mbuf, Kbuf, v.b_qF, bw["S"]], writes=[P(pYg)])
                        def mk_y2s(cc_, g_, y2s_, v_, pY_):
                            def f(e):
                                dq = hsel4(v_.din_T[:].rearrange(
                                    "p (c hh) -> p c hh", c=4)[:, cc_, :]
                                    .unsqueeze(2), g_)
                                return e.tensor_tensor(
                                    y2s_[:].rearrange("p (j w) -> p j w", j=4),
                                    pY_[:, 256:512].rearrange(
                                        "p (j w) -> p j w", j=4),
                                    dq.broadcast_to([128, 4, 64]), ALU.mult)
                            return f
                        tr.op("dve", mk_y2s(cc, g, y2s, v, pYg),
                              reads=[P(pYg), v.b_din_T], writes=[Ybuf])

                        def mk_ya(cc_, g_, y2s_, pY_):
                            def f(e):
                                out = hsel4(ya3[:, cc_, :].rearrange(
                                    "p (hh w) -> p hh w", hh=16), g_)
                                return e.tensor_tensor(
                                    out, pY_[0:128, 0:256].rearrange(
                                        "p (j w) -> p j w", j=4),
                                    y2s_[:].rearrange("p (j w) -> p j w", j=4),
                                    ALU.add)
                            return f
                        tr.op("dve", mk_ya(cc, g, y2s, pYg),
                              reads=[P(pYg), Ybuf], writes=[bw["ya"]])

                        def mk_dl(cc_, g_, heads_, vw_, v_):
                            def f(e):
                                ins = None
                                rb_ = GBASE[g_]
                                tp = (0, 64) if rb_ else None
                                for j, h in enumerate(heads_):
                                    ins = e.matmul(
                                        pM[rb_:rb_ + 64, j * 64:(j + 1) * 64],
                                        vw_[:, j * 64:(j + 1) * 64],
                                        v_.ktok3[:, cc_, h * 64:(h + 1) * 64],
                                        start=True, stop=True, tile_position=tp)
                                return ins
                            return f
                        tr.op("pe", mk_dl(cc, g, heads, vw, v),
                              reads=[Vbuf, v.b_ktok], writes=[P(pM)])

                        def mk_st1(cc_, g_, v_):
                            def f(e):
                                rb_ = GBASE[g_]
                                sg = Sgq(g_)
                                dq = hsel4(v_.dcol[rb_:rb_ + 64, :].rearrange(
                                    "p (c hh) -> p c hh", c=4)[:, cc_, :].unsqueeze(2),
                                    g_)
                                return e.tensor_tensor(
                                    sg.rearrange("p (j w) -> p j w", j=4),
                                    sg.rearrange("p (j w) -> p j w", j=4),
                                    dq.broadcast_to([64, 4, 64]), ALU.mult)
                            return f

                        def mk_st2(g_):
                            def f(e):
                                rb_ = GBASE[g_]
                                return e.tensor_tensor(
                                    Sgq(g_), Sgq(g_), pM[rb_:rb_ + 64, 0:256],
                                    ALU.add)
                            return f
                        tr.op("dve", mk_st1(cc, g, v),
                              reads=[v.b_dcol, bw["S"]], writes=[bw["S"]])
                        tr.op("dve", mk_st2(g),
                              reads=[P(pM), bw["S"]], writes=[bw["S"]])
                        _emit_wq(1)
                        _emit_ln2(1)
                if debug and i == 0:
                    tr.op("sync", lambda e: e.dma_start(
                        out=dbg["y_dbg"].rearrange("(a p) f -> p a f", p=128),
                        in_=ya3),
                        reads=[bw["ya"]], inc=16, grp="g_dbg5")

            def finish(i):
                v = TV(i)
                tr.op("sync", lambda e: e.dma_start(
                    out=yF[:].rearrange("p (a f) -> p a f", a=32),
                    in_=ya[:].rearrange("p (a f) -> p a f", a=32),
                    transpose=True),
                    reads=[bw["ya"]], writes=[bw["yF"]], inc=16, grp="g_yF")
                for tt in range(4):
                    for eh in range(2):
                        ps = pA if eh == 0 else pB

                        def mko(tt_, eh_, ps_):
                            def f(e):
                                ins = None
                                for dc in range(8):
                                    ins = e.matmul(
                                        ps_[:], yFb[:, tt_, dc, :],
                                        wsb3["wo"][:, dc, eh_ * 512:(eh_ + 1) * 512],
                                        start=(dc == 0), stop=(dc == 7))
                                return ins
                            return f
                        tr.op("pe", mko(tt, eh, ps),
                              reads=[bw["w4"], bw["yF"]], writes=[P(ps)])
                        tr.op("dve", (lambda tt_, eh_, ps_, v_: lambda e: e.tensor_tensor(
                            v_.xt3[:, tt_, eh_ * 512:(eh_ + 1) * 512], ps_[:],
                            v_.xt3[:, tt_, eh_ * 512:(eh_ + 1) * 512], ALU.add))(
                                tt, eh, ps, v),
                            reads=[P(ps), v.b_xt], writes=[v.b_xt])
                tr.op("sync", (lambda t_, v_: lambda e: e.dma_start(
                    out=x1_r[:, t_ * 4:(t_ + 1) * 4, :], in_=v_.xt3))(i, v),
                    reads=[v.b_xt], writes=[bx1d], inc=16, grp="g_x1d")

            _ln2_pend = []

            def _emit_ln2(n):
                while n > 0 and _ln2_pend:
                    _ln2_pend.pop(0)()
                    n -= 1

            def ln2stats(i):
                v = TV(i)
                for tt in range(4):
                    a = i * 4 + tt

                    def s1(tt_=tt, a_=a, v_=v):
                        tr.op("act", lambda e: e.activation(
                            scr[:], v_.xt3[:, tt_, :], AF.Square,
                            accum_out=ln2st[:, a_:a_ + 1]),
                            reads=[v_.b_xt], writes=[bw["scr"], bln2])

                    def s2(tt_=tt, a_=a, v_=v):
                        tr.op("act", lambda e: e.activation(
                            scr[:], v_.xt3[:, tt_, :], AF.Identity,
                            accum_out=ln2st[:, 32 + a_:33 + a_]),
                            reads=[v_.b_xt], writes=[bw["scr"], bln2])
                    _ln2_pend.append(s1)
                    _ln2_pend.append(s2)

            quantA = make_quant(tr, SA, "qa", Buf, nsets=2, maxw=704,
                                share_u=None, cbuf=BC["wgam"], pipe3=True)
            _wq_raw_jobs = []

            def _mk_wq(raw_r, dst_r, nrows, width, nblk_tr, dram_buf0, lst):
                nhalf = 2 if width == 2816 else 1
                hw_ = width // nhalf
                pw = 704 if width == 2816 else 512
                nbh = nblk_tr // nhalf
                for rt in range(nrows):
                    if dram_buf0 is None:
                        dram_buf = (bwgq6 if dst_r is wgq_r else bwuq6)[
                            min(rt // 4, 5)]
                    else:
                        dram_buf = dram_buf0
                    for hh in range(nhalf):
                        lst.append(dict(rt=rt, hh=hh, raw_r=raw_r, dst_r=dst_r,
                                        hw=hw_, nbh=nbh, dram_buf=dram_buf,
                                        pw=pw))

            _jg, _ju, _jd = [], [], []
            _mk_wq(wg_raw, wgq_r, 22, 1024, 8, None, _jg)
            _mk_wq(wu_raw, wuq_r, 22, 1024, 8, None, _ju)
            _mk_wq(wd_raw, wdq_r, 8, 2816, 22, bwdq, _jd)
            _jobs = []
            for a, b2 in zip(_jg, _ju):
                _jobs.append(a)
                _jobs.append(b2)
            _jobs.extend(_jd)

            def _wq_load(jj):
                j = _jobs[jj]
                wt = wtile2[jj % 2]
                wtb = bw["wtile%d" % (jj % 2)]
                tr.op("gp", (lambda j_, wt_: lambda e: e.dma_start(
                    out=wt_[:, 0:j_["hw"]],
                    in_=j_["raw_r"][j_["rt"] * 128:(j_["rt"] + 1) * 128,
                                    j_["hh"] * j_["hw"]:(j_["hh"] + 1) * j_["hw"]]))(
                        j, wt),
                    writes=[wtb], inc=16, grp="g_wtile%d" % (jj % 2))

            def _wq_piece(jj, pc, stage):
                j = _jobs[jj]
                wt = wtile2[jj % 2]
                wtb = bw["wtile%d" % (jj % 2)]
                quantA(wt[:, pc * j["pw"]:(pc + 1) * j["pw"]], j["pw"],
                       [wtb], [wtb], tr.op, 2 * jj + pc, stage=stage)

            def _wq_store(jj):
                j = _jobs[jj]
                wt = wtile2[jj % 2]
                wtb = bw["wtile%d" % (jj % 2)]
                nbh_ = j["nbh"]
                tr.op("sync", (lambda j_, wt_, n_: lambda e: e.dma_start(
                    out=wttr[:, 0:n_ * 128].rearrange("p (a f) -> p a f", a=n_),
                    in_=wt_[:, 0:j_["hw"]].rearrange("p (a f) -> p a f", a=n_),
                    transpose=True))(j, wt, nbh_),
                    reads=[wtb], writes=[bw["wttr"]], inc=16, grp="g_wttr")
                tr.op("gp", (lambda j_, n_: lambda e: e.dma_start(
                    out=j_["dst_r"][:, j_["hh"] * n_:(j_["hh"] + 1) * n_,
                                    j_["rt"] * 128:(j_["rt"] + 1) * 128],
                    in_=wttr[:, 0:n_ * 128].rearrange("p (a f) -> p a f", a=n_)))(
                        j, nbh_),
                    reads=[bw["wttr"]], writes=[j["dram_buf"]], inc=16,
                    grp="g_wscr")

            NP = 2 * len(_jobs)
            _wqs = {"s": 0}

            def _emit_wq(nslots):
                for _ in range(nslots):
                    s = _wqs["s"]
                    if s > NP + 2:
                        return
                    _wqs["s"] += 1
                    if s == 0:
                        _wq_load(0)
                        _wq_load(1)
                        continue
                    p = s - 3
                    if 0 <= p < NP:
                        _wq_piece(p // 2, p % 2, "c")
                        if p % 2 == 1:
                            _wq_store(p // 2)
                            if p // 2 + 2 < len(_jobs):
                                _wq_load(p // 2 + 2)
                    p = s - 2
                    if 0 <= p < NP:
                        _wq_piece(p // 2, p % 2, "b")
                    p = s - 1
                    if 0 <= p < NP:
                        _wq_piece(p // 2, p % 2, "a")

            prep_a(0)
            prep_b(0)
            for i in range(NT_A):
                mid = (lambda i_=i: prep_a(i_ + 1)) if i + 1 < NT_A else None
                chunks(i, mid=mid)
                finish(i)
                if i + 1 < NT_A:
                    prep_b(i + 1)
                ln2stats(i)
            _emit_ln2(999)
            _emit_wq(999)
            globals()["_EMIT_WQ"] = lambda n: None

        bar = tr.barrier()
        with ExitStack() as _stB:
            def SBB(name, shape, dtype, **kw):
                return _stB.enter_context(nc.sbuf_tensor(name, shape, dtype, **kw))

            x1t = SBB("x1t", [128, 12 * 1024], BF16)
            xn2 = SBB("xn2", [128, 8 * 1024], BF16)
            xqF = SBB("xqF", [128, 8 * 1024], BF16, is_transpose_dest=True)
            h_t = SBB("h_t", [128, 8 * 2816], BF16)
            hqF0 = SBB("hqF0", [128, 22 * 128], BF16, is_transpose_dest=True)
            hqF1 = SBB("hqF1", [128, 22 * 128], BF16, is_transpose_dest=True)
            wgc0 = SBB("wgc0", [128, 8 * 512], BF16)
            wgc1 = SBB("wgc1", [128, 8 * 512], BF16)
            wuc0 = SBB("wuc0", [128, 8 * 512], BF16)
            wuc1 = SBB("wuc1", [128, 8 * 512], BF16)
            wd_sb = SBB("wd_sb", [128, 22 * 1024], BF16)
            sg0 = SBB("sg0", [128, 512], BF16)
            sg1 = SBB("sg1", [128, 512], BF16)
            ost0 = SBB("ost0", [128, 1024], F32)
            ost1 = ost0
            lnst2 = SBB("lnst2", [128, 80], F32)
            bb = {nm: Buf(nm) for nm in
                  ("h_t", "hqF0", "hqF1", "wgc0", "wgc1",
                   "wuc0", "wuc1", "wd", "sg0", "sg1", "ost0", "ost1",
                   "lnst2")}
            bb["ost1"] = bb["ost0"]
            bb_x1tq = [Buf("x1tq0"), Buf("x1tq1"), Buf("x1tq2")]
            bb_xn2q = [Buf("xn2q0"), Buf("xn2q1")]
            bb_xqFq = [Buf("xqFq0"), Buf("xqFq1")]
            first = {s: True for s in tr.streams}

            _orig_op = tr.op

            def opB(stream, fn, reads=(), writes=(), inc=1, grp=None):
                if first[stream]:
                    first[stream] = False
                    reads = list(reads) + bar
                return _orig_op(stream, fn, reads=reads, writes=writes, inc=inc,
                                grp=grp)

            quantB = make_quant(tr, SBB, "qb", Buf, nsets=2, maxw=512,
                                cbuf=BC["wgam"], pipe3=True)

            qB_q = []
            qst = {"a": 0, "b": 0, "c": 0}

            def qB_push(xap, width, rb, wb):
                qB_q.append((xap, width, rb, wb))
                return len(qB_q) - 1

            def qB_step():
                if qst["b"] > qst["c"]:
                    p = qst["c"]
                    x, w, rb, wb = qB_q[p]
                    quantB(x, w, rb, wb, opB, p, stage="c")
                    qst["c"] += 1
                if qst["a"] > qst["b"]:
                    p = qst["b"]
                    x, w, rb, wb = qB_q[p]
                    quantB(x, w, rb, wb, opB, p, stage="b")
                    qst["b"] += 1
                if qst["a"] < len(qB_q):
                    p = qst["a"]
                    x, w, rb, wb = qB_q[p]
                    quantB(x, w, rb, wb, opB, p, stage="a")
                    qst["a"] += 1

            def qB_drain(pidx):
                while qst["c"] <= pidx:
                    qB_step()



            x1t3 = x1t[:].rearrange("p (a f) -> p a f", a=12)

            def slotq(i, q):
                return (2 * i + q) % 3
            xn23 = xn2[:].rearrange("p (a f) -> p a f", a=8)
            xqFb = xqF[:].rearrange("p (tt dc f) -> p tt dc f", tt=8, dc=8)
            ht3 = h_t[:].rearrange("p (a f) -> p a f", a=8)
            wd3 = wd_sb[:].rearrange("p (nc e) -> p nc e", nc=22)

            def prep_work(i, q):
                sq = slotq(i, q)
                bq = bb_x1tq[sq]
                opB("gp", (lambda t_, q_, sq_: lambda e: e.dma_start(
                    out=x1t3[:, sq_ * 4:(sq_ + 1) * 4, :],
                    in_=x1_r[:, t_ * 8 + q_ * 4:t_ * 8 + (q_ + 1) * 4, :]))(
                        i, q, sq),
                    reads=[bx1d], writes=[bq], inc=16, grp="g_x1tq%d" % sq)
                o = q * 4
                a0 = i * 8 + q * 4

                def L(base):
                    return lnst2[:, base + o:base + o + 4]
                opB("dve", (lambda a0_: lambda e: e.tensor_scalar(
                    L(16), ln2st[:, 32 + a0_:36 + a0_], 1.0 / D, None,
                    ALU.mult))(a0),
                    reads=[bln2], writes=[bb["lnst2"]])
                opB("dve", (lambda a0_: lambda e: e.tensor_scalar(
                    L(24), ln2st[:, a0_:a0_ + 4], 1.0 / D, EPS_LN,
                    ALU.mult, ALU.add))(a0),
                    reads=[bln2], writes=[bb["lnst2"]])
                opB("dve", lambda e: e.tensor_tensor(
                    L(32), L(16), L(16), ALU.mult),
                    reads=[bb["lnst2"]], writes=[bb["lnst2"]])
                opB("dve", lambda e: e.tensor_tensor(
                    L(40), L(24), L(32), ALU.subtract),
                    reads=[bb["lnst2"]], writes=[bb["lnst2"]])
                opB("act", lambda e: e.activation(L(48), L(40), AF.Ln),
                    reads=[bb["lnst2"]], writes=[bb["lnst2"]])
                opB("act", lambda e: e.activation(L(56), L(48), AF.Exp, scale=-0.5),
                    reads=[bb["lnst2"]], writes=[bb["lnst2"]])
                opB("dve", lambda e: e.tensor_tensor(
                    L(64), L(16), L(56), ALU.mult),
                    reads=[bb["lnst2"]], writes=[bb["lnst2"]])
                opB("dve", lambda e: e.tensor_scalar(
                    L(72), L(64), -1.0, None, ALU.mult),
                    reads=[bb["lnst2"]], writes=[bb["lnst2"]])
                for tt in range(q * 4, q * 4 + 4):
                    st_ = sq * 4 + (tt % 4)
                    opB("dve", (lambda tt_, st2_: lambda e: e.tensor_scalar(
                        xn23[:, tt_, :], x1t3[:, st2_, :],
                        lnst2[:, 56 + tt_:57 + tt_], lnst2[:, 72 + tt_:73 + tt_],
                        ALU.mult, ALU.add))(tt, st_),
                        reads=[bq, bb["lnst2"]], writes=[bb_xn2q[q]])
                last = None
                for tt in range(q * 4, q * 4 + 4):
                    for pc in range(2):
                        last = qB_push(xn23[:, tt, pc * 512:(pc + 1) * 512],
                                       512, [bb_xn2q[q]], [bb_xn2q[q]])
                        qB_step()
                prep_last[q] = last

            def prep_xbar(i, q):
                qB_drain(prep_last[q])
                opB("sync", (lambda q_: lambda e: e.dma_start(
                    out=xqF[:].rearrange("p (a f) -> p a f", a=64)[
                        :, q_ * 32:(q_ + 1) * 32, :],
                    in_=xn2[:].rearrange("p (a f) -> p a f", a=64)[
                        :, q_ * 32:(q_ + 1) * 32, :],
                    transpose=True))(q),
                    reads=[bb_xn2q[q]], writes=[bb_xqFq[q]], inc=16,
                    grp="g_xqFq%d" % q)

            prep_last = {}

            NKW = [512] * 5 + [256]
            NKOFF = [0, 512, 1024, 1536, 2048, 2560]

            def b_gate_nck(i, nck, fuse_tail):
                nw = NKW[nck]
                no = NKOFF[nck]
                wg_c = wgc0 if nck % 2 == 0 else wgc1
                wu_c = wuc0 if nck % 2 == 0 else wuc1
                Bg = bb["wgc0"] if nck % 2 == 0 else bb["wgc1"]
                Bu = bb["wuc0"] if nck % 2 == 0 else bb["wuc1"]
                opB("sync", (lambda no_, nw_, w_: lambda e: e.dma_start(
                    out=w_[:, 0:8 * nw_].rearrange("p (dc f) -> p dc f", dc=8),
                    in_=wgq_r[:, :, no_:no_ + nw_]))(no, nw, wg_c),
                    reads=[bwgq6[nck]], writes=[Bg], inc=16,
                    grp="g_wgc0" if nck % 2 == 0 else "g_wgc1")
                opB("sync", (lambda no_, nw_, w_: lambda e: e.dma_start(
                    out=w_[:, 0:8 * nw_].rearrange("p (dc f) -> p dc f", dc=8),
                    in_=wuq_r[:, :, no_:no_ + nw_]))(no, nw, wu_c),
                    reads=[bwuq6[nck]], writes=[Bu], inc=16,
                    grp="g_wuc0" if nck % 2 == 0 else "g_wuc1")
                for tt in range(8):
                    pg = pA if tt % 2 == 0 else pR1
                    pu = pB if tt % 2 == 0 else pR2
                    sg = sg0 if tt % 2 == 0 else sg1
                    Bs = bb["sg0"] if tt % 2 == 0 else bb["sg1"]

                    def mkg(tt_, nw_, w_, ps_):
                        def f(e):
                            ins = None
                            for dc in range(8):
                                ins = e.matmul(
                                    ps_[:, 0:nw_],
                                    xqFb[:, tt_, dc, :],
                                    w_[:, 0:8 * nw_].rearrange(
                                        "p (dc f) -> p dc f", dc=8)[:, dc, :],
                                    start=(dc == 0), stop=(dc == 7))
                            return ins
                        return f
                    opB("pe", mkg(tt, nw, wg_c, pg),
                        reads=[bb_xqFq[tt // 4], Bg], writes=[P(pg)])
                    opB("pe", mkg(tt, nw, wu_c, pu),
                        reads=[bb_xqFq[tt // 4], Bu], writes=[P(pu)])
                    opB("act", (lambda nw_, ps_, sg_: lambda e: e.activation(
                        sg_[:, 0:nw_], ps_[:, 0:nw_], AF.Silu))(nw, pg, sg),
                        reads=[P(pg)], writes=[Bs])
                    opB("dve", (lambda tt_, no_, nw_, ps_, sg_: lambda e: e.tensor_tensor(
                        ht3[:, tt_, no_:no_ + nw_], ps_[:, 0:nw_],
                        sg_[:, 0:nw_], ALU.mult))(tt, no, nw, pu, sg),
                        reads=[P(pu), Bs], writes=[bb["h_t"]])
                    hlast[tt] = qB_push(ht3[:, tt, no:no + nw], nw,
                                        [bb["h_t"]], [bb["h_t"]])
                    qB_step()
                    if fuse_tail:
                        if tt >= 1:
                            b_tail_xbar(i, tt - 1)
                        if tt >= 2:
                            b_tail_tt(i, tt - 2)
                            if tt - 2 == 3 and i + 1 < NT_B:
                                prep_work(i + 1, 1)
                                prep_xbar(i + 1, 0)
                if fuse_tail:
                    b_tail_xbar(i, 7)
                    b_tail_tt(i, 6)
                    b_tail_tt(i, 7)
                    if i + 1 < NT_B:
                        prep_xbar(i + 1, 1)

            hlast = {}

            def b_tail_xbar(i, tt):
                qB_drain(hlast[tt])
                hqF = hqF0 if tt % 2 == 0 else hqF1
                Bh = bb["hqF0"] if tt % 2 == 0 else bb["hqF1"]
                opB("sync", (lambda tt_, hq_: lambda e: e.dma_start(
                    out=hq_[:].rearrange("p (a f) -> p a f", a=22),
                    in_=ht3[:, tt_, :].rearrange("p (a f) -> p a f", a=22),
                    transpose=True))(tt, hqF),
                    reads=[bb["h_t"]], writes=[Bh], inc=16,
                    grp="g_hqF0" if tt % 2 == 0 else "g_hqF1")

            def b_tail_tt(i, tt):
                hqF = hqF0 if tt % 2 == 0 else hqF1
                Bh = bb["hqF0"] if tt % 2 == 0 else bb["hqF1"]
                hq3 = hqF[:].rearrange("p (a f) -> p a f", a=22)
                pdd = (pPT, pY)
                for eh in range(2):
                    pd = pdd[eh]

                    def mkd(tt_, eh_, hq3_, pd_):
                        def f(e):
                            ins = None
                            for ncc in range(22):
                                ins = e.matmul(
                                    pd_[:], hq3_[:, ncc, :],
                                    wd3[:, ncc, eh_ * 512:(eh_ + 1) * 512],
                                    start=(ncc == 0), stop=(ncc == 21))
                            return ins
                        return f
                    opB("pe", mkd(tt, eh, hq3, pd),
                        reads=[Bh, bb["wd"]], writes=[P(pd)])
                ost = ost0 if tt % 2 == 0 else ost1
                Bo = bb["ost0"] if tt % 2 == 0 else bb["ost1"]
                st_ = slotq(i, tt // 4) * 4 + (tt % 4)
                for eh in range(2):
                    pd = pdd[eh]
                    opB("dve", (lambda st2_, eh_, pd_, o_: lambda e: e.tensor_tensor(
                        o_[:, eh_ * 512:(eh_ + 1) * 512], pd_[:],
                        x1t3[:, st2_, eh_ * 512:(eh_ + 1) * 512], ALU.add))(
                            st_, eh, pd, ost),
                        reads=[P(pd), bb_x1tq[slotq(i, tt // 4)]], writes=[Bo])
                opB("sync", (lambda t_, tt_, o_: lambda e: e.dma_start(
                    out=out_r[:, t_ * 8 + tt_, :], in_=o_[:]))(i, tt, ost),
                    reads=[Bo], inc=16,
                    grp="g_out0" if tt % 2 == 0 else "g_out1")

            _emit_wq = globals()["_EMIT_WQ"]
            prep_work(0, 0)
            prep_work(0, 1)
            prep_xbar(0, 0)
            prep_xbar(0, 1)
            wd_loaded = [False]

            def _load_wd():
                if wd_loaded[0]:
                    return
                wd_loaded[0] = True
                opB("sync", lambda e: e.dma_start(
                    out=wd_sb[:].rearrange("p (nc e) -> p nc e", nc=22),
                    in_=wdq_r),
                    reads=[bwdq], writes=[bb["wd"]], inc=16, grp="g_wdsb")

            for i in range(NT_B):
                for nck in range(5):
                    b_gate_nck(i, nck, False)
                    if nck == 2 and i + 1 < NT_B:
                        prep_work(i + 1, 0)
                _load_wd()
                b_gate_nck(i, 5, True)
            tr.finalize(nc, _st, block)
    return nc


def kernel(L0_attn_x, L0_attn_Wq, L0_attn_Wk, L0_attn_Wv, L0_attn_Wo,
           L0_attn_W_gamma, L0_attn_b_gamma, L0_attn_W_lambda, L0_attn_b_lambda,
           L0_ffn_w_gate, L0_ffn_w_up, L0_ffn_w_down):
    ident, icol4, ones1, ones16, oh4 = _host_consts()
    x = np.ascontiguousarray(np.asarray(L0_attn_x, np.float32))
    wqT = np.ascontiguousarray(np.asarray(L0_attn_Wq, np.float32).T)
    wkT = np.ascontiguousarray(np.asarray(L0_attn_Wk, np.float32).T)
    wvT = np.ascontiguousarray(np.asarray(L0_attn_Wv, np.float32).T)
    woT = np.ascontiguousarray(np.asarray(L0_attn_Wo, np.float32).T)
    wg = np.ascontiguousarray(np.asarray(L0_ffn_w_gate, np.float32))
    wu = np.ascontiguousarray(np.asarray(L0_ffn_w_up, np.float32))
    wd = np.ascontiguousarray(np.asarray(L0_ffn_w_down, np.float32))

    def bc64(v):
        return np.ascontiguousarray(
            np.broadcast_to(np.tile(np.asarray(v, np.float32), 4), (128, 64)))

    nc = build_nc(debug=False)
    in_maps = []
    for b in range(8):
        in_maps.append({
            "x": np.ascontiguousarray(x[b]),
            "wqT": wqT, "wkT": wkT, "wvT": wvT, "woT": woT,
            "wg_raw": wg, "wu_raw": wu, "wd_raw": wd,
            "wgam_in": bc64(L0_attn_W_gamma), "bgam_in": bc64(L0_attn_b_gamma),
            "wlam_in": bc64(L0_attn_W_lambda), "blam_in": bc64(L0_attn_b_lambda),
            "c_ident": ident, "c_icol4": icol4, "c_ones1": ones1,
            "c_ones16": ones16, "c_oh4": oh4,
        })
    res = run_bass_kernel_spmd(nc, in_maps, core_ids=list(range(8)))
    return np.stack([np.asarray(r["out"], np.float32) for r in res.results])
